# version 01
# baseline (speedup 1.0000x reference)
import contextlib
import numpy as np
import concourse.bass as bass
import concourse.mybir as mybir
from concourse.bass_utils import run_bass_kernel_spmd
from concourse.aot_env import allow_ici_all_to_all

F32 = mybir.dt.float32
BF16 = mybir.dt.bfloat16
AF = mybir.ActivationFunctionType
ALU = mybir.AluOpType
AX = mybir.AxisListType

T = 2048
D = 1024
NP = 2048
NSS = 16
G = 8
TQ = 4
TC = 256
NCH = T // TC
MT = 16

DEV_TAP = None

ENGS = ("sync", "scalar", "vector", "tensor", "gpsimd")


class P:

    def __init__(self, nc):
        self.nc = nc
        self.ops = {e: [] for e in ENGS}
        self.c = {}
        self.sems = {}

    def sem(self, stack, name):
        self.sems[name] = stack.enter_context(self.nc.semaphore(name))
        self.c[name] = 0

    def emit(self, eng, fn):
        self.ops[eng].append(fn)

    def wait(self, eng, sname, val=None):
        v = self.c[sname] if val is None else val
        if v <= 0:
            return
        s = self.sems[sname]
        self.emit(eng, lambda E, s=s, v=v: E.wait_ge(s, v))

    def dma(self, eng, out, in_, sname, slow=False):
        self.c[sname] += 16
        s = self.sems[sname]
        self.emit(eng, lambda E, o=out, i=in_, s=s, sl=slow: E.dma_start(
            out=o, in_=i, allow_slow_non_contiguous=sl).then_inc(s, 16))
        return self.c[sname]

    def op(self, eng, fn, sname=None):
        if sname is None:
            self.emit(eng, lambda E, f=fn: f(E))
            return None
        self.c[sname] += 1
        s = self.sems[sname]
        self.emit(eng, lambda E, f=fn, s=s: f(E).then_inc(s, 1))
        return self.c[sname]


def build(tap=None):
    nc = bass.Bass("TRN2", target_bir_lowering=False, debug=False)
    stack = contextlib.ExitStack()

    d_x = nc.dram_tensor("x", [T, D], F32, kind="ExternalInput").ap()
    d_win = nc.dram_tensor("w_in_t", [D, 2 * D], F32, kind="ExternalInput").ap()
    d_wout = nc.dram_tensor("w_out_t", [D, D], F32, kind="ExternalInput").ap()
    d_phi = nc.dram_tensor("phi", [D, NP], F32, kind="ExternalInput").ap()
    d_xproj = nc.dram_tensor("xproj_t", [D, 33], F32, kind="ExternalInput").ap()
    d_wffn = nc.dram_tensor("w_ffn", [8, D, D], F32, kind="ExternalInput").ap()
    d_bffn = nc.dram_tensor("b_ffn", [8, D], F32, kind="ExternalInput").ap()
    d_convw = nc.dram_tensor("conv_w", [128, 32], F32, kind="ExternalInput").ap()
    d_convb = nc.dram_tensor("conv_b", [128, 8], F32, kind="ExternalInput").ap()
    d_dtw = nc.dram_tensor("dt_w", [128, 8], F32, kind="ExternalInput").ap()
    d_dtb = nc.dram_tensor("dt_b", [128, 8], F32, kind="ExternalInput").ap()
    d_alog = nc.dram_tensor("a_log", [128, 128], F32, kind="ExternalInput").ap()
    d_rank = nc.dram_tensor("rank", [1, 1], mybir.dt.uint32, kind="ExternalInput").ap()
    d_idf = nc.dram_tensor("ident_f", [128, 128], F32, kind="ExternalInput").ap()
    d_idb = nc.dram_tensor("ident_b", [128, 128], BF16, kind="ExternalInput").ap()
    d_out = nc.dram_tensor("out", [T, D], F32, kind="ExternalOutput").ap()

    d_fin = nc.dram_tensor("fence_in", [128, 4], F32).ap()
    d_fout = nc.dram_tensor("fence_out", [128, 4], F32).ap()
    d_gA = nc.dram_tensor("guard_A", [T, 512], F32).ap()
    d_E = nc.dram_tensor("E_scr", [T, NP], F32).ap()
    d_gB = nc.dram_tensor("guard_B", [T, 512], F32).ap()
    d_logits = nc.dram_tensor("logits_scr", [T, NP], F32).ap()
    d_den = nc.dram_tensor("den_scr", [T, NP], F32).ap()
    d_gC = nc.dram_tensor("guard_C", [T, 512], F32).ap()
    d_sz = nc.dram_tensor("sz_scr", [G, 128, T], F32).ap()
    d_xconv = nc.dram_tensor("xconv_scr", [G, 128, T], F32).ap()
    d_dt = nc.dram_tensor("dt_scr", [G, 128, T], F32).ap()
    d_ssm = nc.dram_tensor("ssm_scr", [33, T], F32).ap()
    d_macc = nc.dram_tensor("macc_scr", [T, D], F32).ap()
    d_guard0 = nc.dram_tensor("guard0", [8, D, 256], BF16).ap()
    d_ssnd = nc.dram_tensor("slots_snd", [8, D, 256], BF16).ap()
    d_guard1 = nc.dram_tensor("guard1", [8, D, 256], BF16).ap()
    d_sag = nc.dram_tensor("slots_ag", [8, 8, D, 256], BF16).ap()
    d_guard2 = nc.dram_tensor("guard2", [8, 256, D], BF16).ap()
    d_ysnd = nc.dram_tensor("y_snd", [8, 256, D], BF16).ap()
    d_guard3 = nc.dram_tensor("guard3", [8, 256, D], BF16).ap()
    d_yag = nc.dram_tensor("y_ag", [8, 8, 256, D], BF16).ap()
    d_dev = None
    if tap is not None:
        d_dev = nc.dram_tensor("dev", [128, 81920], F32, kind="ExternalOutput").ap()

    NF32 = 53152
    mega = nc.alloc_sbuf_tensor("mega", [128, NF32], F32).ap()
    cursor = [0]

    def f32v(n_elems, at=None):
        off = cursor[0] if at is None else at
        v = mega[:, off:off + n_elems]
        if at is None:
            cursor[0] = off + n_elems
        return v

    def bf16v(n_elems, at=None):
        assert n_elems % 2 == 0
        off = cursor[0] if at is None else at
        v = mega[:, off:off + n_elems // 2].bitcast(BF16)
        if at is None:
            cursor[0] = off + n_elems // 2
        return v

    sb_aneg = f32v(128)
    sb_convw = f32v(32)
    sb_convb = f32v(8)
    sb_dtw = f32v(8)
    sb_dtb = f32v(8)
    sb_idf = f32v(128)
    sb_idb = bf16v(128)
    sb_mu = f32v(16)
    sb_rstd = f32v(16)
    sb_ln1 = f32v(16)
    sb_ln2 = f32v(16)
    sb_rs = f32v(16)
    sb_rrs = f32v(16)
    sb_hcar = f32v(128)
    sb_e0 = f32v(16)
    sb_fixv = f32v(16)
    sb_xpj = f32v(8 * 33)
    sb_bbc = f32v(1024)
    sb_tmp1 = f32v(16)
    sb_rank = f32v(2).bitcast(mybir.dt.uint32)
    assert cursor[0] <= 1952, cursor[0]
    cursor[0] = 1952
    sb_sbc = f32v(2048)
    sb_ssm = f32v(2048)
    sb_xntok = bf16v(16 * 1024)
    BASE_A = cursor[0]
    sb_xnd = bf16v(8 * 2048, at=BASE_A)
    sb_Bbc = f32v(4096, at=BASE_A)
    sb_Cbc = f32v(4096, at=BASE_A + 4096)
    sb_srcv = bf16v(16 * 1024, at=BASE_A)
    BASE_B = BASE_A + 8192
    sb_stg = bf16v(16 * 1024, at=BASE_B)
    sb_winb = bf16v(8 * 2048, at=BASE_B)
    sb_dA = f32v(4096, at=BASE_B)
    sb_dBx = f32v(4096, at=BASE_B + 4096)
    sb_yb = bf16v(16 * 1024, at=BASE_B)
    BASE_C = BASE_B + 8192
    sb_phib = bf16v(8 * 2048, at=BASE_C)
    sb_h = f32v(4096, at=BASE_C)
    sb_xcs = f32v(2048, at=BASE_C + 4096)
    sb_dts = f32v(2048, at=BASE_C + 6144)
    sb_wfs = f32v(8 * 1024, at=BASE_C)
    BASE_D = BASE_C + 8192
    sb_xt = [f32v(4096, at=BASE_D + 4096 * i) for i in range(2)]
    sb_wstage = f32v(8192, at=BASE_D)
    sb_xinp = f32v(2064, at=BASE_D)
    sb_ct0 = f32v(2048, at=BASE_D + 2176)
    sb_ct1 = f32v(2048, at=BASE_D + 4224)
    sb_xcg = f32v(2048, at=BASE_D + 6272)
    sb_szs = f32v(2048, at=BASE_D)
    sb_ygt = bf16v(2 * 2048, at=BASE_D + 2048)
    sb_yg = f32v(256, at=BASE_D + 4096)
    sb_ug = f32v(256, at=BASE_D + 4352)
    sb_xres = f32v(2048, at=BASE_D + 4608)
    sb_wfb = bf16v(8 * 1024, at=BASE_D)
    sb_yev = [f32v(512, at=BASE_D + 4608 + 512 * i) for i in range(2)]
    sb_ybf = bf16v(512, at=BASE_D + 5632)
    sb_Ei = f32v(2048, at=BASE_D)
    sb_cmbf = f32v(2048, at=BASE_D + 2048)
    sb_cmb = bf16v(512, at=BASE_D + 5632)
    sb_mact = [f32v(1024, at=BASE_D + 4096 + 1024 * i) for i in range(2)]
    sb_outt = [f32v(1024, at=BASE_D + 6144 + 1024 * i) for i in range(2)]
    BASE_E = BASE_D + 8192
    sb_woutb = bf16v(8 * 1024, at=BASE_E)
    sb_Et = [f32v(512, at=BASE_E + 4096 + 512 * i) for i in range(2)]
    sb_Lt = [f32v(512, at=BASE_E + 5120 + 512 * i) for i in range(2)]
    sb_x2 = f32v(1024, at=BASE_E + 4096)
    sb_xnf = f32v(1024, at=BASE_E + 5120)
    sb_cnp = bf16v(16 * 128, at=BASE_E + 4096)
    sb_lgt = [f32v(512, at=BASE_E + 512 * i) for i in range(2)]
    sb_dnt = [f32v(512, at=BASE_E + 1024 + 512 * i) for i in range(2)]
    sb_dsp = [bf16v(512, at=BASE_E + 2048 + 256 * i) for i in range(2)]
    sb_sev = bf16v(8 * 512, at=BASE_E + 2560)
    assert BASE_E + 6144 <= NF32, (BASE_E, NF32)

    ps = [nc.alloc_psum_tensor(f"ps{i}", [128, 512], F32).ap() for i in range(4)]
    ps_big = nc.alloc_psum_tensor("ps_big", [128, 2048], F32).ap()
    ps8 = ps + [ps_big[:, 512 * i:512 * (i + 1)] for i in range(4)]

    p = P(nc)
    for s in ("ld", "l0", "l1", "sp", "s0", "s1", "te", "ve", "se", "ge", "gl",
              "cc", "cc2", "cc3", "st"):
        p.sem(stack, s)

    p.dma("sync", sb_rank[0:1, 0:1], d_rank, "ld")
    ld_rank = p.c["ld"]
    sag_v = d_sag.rearrange("b c (j p) t -> c b p j t", p=128)
    yg5 = d_yag.rearrange("r c (k p) e -> c r p k e", p=128)
    dyn_sr = [None]
    dyn_y = [None]
    def _mk_dyn(E):
        rv = E.value_load(sb_rank[0:1, 0:1])
        dyn_sr[0] = sag_v[bass.ts(rv, 1)].squeeze(0)
        dyn_y[0] = yg5[bass.ts(rv, 1)].squeeze(0)
    p.dma("sync", sb_idf, d_idf, "ld")
    p.dma("sync", sb_idb, d_idb, "ld")
    p.dma("sync", sb_aneg, d_alog, "ld")
    p.dma("sync", sb_convw, d_convw, "ld")
    p.dma("sync", sb_convb, d_convb, "ld")
    p.dma("sync", sb_dtw, d_dtw, "ld")
    p.dma("sync", sb_dtb, d_dtb, "ld")
    p.dma("sync", sb_xpj.rearrange("p (g e) -> p g e", g=8),
          d_xproj.rearrange("(g p) e -> p g e", p=128), "ld")
    ld_params = p.c["ld"]
    p.wait("scalar", "ld", ld_params)
    p.op("scalar", lambda E: E.activation(sb_aneg, sb_aneg, AF.Exp), "se")
    p.wait("gpsimd", "se", p.c["se"])
    p.emit("gpsimd", _mk_dyn)
    p.wait("vector", "se", p.c["se"])
    p.op("vector", lambda E: E.tensor_scalar_mul(sb_aneg, sb_aneg, -1.0), "ve")

    x_r = d_x.rearrange("(i p) d -> p i d", p=128)
    ld_xgrp = []
    ve_ln_done = [0] * 4
    for grp in range(4):
        if grp >= 2:
            p.wait("sync", "ve", ve_ln_done[grp - 2])
        ld_xgrp.append(p.dma("sync", sb_xt[grp % 2].rearrange("p (i d) -> p i d", d=D),
                             x_r[:, 4 * grp:4 * (grp + 1), :], f"l{grp % 2}"))
        for sl in range(4):
            i = 4 * grp + sl
            xt = sb_xt[grp % 2][:, D * sl:D * (sl + 1)]
            p.wait("vector", f"l{grp % 2}", ld_xgrp[grp])
            p.op("vector", lambda E, xt=xt, i=i: E.reduce_sum(sb_ln1[:, i:i + 1], xt, AX.X))
            p.wait("scalar", f"l{grp % 2}", ld_xgrp[grp])
            p.op("scalar", lambda E, xt=xt: E.square(sb_x2, xt), "se")
            p.wait("vector", "se", p.c["se"])
            p.op("vector", lambda E, i=i: E.reduce_sum(sb_ln2[:, i:i + 1], sb_x2, AX.X))
            p.op("vector", lambda E, i=i: E.tensor_scalar_mul(sb_mu[:, i:i + 1], sb_ln1[:, i:i + 1], 1.0 / D))
            p.op("vector", lambda E, i=i: E.tensor_scalar_mul(sb_ln2[:, i:i + 1], sb_ln2[:, i:i + 1], 1.0 / D), "ve")
            p.wait("scalar", "ve", p.c["ve"])
            p.op("scalar", lambda E, i=i: E.square(sb_tmp1[:, 0:1], sb_mu[:, i:i + 1]), "se")
            p.wait("vector", "se", p.c["se"])
            p.op("vector", lambda E, i=i: E.tensor_sub(sb_ln2[:, i:i + 1], sb_ln2[:, i:i + 1], sb_tmp1[:, 0:1]))
            p.op("vector", lambda E, i=i: E.tensor_scalar_add(sb_ln2[:, i:i + 1], sb_ln2[:, i:i + 1], 1e-5), "ve")
            p.wait("scalar", "ve", p.c["ve"])
            p.op("scalar", lambda E, i=i: E.activation(sb_ln1[:, i:i + 1], sb_ln2[:, i:i + 1], AF.Sqrt), "se")
            p.wait("vector", "se", p.c["se"])
            p.op("vector", lambda E, i=i: E.reciprocal(sb_rstd[:, i:i + 1], sb_ln1[:, i:i + 1]))
            p.wait("vector", "te", p.c["te"])
            p.op("vector", lambda E, xt=xt, i=i: E.tensor_scalar(
                sb_xnf, xt, sb_mu[:, i:i + 1], sb_rstd[:, i:i + 1], ALU.subtract, ALU.mult), "ve")
            ve_xn = p.c["ve"]
            p.wait("scalar", "ve", ve_xn)
            p.op("scalar", lambda E, i=i: E.activation(sb_xntok[:, D * i:D * (i + 1)], sb_xnf, AF.Copy), "se")
            for j in range(G):
                pst = ps[2 + (j % 2)][:, :128].bitcast(BF16)[:, :128]
                p.wait("tensor", "se", p.c["se"])
                p.op("tensor", lambda E, i=i, j=j, pst=pst: E.transpose(
                    pst, sb_xntok[:, D * i + 128 * j:D * i + 128 * (j + 1)], sb_idb), "te")
                p.wait("scalar", "te", p.c["te"])
                p.op("scalar", lambda E, i=i, j=j, pst=pst: E.activation(
                    sb_xnd[:, 2048 * j + 128 * i:2048 * j + 128 * (i + 1)], pst, AF.Copy), "se")
        ve_ln_done[grp] = p.c["ve"]
    se_xn_all = p.c["se"]

    if tap == "p0":
        p.wait("scalar", "se", p.c["se"])
        p.wait("scalar", "ve", p.c["ve"])
        p.dma("scalar", d_dev[:, 0:8192], sb_xnd.bitcast(F32), "st")
        p.dma("scalar", d_dev[:, 8192:16384], sb_xntok.bitcast(F32), "st")
        p.dma("scalar", d_dev[:, 16384:16400], sb_mu, "st")
        p.dma("scalar", d_dev[:, 16400:16416], sb_rstd, "st")
        return _finish(nc, stack, p)

    p.wait("sync", "ve", p.c["ve"])
    for wname, src, dst in (("win", d_win, sb_winb), ("phi", d_phi, sb_phib)):
        for half in range(2):
            p.wait("sync", "se", p.c["se"])
            ldw = p.dma("sync", sb_wstage.rearrange("p (j e) -> p j e", j=4),
                        src.rearrange("(j p) e -> p j e", p=128)[:, 4 * half:4 * (half + 1), :], "ld")
            p.wait("scalar", "ld", ldw)
            for jj in range(4):
                j = 4 * half + jj
                p.op("scalar", lambda E, jj=jj, j=j, dst=dst: E.activation(
                    dst[:, 2048 * j:2048 * (j + 1)], sb_wstage[:, 2048 * jj:2048 * (jj + 1)], AF.Copy), "se")

    p.wait("tensor", "se", p.c["se"])
    for j in range(2):
        p.op("tensor", lambda E, j=j: E.matmul(
            ps[3], sb_xnd[:, 2048 * j:2048 * j + 128],
            sb_phib[:, 2048 * j:2048 * j + 512],
            start=(j == 0), stop=(j == 1)), "te" if j == 1 else None)
    p.wait("vector", "te", p.c["te"])
    p.op("vector", lambda E: E.tensor_copy(sb_tmp1[:, 0:1], ps[3][:, 0:1]), "ve")

    last_store = {}

    def guarded_store(tag, bufs, idx):
        key = (tag, idx % len(bufs))
        if key in last_store:
            p.wait("scalar", "sp", last_store[key])

    def wait_slot_stores(eng, sem):
        p.wait(eng, sem, p.c[sem])

    for i in range(MT):
        for q in range(TQ):
            pq = ps[q % 2]
            p.wait("tensor", "se", p.c["se"])
            for j in range(G):
                p.op("tensor", lambda E, i=i, j=j, q=q, pq=pq: E.matmul(
                    pq, sb_xnd[:, 2048 * j + 128 * i:2048 * j + 128 * (i + 1)],
                    sb_phib[:, 2048 * j + 512 * q:2048 * j + 512 * (q + 1)],
                    start=(j == 0), stop=(j == G - 1)), "te" if j == G - 1 else None)
            p.wait("scalar", "te", p.c["te"])
            wait_slot_stores("scalar", f"s{q % 2}")
            eb, lb = sb_Et[q % 2], sb_Lt[q % 2]
            p.op("scalar", lambda E, pq=pq, eb=eb: E.activation(eb, pq, AF.Exp), "se")
            p.op("scalar", lambda E, pq=pq, lb=lb: E.activation(lb, pq, AF.Copy), "se")
            p.wait("scalar", "se", p.c["se"])
            p.dma("scalar", d_E[128 * i:128 * (i + 1), 512 * q:512 * (q + 1)], eb, f"s{q % 2}")
            p.dma("scalar", d_logits[128 * i:128 * (i + 1), 512 * q:512 * (q + 1)], lb, f"s{q % 2}")
            if tap == "p1" and i == 0 and q == 0:
                p.op("scalar", lambda E, pq=pq: E.activation(sb_sev.bitcast(F32)[:, 0:512], pq, AF.Copy), "se")
                p.dma("scalar", d_dev[:, 66048:66560], sb_sev.bitcast(F32)[:, 0:512], "st")
                p.dma("scalar", d_dev[:, 66560:67072], lb, "st")
    p.wait("gpsimd", "s0", p.c["s0"])
    p.wait("gpsimd", "s1", p.c["s1"])
    p.op("gpsimd", lambda E: E.collective_compute(
        "AllReduce", ALU.add, replica_groups=[list(range(8))],
        ins=[d_E.opt()], outs=[d_den.opt()]), "cc")
    cc_ar = p.c["cc"]

    if tap == "p1":
        p.wait("scalar", "sp", p.c["sp"])
        p.dma("scalar", d_dev[:, 0:32768].rearrange("p (i q) -> p i q", i=16), d_E.rearrange("(i p) q -> p i q", p=128), "st")
        p.dma("scalar", d_dev[:, 32768:65536].rearrange("p (i q) -> p i q", i=16), d_logits.rearrange("(i p) q -> p i q", p=128), "st")
        p.wait("tensor", "sp", p.c["sp"])
        for j in range(G):
            p.op("tensor", lambda E, j=j: E.matmul(
                ps[0], sb_xnd[:, 2048 * j:2048 * j + 128],
                sb_phib[:, 2048 * j:2048 * j + 512],
                start=(j == 0), stop=(j == G - 1)), "te" if j == G - 1 else None)
        p.wait("scalar", "te", p.c["te"])
        p.op("scalar", lambda E: E.activation(sb_Lt[0], ps[0], AF.Copy), "se")
        p.dma("scalar", d_dev[:, 65536:66048], sb_Lt[0], "st")
        return _finish(nc, stack, p)

    for g in range(16):
        for q in range(TQ):
            pq = ps[q % 2]
            p.wait("tensor", "se", p.c["se"])
            for j in range(G):
                p.op("tensor", lambda E, g=g, j=j, q=q, pq=pq: E.matmul(
                    pq, sb_winb[:, 2048 * j + 128 * g:2048 * j + 128 * (g + 1)],
                    sb_xnd[:, 2048 * j + 512 * q:2048 * j + 512 * (q + 1)],
                    start=(j == 0), stop=(j == G - 1)), "te" if j == G - 1 else None)
            p.wait("scalar", "te", p.c["te"])
            if g < 8:
                if q == 0:
                    p.wait("scalar", "ve", p.c["ve"])
                p.op("scalar", lambda E, q=q, pq=pq: E.activation(
                    sb_xinp[:, 16 + 512 * q:16 + 512 * (q + 1)], pq, AF.Copy), "se")
            else:
                if q == 0:
                    wait_slot_stores("scalar", "sp")
                p.op("scalar", lambda E, q=q, pq=pq: E.activation(
                    sb_ct0[:, 512 * q:512 * (q + 1)], pq, AF.Silu), "se")
        if g < 8:
            p.wait("vector", "se", p.c["se"])
            if g == 0:
                p.op("vector", lambda E: E.memset(sb_xinp[:, 0:16], 0.0))
            x0, x1 = sb_xinp[:, 13:13 + T], sb_xinp[:, 14:14 + T]
            x2, x3 = sb_xinp[:, 15:15 + T], sb_xinp[:, 16:16 + T]
            p.wait("vector", "te", p.c["te"])
            p.op("vector", lambda E, g=g, x0=x0: E.tensor_scalar_mul(sb_ct0, x0, sb_convw[:, 4 * g:4 * g + 1]))
            p.op("vector", lambda E, g=g, x1=x1: E.scalar_tensor_tensor(
                sb_ct1, x1, sb_convw[:, 4 * g + 1:4 * g + 2], sb_ct0, ALU.mult, ALU.add))
            p.op("vector", lambda E, g=g, x2=x2: E.scalar_tensor_tensor(
                sb_ct0, x2, sb_convw[:, 4 * g + 2:4 * g + 3], sb_ct1, ALU.mult, ALU.add))
            p.op("vector", lambda E, g=g, x3=x3: E.scalar_tensor_tensor(
                sb_ct1, x3, sb_convw[:, 4 * g + 3:4 * g + 4], sb_ct0, ALU.mult, ALU.add), "ve")
            p.wait("scalar", "ve", p.c["ve"])
            wait_slot_stores("scalar", "sp")
            p.wait("scalar", "te", p.c["te"])
            p.op("scalar", lambda E, g=g: E.activation(sb_xcg, sb_ct1, AF.Silu, bias=sb_convb[:, g:g + 1]), "se")
            se_xcv = p.c["se"]
            p.wait("scalar", "se", se_xcv)
            p.dma("scalar", d_xconv[g], sb_xcg, "sp")
            p.wait("tensor", "se", se_xcv)
            for q in range(TQ):
                p.op("tensor", lambda E, g=g, q=q: E.matmul(
                    ps_big[0:33, 512 * q:512 * (q + 1)], sb_xpj[:, 33 * g:33 * (g + 1)],
                    sb_xcg[:, 512 * q:512 * (q + 1)],
                    start=(g == 0), stop=(g == 7)), "te" if q == TQ - 1 else None)
        else:
            p.wait("scalar", "se", p.c["se"])
            p.dma("scalar", d_sz[g - 8], sb_ct0, "sp")
    p.wait("scalar", "te", p.c["te"])
    p.op("scalar", lambda E: E.activation(sb_ssm[0:33, :], ps_big[0:33, :], AF.Copy), "se")
    se_ssm = p.c["se"]
    p.wait("scalar", "se", se_ssm)
    p.dma("scalar", d_ssm, sb_ssm[0:33, :], "sp")
    sp_ssm = p.c["sp"]

    if tap == "p2":
        p.wait("scalar", "sp", p.c["sp"])
        p.dma("scalar", d_dev[:, 0:16384].rearrange("p (g t) -> p g t", g=8), d_xconv.rearrange("g p t -> p g t"), "st")
        p.dma("scalar", d_dev[0:33, 16384:18432], d_ssm, "st")
        p.dma("scalar", d_dev[:, 18432:34816].rearrange("p (g t) -> p g t", g=8), d_sz.rearrange("g p t -> p g t"), "st")
        return _finish(nc, stack, p)

    p.wait("sync", "sp", sp_ssm)
    ld_sbc = p.dma("sync", sb_sbc, d_ssm[32:33, :].partition_broadcast(128), "ld")
    p.wait("sync", "se", p.c["se"])
    ld_wo = p.dma("sync", sb_wfs.rearrange("p (j e) -> p j e", j=8),
                  d_wout.rearrange("(j p) e -> p j e", p=128), "ld")
    p.wait("scalar", "ld", ld_wo)
    for j in range(G):
        p.op("scalar", lambda E, j=j: E.activation(
            sb_woutb[:, 1024 * j:1024 * (j + 1)], sb_wfs[:, 1024 * j:1024 * (j + 1)], AF.Copy), "se")
    se_wout = p.c["se"]
    p.wait("scalar", "ld", ld_wo)
    p.wait("vector", "ld", ld_wo)
    p.wait("vector", "sp", p.c["sp"])
    for g in range(G):
        wait_slot_stores("scalar", "sp")
        p.wait("vector", "sp", p.c["sp"])
        p.op("vector", lambda E, g=g: E.tensor_scalar(
            sb_ct0, sb_sbc, sb_dtw[:, g:g + 1], sb_dtb[:, g:g + 1], ALU.mult, ALU.add), "ve")
        p.wait("scalar", "ve", p.c["ve"])
        p.op("scalar", lambda E: E.activation(sb_ct1, sb_ct0, AF.Abs))
        p.op("scalar", lambda E: E.activation(sb_ct1, sb_ct1, AF.Exp, scale=-1.0))
        p.op("scalar", lambda E: E.activation(sb_ct1, sb_ct1, AF.Ln, bias=1.0))
        p.op("scalar", lambda E: E.activation(sb_ct0, sb_ct0, AF.Relu), "se")
        p.wait("vector", "se", p.c["se"])
        p.op("vector", lambda E: E.tensor_add(sb_ct0, sb_ct0, sb_ct1), "ve")
        p.wait("scalar", "ve", p.c["ve"])
        p.dma("scalar", d_dt[g], sb_ct0, "sp")

    if tap == "p3":
        p.wait("scalar", "sp", p.c["sp"])
        p.dma("scalar", d_dev[:, 0:16384].rearrange("p (g t) -> p g t", g=8), d_dt.rearrange("g p t -> p g t"), "st")
        return _finish(nc, stack, p)

    p.op("vector", lambda E: E.memset(sb_hcar, 0.0), "ve")
    for c in range(NCH):
        c0 = TC * c
        p.wait("sync", "ve", p.c["ve"])
        p.wait("sync", "ge", p.c["ge"])
        p.wait("sync", "sp", p.c["sp"])
        bsrc = d_ssm[0:16, c0:c0 + TC].unsqueeze(0).broadcast_to([128, 16, TC])
        csrc = d_ssm[16:32, c0:c0 + TC].unsqueeze(0).broadcast_to([128, 16, TC])
        lsem = f"l{c % 2}"
        p.dma("sync", sb_Bbc.rearrange("p (n t) -> p n t", n=16), bsrc, lsem)
        p.dma("sync", sb_Cbc.rearrange("p (n t) -> p n t", n=16), csrc, lsem)
        p.dma("sync", sb_xcs.rearrange("p (g t) -> p g t", g=8),
              d_xconv[:, :, c0:c0 + TC].rearrange("g p t -> p g t"), lsem)
        p.dma("sync", sb_dts.rearrange("p (g t) -> p g t", g=8),
              d_dt[:, :, c0:c0 + TC].rearrange("g p t -> p g t"), lsem)
        p.dma("sync", sb_szs.rearrange("p (g t) -> p g t", g=8),
              d_sz[:, :, c0:c0 + TC].rearrange("g p t -> p g t"), lsem)
        ld_grp = p.dma("sync", sb_xres.rearrange("p (i d) -> p i d", i=2),
                       d_x.rearrange("(i p) d -> p i d", p=128)[:, 2 * c:2 * c + 2, :], lsem)
        p.wait("vector", "te", p.c["te"])
        for g in range(G):
            dts = sb_dts[:, TC * g:TC * (g + 1)]
            xcs = sb_xcs[:, TC * g:TC * (g + 1)]
            p.wait("scalar", lsem, ld_grp)
            p.wait("scalar", "ve", p.c["ve"])
            for n in range(NSS):
                p.op("scalar", lambda E, g=g, n=n, dts=dts: E.activation(
                    sb_dA[:, TC * n:TC * (n + 1)], dts, AF.Exp,
                    scale=sb_aneg[:, 16 * g + n:16 * g + n + 1]),
                    "se" if n == NSS - 1 else None)
            p.op("scalar", lambda E, g=g, dts=dts: E.activation(
                sb_e0, sb_aneg[:, 16 * g:16 * (g + 1)], AF.Exp, scale=dts[:, 0:1]), "se")
            se_e0 = p.c["se"]
            p.wait("vector", lsem, ld_grp)
            p.wait("vector", "ge", p.c["ge"])
            p.op("vector", lambda E, dts=dts, xcs=xcs: E.tensor_mul(sb_ug, dts, xcs), "ve")
            ve_u = p.c["ve"]
            p.wait("gpsimd", lsem, ld_grp)
            p.wait("gpsimd", "ve", ve_u)
            u3 = sb_ug.unsqueeze(1).broadcast_to([128, 16, TC])
            p.op("gpsimd", lambda E, u3=u3: E.tensor_tensor(
                sb_dBx.rearrange("p (n t) -> p n t", n=16),
                sb_Bbc.rearrange("p (n t) -> p n t", n=16), u3, ALU.mult), "ge")
            ge_dbx = p.c["ge"]
            dA3 = sb_dA.rearrange("p (n t) -> p n t", n=16)
            dBx3 = sb_dBx.rearrange("p (n t) -> p n t", n=16)
            p.wait("vector", "se", se_e0)
            p.op("vector", lambda E, g=g: E.tensor_mul(sb_fixv, sb_e0, sb_hcar[:, 16 * g:16 * (g + 1)]))
            p.wait("vector", "ge", ge_dbx)
            p.op("vector", lambda E, dBx3=dBx3: E.tensor_add(dBx3[:, :, 0], dBx3[:, :, 0], sb_fixv))
            p.op("vector", lambda E, dA3=dA3: E.memset(dA3[:, :, 0], 0.0))
            p.op("vector", lambda E: E.tensor_tensor_scan(sb_h, sb_dA, sb_dBx, 0.0, ALU.mult, ALU.add))
            h3 = sb_h.rearrange("p (n t) -> p n t", n=16)
            p.op("vector", lambda E, g=g, h3=h3: E.tensor_copy(sb_hcar[:, 16 * g:16 * (g + 1)], h3[:, :, TC - 1]), "ve")
            ve_scan = p.c["ve"]
            p.wait("gpsimd", "ve", ve_scan)
            p.op("gpsimd", lambda E: E.tensor_tensor(sb_dBx, sb_h, sb_Cbc, ALU.mult), "ge")
            ge_hc = p.c["ge"]
            p.wait("vector", "ge", ge_hc)
            hc3 = sb_dBx.rearrange("p (n t) -> p t n", n=16)
            p.op("vector", lambda E, hc3=hc3: E.tensor_reduce(sb_yg, hc3, AX.X, ALU.add))
            ygslot = sb_ygt[:, 2048 * (c % 2) + 256 * g:2048 * (c % 2) + 256 * (g + 1)]
            p.op("vector", lambda E, g=g, ygslot=ygslot: E.tensor_mul(
                ygslot, sb_yg, sb_szs[:, TC * g:TC * (g + 1)]), "ve")
        ve_chunk = p.c["ve"]
        p.wait("tensor", "ve", ve_chunk)
        if c == 0:
            p.wait("tensor", "se", se_wout)
        for ti in range(2):
            for eh in range(2):
                pq = ps[2 * ti + eh]
                for g in range(G):
                    stat = sb_ygt[:, 2048 * (c % 2) + 256 * g + 128 * ti:
                                  2048 * (c % 2) + 256 * g + 128 * (ti + 1)]
                    p.op("tensor", lambda E, stat=stat, g=g, eh=eh, pq=pq: E.matmul(
                        pq, stat, sb_woutb[:, 1024 * g + 512 * eh:1024 * g + 512 * (eh + 1)],
                        start=(g == 0), stop=(g == G - 1)), "te" if g == G - 1 else None)
        te_op = p.c["te"]
        p.wait("vector", "te", te_op)
        p.wait("vector", lsem, ld_grp)
        p.wait("vector", f"s{(c - 1) % 2}", p.c[f"s{(c - 1) % 2}"] if c > 0 else 0)
        for ti in range(2):
            for eh in range(2):
                pq = ps[2 * ti + eh]
                dst = sb_mact[ti][:, 512 * eh:512 * (eh + 1)]
                p.op("vector", lambda E, pq=pq, dst=dst, ti=ti, eh=eh: E.scalar_tensor_tensor(
                    dst, pq, 1.0, sb_xres[:, 1024 * ti + 512 * eh:1024 * ti + 512 * (eh + 1)],
                    ALU.mult, ALU.add), "ve")
        p.wait("scalar", "ve", p.c["ve"])
        wait_slot_stores("scalar", f"s{c % 2}")
        for ti in range(2):
            p.dma("scalar", d_macc[128 * (2 * c + ti):128 * (2 * c + ti + 1), :],
                  sb_mact[ti], f"s{c % 2}")

    if tap == "p4":
        p.wait("scalar", "sp", p.c["sp"])
        p.dma("scalar", d_dev[:, 0:16384].rearrange("p (i d) -> p i d", i=16), d_macc.rearrange("(i p) d -> p i d", p=128), "st")
        return _finish(nc, stack, p)

    p.wait("sync", "cc", cc_ar)
    p.wait("sync", "te", p.c["te"])
    for q in range(TQ):
        for i in range(MT):
            lb, db, dspb = sb_lgt[i % 2], sb_dnt[i % 2], sb_dsp[i % 2]
            if i >= 2 or q > 0:
                p.wait("sync", "se", p.c["se"])
            p.dma("sync", lb, d_logits[128 * i:128 * (i + 1), 512 * q:512 * (q + 1)], f"l{i % 2}")
            ld_lq = p.dma("sync", db, d_den[128 * i:128 * (i + 1), 512 * q:512 * (q + 1)], f"l{i % 2}")
            p.wait("scalar", f"l{i % 2}", ld_lq)
            p.op("scalar", lambda E, db=db: E.activation(db, db, AF.Ln), "se")
            p.wait("vector", "se", p.c["se"])
            p.op("vector", lambda E, lb=lb, db=db: E.tensor_sub(lb, lb, db), "ve")
            p.wait("scalar", "ve", p.c["ve"])
            p.wait("scalar", "te", p.c["te"])
            p.op("scalar", lambda E, lb=lb, dspb=dspb: E.activation(dspb, lb, AF.Exp), "se")
            se_dsp = p.c["se"]
            if tap == "p5q0" and q == 0:
                p.wait("scalar", "se", se_dsp)
                p.dma("scalar", d_dev[:, 47104 + 256 * i:47104 + 256 * (i + 1)], dspb.bitcast(F32), "st")
                p.dma("scalar", d_dev[:, 51200 + 512 * i:51200 + 512 * (i + 1)], db, "st")
                p.dma("scalar", d_dev[:, 59392 + 512 * i:59392 + 512 * (i + 1)], lb, "st")
            p.wait("tensor", "se", se_dsp)
            if i == 0:
                p.wait("tensor", "ve", p.c["ve"])
            for j in range(G):
                p.op("tensor", lambda E, i=i, j=j, dspb=dspb: E.matmul(
                    ps8[j], sb_xntok[:, 1024 * i + 128 * j:1024 * i + 128 * (j + 1)],
                    dspb, start=(i == 0), stop=(i == MT - 1)), "te" if j == G - 1 else None)
        te_sl = p.c["te"]
        p.wait("scalar", "te", te_sl)
        wait_slot_stores("scalar", "sp")
        for j in range(G):
            p.op("scalar", lambda E, j=j: E.activation(sb_sev[:, 512 * j:512 * (j + 1)], ps8[j], AF.Copy), "se")
        p.wait("scalar", "se", p.c["se"])
        sev3 = sb_sev.rearrange("p (j t) -> p j t", j=8)
        for half in range(2):
            r = 2 * q + half
            p.dma("scalar", d_ssnd[r].rearrange("(j p) t -> p j t", p=128),
                  sev3[:, :, 256 * half:256 * (half + 1)], "sp")
    sp_slots = p.c["sp"]
    p.wait("gpsimd", "sp", sp_slots)
    p.op("gpsimd", lambda E: E.collective_compute(
        "AllGather", ALU.bypass, replica_groups=[list(range(8))],
        ins=[d_ssnd.opt()], outs=[d_sag.opt()]), "cc2")
    cc_s = p.c["cc2"]

    if tap == "p5":
        p.wait("scalar", "cc2", cc_s)
        p.dma("scalar", d_dev[:, 0:8192], d_sag.bitcast(F32)[0, 0][0:128, :].unsqueeze(1).broadcast_to([128, 64, 128]), "st")
        return _finish(nc, stack, p)

    p.wait("gpsimd", "cc2", cc_s)
    srcv5 = sb_srcv.rearrange("p (j n b q) -> p j n b q", j=8, n=8, b=8)
    stg4 = sb_stg.rearrange("p (b j t) -> p b j t", b=8, j=8)
    for b8 in range(8):
        p.c["gl"] += 16
        ld_sr = p.c["gl"]
        p.emit("gpsimd", lambda E, b8=b8: E.dma_start(
            out=stg4[:, b8], in_=dyn_sr[0][b8], bounds_check=None).then_inc(p.sems["gl"], 16))
    p.wait("vector", "gl", ld_sr)
    stg_r = sb_stg.rearrange("p (b j n q) -> p j n b q", b=8, j=8, n=8)
    for j in range(G):
        p.op("vector", lambda E, j=j: E.tensor_copy(srcv5[:, j], stg_r[:, j]),
             "ve" if j == G - 1 else None)
    ve_srcv = p.c["ve"]
    if tap == "p6a":
        p.wait("scalar", "gl", ld_sr)
        p.dma("scalar", d_dev[:, 0:8192], sb_srcv.bitcast(F32), "st")
        return _finish(nc, stack, p)

    for n in range(8):
        p.wait("sync", "se", p.c["se"])
        ld_w = p.dma("sync", sb_wfs.rearrange("p (j e) -> p j e", j=8),
                     d_wffn[n].rearrange("(j p) e -> p j e", p=128), f"l{n % 2}")
        p.wait("scalar", f"l{n % 2}", ld_w)
        for j in range(G):
            p.op("scalar", lambda E, j=j: E.activation(
                sb_wfb[:, 1024 * j:1024 * (j + 1)], sb_wfs[:, 1024 * j:1024 * (j + 1)], AF.Copy), "se")
        se_wf = p.c["se"]
        p.wait("sync", "ve", p.c["ve"])
        ld_bb = p.dma("sync", sb_bbc, d_bffn[n:n + 1, :].partition_broadcast(128), "ld")
        p.wait("tensor", "se", se_wf)
        p.wait("tensor", "ld", ld_bb)
        p.wait("tensor", "ve", ve_srcv)
        p.wait("tensor", "ve", p.c["ve"])
        for rh in range(2):
            for eh in range(2):
                pq = ps[2 * rh + eh]
                for j in range(G):
                    stat = sb_srcv[:, (j * 8 + n) * 256 + 128 * rh:
                                   (j * 8 + n) * 256 + 128 * (rh + 1)]
                    p.op("tensor", lambda E, stat=stat, j=j, eh=eh, pq=pq: E.matmul(
                        pq, stat, sb_wfb[:, 1024 * j + 512 * eh:1024 * j + 512 * (eh + 1)],
                        start=(j == 0), stop=(j == G - 1)), "te" if j == G - 1 else None)
        te_y = p.c["te"]
        p.wait("vector", "te", te_y)
        p.wait("vector", "ld", ld_bb)
        for rh in range(2):
            for eh in range(2):
                pq = ps[2 * rh + eh]
                p.wait("vector", "se", p.c["se"])
                p.op("vector", lambda E, pq=pq, rh=rh, eh=eh: E.scalar_tensor_tensor(
                    sb_yev[rh], pq, 1.0, sb_bbc[:, 512 * eh:512 * (eh + 1)], ALU.mult, ALU.add), "ve")
                p.wait("scalar", "ve", p.c["ve"])
                wait_slot_stores("scalar", "sp")
                p.op("scalar", lambda E, rh=rh: E.activation(sb_ybf, sb_yev[rh], AF.Copy), "se")
                p.wait("scalar", "se", p.c["se"])
                for bb in range(4):
                    b = 4 * rh + bb
                    p.dma("scalar", d_ysnd[b, 32 * n:32 * (n + 1), 512 * eh:512 * (eh + 1)],
                          sb_ybf[32 * bb:32 * (bb + 1), :], "sp")
    sp_y = p.c["sp"]
    p.wait("gpsimd", "sp", sp_y)
    p.op("gpsimd", lambda E: E.collective_compute(
        "AllGather", ALU.bypass, replica_groups=[list(range(8))],
        ins=[d_ysnd.opt()], outs=[d_yag.opt()]), "cc3")
    cc_y = p.c["cc3"]

    if tap == "p6":
        p.wait("scalar", "cc3", cc_y)
        p.dma("scalar", d_dev[:, 0:1024].rearrange("p (k e) -> p k e", k=2),
              d_yag.bitcast(F32)[0, 0].rearrange("(k p) e -> p k e", p=128), "st")
        return _finish(nc, stack, p)

    p.wait("gpsimd", "cc3", cc_y)
    yb4 = sb_yb.rearrange("p (r k e) -> p r k e", r=8, k=2)
    for r in range(8):
        p.c["gl"] += 16
        ld_yb = p.c["gl"]
        p.emit("gpsimd", lambda E, r=r: E.dma_start(
            out=yb4[:, r], in_=dyn_y[0][r], bounds_check=None).then_inc(p.sems["gl"], 16))

    if tap == "p7":
        p.wait("scalar", "ve", p.c["ve"])
        p.wait("scalar", "gl", ld_yb)
        p.dma("scalar", d_dev[:, 0:16], sb_rs, "st")
        p.dma("scalar", d_dev[:, 16:32], sb_rrs, "st")
        p.dma("scalar", d_dev[:, 1024:9216], sb_yb.bitcast(F32), "st")
    p.wait("sync", "sp", p.c["sp"])
    p.wait("sync", "s0", p.c["s0"])
    p.wait("sync", "s1", p.c["s1"])
    for i in range(MT):
        p.wait("sync", "se", p.c["se"])
        p.wait("sync", "ve", p.c["ve"])
        p.dma("sync", sb_Ei, d_logits[128 * i:128 * (i + 1), :], f"l{i % 2}")
        ld_m = p.dma("sync", sb_mact[i % 2], d_macc[128 * i:128 * (i + 1), :], f"l{i % 2}")
        p.wait("scalar", f"l{i % 2}", ld_m)
        p.wait("scalar", "ve", p.c["ve"])
        p.op("scalar", lambda E: E.activation(sb_Ei, sb_Ei, AF.Exp), "se")
        p.wait("vector", "se", p.c["se"])
        p.wait("vector", "te", p.c["te"])
        p.op("vector", lambda E, i=i: E.reduce_sum(sb_rs[:, i:i + 1], sb_Ei, AX.X), "ve")
        p.wait("scalar", "ve", p.c["ve"])
        p.op("scalar", lambda E, i=i: E.activation(sb_rrs[:, i:i + 1], sb_rs[:, i:i + 1], AF.Ln, scale=1.0), "se")
        p.op("scalar", lambda E: E.activation(sb_cmbf, sb_Ei, AF.Ln), "se")
        p.wait("vector", "se", p.c["se"])
        p.op("vector", lambda E, i=i: E.tensor_scalar_mul(sb_rrs[:, i:i + 1], sb_rrs[:, i:i + 1], -1.0), "ve")
        p.wait("scalar", "ve", p.c["ve"])
        p.op("scalar", lambda E, i=i: E.activation(sb_cmbf, sb_cmbf, AF.Exp, bias=sb_rrs[:, i:i + 1]), "se")
        p.op("scalar", lambda E: E.engine_nop(), "se") if False else None
        p.wait("vector", "se", p.c["se"])
        p.op("vector", lambda E: E.tensor_copy(sb_tmp1[:, 0:1], sb_cmbf[:, 0:1]), "ve")
        ve_cmb = p.c["ve"]
        if tap == "p7" and i == 14:
            p.wait("scalar", "ve", ve_cmb)
            p.dma("scalar", d_dev[:, 9216:11264], sb_cmbf, "st")
            p.dma("scalar", d_dev[:, 11264:11280], sb_rs, "st")
            p.dma("scalar", d_dev[:, 11280:11296], sb_rrs, "st")
            p.dma("scalar", d_dev[:, 11296:13344], sb_Ei, "st")
        for k in range(16):
            pst = ps[2 + (k % 2)][:, 0:128]
            p.wait("tensor", "ve", ve_cmb)
            p.wait("tensor", "se", p.c["se"])
            p.op("tensor", lambda E, k=k, pst=pst: E.transpose(
                pst, sb_cmbf[:, 128 * k:128 * (k + 1)], sb_idf), "te")
            p.wait("scalar", "te", p.c["te"])
            p.op("scalar", lambda E, k=k, pst=pst: E.activation(
                sb_cnp[:, 128 * k:128 * (k + 1)], pst, AF.Copy), "se")
        se_cnp = p.c["se"]
        p.wait("tensor", "se", se_cnp)
        p.wait("tensor", "gl", ld_yb)
        p.wait("tensor", "ve", p.c["ve"])
        for eh in range(2):
            pq = ps[eh]
            for k in range(16):
                p.op("tensor", lambda E, k=k, eh=eh, pq=pq: E.matmul(
                    pq, sb_cnp[:, 128 * k:128 * (k + 1)],
                    sb_yb[:, 1024 * k + 512 * eh:1024 * k + 512 * (eh + 1)],
                    start=(k == 0), stop=(k == 15)), "te" if k == 15 else None)
        te_out = p.c["te"]
        p.wait("vector", "te", te_out)
        p.wait("vector", f"l{i % 2}", ld_m)
        for eh in range(2):
            p.op("vector", lambda E, i=i, eh=eh: E.scalar_tensor_tensor(
                sb_outt[i % 2][:, 512 * eh:512 * (eh + 1)], ps[eh], 1.0,
                sb_mact[i % 2][:, 512 * eh:512 * (eh + 1)], ALU.mult, ALU.add), "ve")
        p.wait("scalar", "ve", p.c["ve"])
        wait_slot_stores("scalar", f"s{i % 2}")
        p.dma("scalar", d_out[128 * i:128 * (i + 1), :], sb_outt[i % 2], f"s{i % 2}")

    if tap in ("post", "p5q0"):
        p.wait("scalar", "st", p.c["st"])
        p.dma("scalar", d_dev[:, 0:8192], sb_yb.bitcast(F32), "st")
        for r in range(4):
            p.dma("scalar",
                  d_dev[:, 8192 + 1024 * r:8192 + 1024 * (r + 1)].rearrange("p (k e) -> p k e", k=2),
                  d_yag.bitcast(F32)[r, 0].rearrange("(k p) e -> p k e", p=128), "st")
        p.dma("scalar", d_dev[:, 12288:13312], d_macc[0:128, :], "st")
        p.dma("scalar", d_dev[:, 13312:15360], sb_cmbf, "st")
        p.dma("scalar", d_dev[:, 15360:15376], sb_rs, "st")
        p.dma("scalar", d_dev[:, 15376:15392], sb_rrs, "st")
        p.dma("scalar", d_dev[:, 15392:15904], d_ysnd.bitcast(F32)[0, 0:128, :], "st")
        for r in range(8):
            p.dma("scalar", d_dev[:, 36864 + 1024 * r:36864 + 1024 * (r + 1)].rearrange("p (j t) -> p j t", j=8),
                  d_ssnd.bitcast(F32)[r].rearrange("(j p) t -> p j t", p=128), "st")
        p.dma("scalar", d_dev[:, 45056:45568], d_den[0:128, 0:512], "st")
        p.dma("scalar", d_dev[:, 45568:46080], d_den[1920:2048, 0:512], "st")
        p.dma("scalar", d_dev[:, 46080:46592], d_logits[0:128, 0:512], "st")
        for s in range(8):
            p.dma("scalar", d_dev[:, 24576 + 1024 * s:24576 + 1024 * (s + 1)].rearrange("p (j t) -> p j t", j=8),
                  d_sag.bitcast(F32)[s, 0].rearrange("(j p) t -> p j t", p=128), "st")
        p.dma("scalar", d_dev[:, 32768:33792].rearrange("p (j t) -> p j t", j=8),
              d_ssnd.bitcast(F32)[0].rearrange("(j p) t -> p j t", p=128), "st")
        p.dma("scalar", d_dev[:, 33792:34816], d_den[0:128, 1024:2048], "st")
        p.dma("scalar", d_dev[:, 34816:35840], d_logits[0:128, 1024:2048], "st")

    return _finish(nc, stack, p)


def _finish(nc, stack, p):
    p.wait("sync", "st", p.c["st"])
    p.wait("sync", "sp", p.c["sp"])
    p.wait("sync", "s0", p.c["s0"])
    p.wait("sync", "s1", p.c["s1"])
    with nc.Block() as block:
        @block.sync
        def _(E):
            for fn in p.ops["sync"]:
                fn(E)

        @block.scalar
        def _(E):
            for fn in p.ops["scalar"]:
                fn(E)

        @block.vector
        def _(E):
            for fn in p.ops["vector"]:
                fn(E)

        @block.tensor
        def _(E):
            for fn in p.ops["tensor"]:
                fn(E)

        @block.gpsimd
        def _(E):
            for fn in p.ops["gpsimd"]:
                fn(E)
    stack.close()
    return nc


def kernel(L0_attn_x, L0_attn_in_proj_w, L0_attn_conv_w, L0_attn_conv_b,
           L0_attn_x_proj_w, L0_attn_dt_proj_w, L0_attn_dt_proj_b,
           L0_attn_A_log, L0_attn_out_proj_w, L0_ffn_phi, L0_ffn_W, L0_ffn_b):
    import ml_dtypes
    x = np.ascontiguousarray(np.asarray(L0_attn_x, dtype=np.float32))
    w_in_t = np.ascontiguousarray(np.asarray(L0_attn_in_proj_w, dtype=np.float32).T)
    w_out_t = np.ascontiguousarray(np.asarray(L0_attn_out_proj_w, dtype=np.float32).T)
    phi = np.ascontiguousarray(np.asarray(L0_ffn_phi, dtype=np.float32).reshape(D, NP))
    xproj_t = np.ascontiguousarray(np.asarray(L0_attn_x_proj_w, dtype=np.float32).T)
    conv_w = np.ascontiguousarray(
        np.asarray(L0_attn_conv_w, dtype=np.float32)[:, 0, :].reshape(8, 128, 4).transpose(1, 0, 2).reshape(128, 32))
    conv_b = np.ascontiguousarray(np.asarray(L0_attn_conv_b, dtype=np.float32).reshape(8, 128).T)
    dt_w = np.ascontiguousarray(np.asarray(L0_attn_dt_proj_w, dtype=np.float32).reshape(8, 128).T)
    dt_b = np.ascontiguousarray(np.asarray(L0_attn_dt_proj_b, dtype=np.float32).reshape(8, 128).T)
    a_log = np.ascontiguousarray(
        np.asarray(L0_attn_A_log, dtype=np.float32).reshape(8, 128, NSS).transpose(1, 0, 2).reshape(128, 128))
    W = np.asarray(L0_ffn_W, dtype=np.float32)
    bffn = np.asarray(L0_ffn_b, dtype=np.float32)
    idf = np.eye(128, dtype=np.float32)
    idb = np.eye(128, dtype=ml_dtypes.bfloat16)

    with allow_ici_all_to_all():
        nc = build(tap=DEV_TAP)
    in_maps = []
    for c in range(8):
        in_maps.append({
            "x": np.ascontiguousarray(x[c]),
            "w_in_t": w_in_t, "w_out_t": w_out_t, "phi": phi,
            "xproj_t": xproj_t, "conv_w": conv_w, "conv_b": conv_b,
            "dt_w": dt_w, "dt_b": dt_b, "a_log": a_log,
            "w_ffn": np.ascontiguousarray(W[8 * c:8 * (c + 1)]),
            "b_ffn": np.ascontiguousarray(bffn[8 * c:8 * (c + 1)]),
            "ident_f": idf, "ident_b": idb,
            "rank": np.array([[c]], dtype=np.uint32),
        })
    with allow_ici_all_to_all():
        res = run_bass_kernel_spmd(nc, in_maps, core_ids=list(range(8)))
    outs = [r["out"] if not isinstance(r["out"], dict) else r["out"] for r in res.results]
    return np.stack([np.asarray(o) for o in outs])


# version 02
# speedup vs baseline: 1.0112x; 1.0112x over previous
import contextlib
import numpy as np
import concourse.bass as bass
import concourse.mybir as mybir
from concourse.bass_utils import run_bass_kernel_spmd
from concourse.aot_env import allow_ici_all_to_all

F32 = mybir.dt.float32
BF16 = mybir.dt.bfloat16
AF = mybir.ActivationFunctionType
ALU = mybir.AluOpType
AX = mybir.AxisListType

T = 2048
D = 1024
NP = 2048
NSS = 16
G = 8
TQ = 4
TC = 256
NCH = T // TC
MT = 16

DEV_TAP = None

ENGS = ("sync", "scalar", "vector", "tensor", "gpsimd")


class P:

    def __init__(self, nc):
        self.nc = nc
        self.ops = {e: [] for e in ENGS}
        self.c = {}
        self.sems = {}

    def sem(self, stack, name):
        self.sems[name] = stack.enter_context(self.nc.semaphore(name))
        self.c[name] = 0

    def emit(self, eng, fn):
        self.ops[eng].append(fn)

    def wait(self, eng, sname, val=None):
        v = self.c[sname] if val is None else val
        if v <= 0:
            return
        s = self.sems[sname]
        self.emit(eng, lambda E, s=s, v=v: E.wait_ge(s, v))

    def dma(self, eng, out, in_, sname, slow=False):
        self.c[sname] += 16
        s = self.sems[sname]
        self.emit(eng, lambda E, o=out, i=in_, s=s, sl=slow: E.dma_start(
            out=o, in_=i, allow_slow_non_contiguous=sl).then_inc(s, 16))
        return self.c[sname]

    def op(self, eng, fn, sname=None):
        if sname is None:
            self.emit(eng, lambda E, f=fn: f(E))
            return None
        self.c[sname] += 1
        s = self.sems[sname]
        self.emit(eng, lambda E, f=fn, s=s: f(E).then_inc(s, 1))
        return self.c[sname]


def build(tap=None):
    nc = bass.Bass("TRN2", target_bir_lowering=False, debug=False)
    stack = contextlib.ExitStack()

    d_x = nc.dram_tensor("x", [T, D], F32, kind="ExternalInput").ap()
    d_win = nc.dram_tensor("w_in_t", [D, 2 * D], F32, kind="ExternalInput").ap()
    d_wout = nc.dram_tensor("w_out_t", [D, D], F32, kind="ExternalInput").ap()
    d_phi = nc.dram_tensor("phi", [D, NP], F32, kind="ExternalInput").ap()
    d_xproj = nc.dram_tensor("xproj_t", [D, 33], F32, kind="ExternalInput").ap()
    d_wffn = nc.dram_tensor("w_ffn", [8, D, D], F32, kind="ExternalInput").ap()
    d_bffn = nc.dram_tensor("b_ffn", [8, D], F32, kind="ExternalInput").ap()
    d_convw = nc.dram_tensor("conv_w", [128, 32], F32, kind="ExternalInput").ap()
    d_convb = nc.dram_tensor("conv_b", [128, 8], F32, kind="ExternalInput").ap()
    d_dtw = nc.dram_tensor("dt_w", [128, 8], F32, kind="ExternalInput").ap()
    d_dtb = nc.dram_tensor("dt_b", [128, 8], F32, kind="ExternalInput").ap()
    d_alog = nc.dram_tensor("a_log", [128, 128], F32, kind="ExternalInput").ap()
    d_rank = nc.dram_tensor("rank", [1, 1], mybir.dt.uint32, kind="ExternalInput").ap()
    d_idf = nc.dram_tensor("ident_f", [128, 128], F32, kind="ExternalInput").ap()
    d_idb = nc.dram_tensor("ident_b", [128, 128], BF16, kind="ExternalInput").ap()
    d_out = nc.dram_tensor("out", [T, D], F32, kind="ExternalOutput").ap()

    d_fin = nc.dram_tensor("fence_in", [128, 4], F32).ap()
    d_fout = nc.dram_tensor("fence_out", [128, 4], F32).ap()
    d_gA = nc.dram_tensor("guard_A", [T, 512], F32).ap()
    d_E = nc.dram_tensor("E_scr", [T, NP], F32).ap()
    d_gB = nc.dram_tensor("guard_B", [T, 512], F32).ap()
    d_logits = nc.dram_tensor("logits_scr", [T, NP], F32).ap()
    d_den = nc.dram_tensor("den_scr", [T, NP], F32).ap()
    d_gC = nc.dram_tensor("guard_C", [T, 512], F32).ap()
    d_sz = nc.dram_tensor("sz_scr", [G, 128, T], F32).ap()
    d_xconv = nc.dram_tensor("xconv_scr", [G, 128, T], F32).ap()
    d_dt = nc.dram_tensor("dt_scr", [G, 128, T], F32).ap()
    d_ssm = nc.dram_tensor("ssm_scr", [33, T], F32).ap()
    d_macc = nc.dram_tensor("macc_scr", [T, D], F32).ap()
    d_guard0 = nc.dram_tensor("guard0", [8, D, 256], BF16).ap()
    d_ssnd = nc.dram_tensor("slots_snd", [8, D, 256], BF16).ap()
    d_guard1 = nc.dram_tensor("guard1", [8, D, 256], BF16).ap()
    d_sag = nc.dram_tensor("slots_ag", [8, 8, D, 256], BF16).ap()
    d_guard2 = nc.dram_tensor("guard2", [8, 256, D], BF16).ap()
    d_ysnd = nc.dram_tensor("y_snd", [8, 256, D], BF16).ap()
    d_guard3 = nc.dram_tensor("guard3", [8, 256, D], BF16).ap()
    d_yag = nc.dram_tensor("y_ag", [8, 8, 256, D], BF16).ap()
    d_dev = None
    if tap is not None:
        d_dev = nc.dram_tensor("dev", [128, 81920], F32, kind="ExternalOutput").ap()

    NF32 = 53152
    mega = nc.alloc_sbuf_tensor("mega", [128, NF32], F32).ap()
    cursor = [0]

    def f32v(n_elems, at=None):
        off = cursor[0] if at is None else at
        v = mega[:, off:off + n_elems]
        if at is None:
            cursor[0] = off + n_elems
        return v

    def bf16v(n_elems, at=None):
        assert n_elems % 2 == 0
        off = cursor[0] if at is None else at
        v = mega[:, off:off + n_elems // 2].bitcast(BF16)
        if at is None:
            cursor[0] = off + n_elems // 2
        return v

    sb_aneg = f32v(128)
    sb_convw = f32v(32)
    sb_convb = f32v(8)
    sb_dtw = f32v(8)
    sb_dtb = f32v(8)
    sb_idf = f32v(128)
    sb_idb = bf16v(128)
    sb_mu = f32v(16)
    sb_rstd = f32v(16)
    sb_ln1 = f32v(16)
    sb_ln2 = f32v(16)
    sb_rs = f32v(16)
    sb_rrs = f32v(16)
    sb_hcar = f32v(128)
    sb_e0 = f32v(16)
    sb_fixv = f32v(16)
    sb_xpj = f32v(8 * 33)
    sb_bbc = f32v(1024)
    sb_tmp1 = f32v(16)
    sb_rank = f32v(2).bitcast(mybir.dt.uint32)
    assert cursor[0] <= 1952, cursor[0]
    cursor[0] = 1952
    sb_sbc = f32v(2048)
    sb_ssm = f32v(2048)
    sb_xntok = bf16v(16 * 1024)
    BASE_A = cursor[0]
    sb_xnd = bf16v(8 * 2048, at=BASE_A)
    sb_Bbc = f32v(4096, at=BASE_A)
    sb_Cbc = f32v(4096, at=BASE_A + 4096)
    sb_srcv = bf16v(16 * 1024, at=BASE_A)
    BASE_B = BASE_A + 8192
    sb_stg = bf16v(16 * 1024, at=BASE_B)
    sb_winb = bf16v(8 * 2048, at=BASE_B)
    sb_dA = f32v(4096, at=BASE_B)
    sb_dBx = f32v(4096, at=BASE_B + 4096)
    sb_yb = bf16v(16 * 1024, at=BASE_B)
    BASE_C = BASE_B + 8192
    sb_phib = bf16v(8 * 2048, at=BASE_C)
    sb_h = f32v(4096, at=BASE_C)
    sb_xcs = f32v(2048, at=BASE_C + 4096)
    sb_dts = f32v(2048, at=BASE_C + 6144)
    sb_wfs = f32v(8 * 1024, at=BASE_C)
    BASE_D = BASE_C + 8192
    sb_xt = [f32v(4096, at=BASE_D + 4096 * i) for i in range(2)]
    sb_wstage = f32v(8192, at=BASE_D)
    sb_xinp = f32v(2064, at=BASE_D)
    sb_ct0 = f32v(2048, at=BASE_D + 2176)
    sb_ct1 = f32v(2048, at=BASE_D + 4224)
    sb_xcg = f32v(2048, at=BASE_D + 6272)
    sb_szs = f32v(2048, at=BASE_D)
    sb_ygt = bf16v(2 * 2048, at=BASE_D + 2048)
    sb_yg = f32v(256, at=BASE_D + 4096)
    sb_ug = f32v(256, at=BASE_D + 4352)
    sb_xres = f32v(2048, at=BASE_D + 4608)
    sb_wfb = bf16v(8 * 1024, at=BASE_D)
    sb_yev = [f32v(512, at=BASE_D + 4608 + 512 * i) for i in range(2)]
    sb_ybf = bf16v(512, at=BASE_D + 5632)
    sb_Ei = f32v(2048, at=BASE_D)
    sb_cmbf = f32v(2048, at=BASE_D + 2048)
    sb_cmb = bf16v(512, at=BASE_D + 5632)
    sb_mact = [f32v(1024, at=BASE_D + 4096 + 1024 * i) for i in range(2)]
    sb_outt = [f32v(1024, at=BASE_D + 6144 + 1024 * i) for i in range(2)]
    BASE_E = BASE_D + 8192
    sb_woutb = bf16v(8 * 1024, at=BASE_E)
    sb_Et = [f32v(512, at=BASE_E + 4096 + 512 * i) for i in range(2)]
    sb_Lt = [f32v(512, at=BASE_E + 5120 + 512 * i) for i in range(2)]
    sb_x2 = f32v(1024, at=BASE_E + 4096)
    sb_xnf = f32v(1024, at=BASE_E + 5120)
    sb_cnp = bf16v(16 * 128, at=BASE_E + 4096)
    sb_lgt = [f32v(512, at=BASE_E + 512 * i) for i in range(2)]
    sb_dnt = [f32v(512, at=BASE_E + 1024 + 512 * i) for i in range(2)]
    sb_dsp = [bf16v(512, at=BASE_E + 2048 + 256 * i) for i in range(2)]
    sb_sev = bf16v(8 * 512, at=BASE_E + 2560)
    assert BASE_E + 6144 <= NF32, (BASE_E, NF32)

    ps = [nc.alloc_psum_tensor(f"ps{i}", [128, 512], F32).ap() for i in range(4)]
    ps_big = nc.alloc_psum_tensor("ps_big", [128, 2048], F32).ap()
    ps8 = ps + [ps_big[:, 512 * i:512 * (i + 1)] for i in range(4)]

    p = P(nc)
    for s in ("ld", "l0", "l1", "sp", "s0", "s1", "te", "ve", "se", "ge", "gl",
              "cc", "cc2", "cc3", "st"):
        p.sem(stack, s)

    p.dma("sync", sb_rank[0:1, 0:1], d_rank, "ld")
    ld_rank = p.c["ld"]
    sag_v = d_sag.rearrange("b c (j p) t -> c b p j t", p=128)
    yg5 = d_yag.rearrange("r c (k p) e -> c r p k e", p=128)
    dyn_sr = [None]
    dyn_y = [None]
    def _mk_dyn(E):
        rv = E.value_load(sb_rank[0:1, 0:1])
        dyn_sr[0] = sag_v[bass.ts(rv, 1)].squeeze(0)
        dyn_y[0] = yg5[bass.ts(rv, 1)].squeeze(0)
    p.dma("sync", sb_idf, d_idf, "ld")
    p.dma("sync", sb_idb, d_idb, "ld")
    p.dma("sync", sb_aneg, d_alog, "ld")
    p.dma("sync", sb_convw, d_convw, "ld")
    p.dma("sync", sb_convb, d_convb, "ld")
    p.dma("sync", sb_dtw, d_dtw, "ld")
    p.dma("sync", sb_dtb, d_dtb, "ld")
    p.dma("sync", sb_xpj.rearrange("p (g e) -> p g e", g=8),
          d_xproj.rearrange("(g p) e -> p g e", p=128), "ld")
    ld_params = p.c["ld"]
    p.wait("scalar", "ld", ld_params)
    p.op("scalar", lambda E: E.activation(sb_aneg, sb_aneg, AF.Exp), "se")
    p.wait("gpsimd", "se", p.c["se"])
    p.emit("gpsimd", _mk_dyn)
    p.wait("vector", "se", p.c["se"])
    p.op("vector", lambda E: E.tensor_scalar_mul(sb_aneg, sb_aneg, -1.0), "ve")

    x_r = d_x.rearrange("(i p) d -> p i d", p=128)
    ld_xgrp = []
    ve_ln_done = [0] * 4
    for grp in range(4):
        if grp >= 2:
            p.wait("sync", "ve", ve_ln_done[grp - 2])
        ld_xgrp.append(p.dma("sync", sb_xt[grp % 2].rearrange("p (i d) -> p i d", d=D),
                             x_r[:, 4 * grp:4 * (grp + 1), :], f"l{grp % 2}"))
        for sl in range(4):
            i = 4 * grp + sl
            xt = sb_xt[grp % 2][:, D * sl:D * (sl + 1)]
            p.wait("vector", f"l{grp % 2}", ld_xgrp[grp])
            p.op("vector", lambda E, xt=xt, i=i: E.reduce_sum(sb_ln1[:, i:i + 1], xt, AX.X))
            p.wait("scalar", f"l{grp % 2}", ld_xgrp[grp])
            p.op("scalar", lambda E, xt=xt: E.square(sb_x2, xt), "se")
            p.wait("vector", "se", p.c["se"])
            p.op("vector", lambda E, i=i: E.reduce_sum(sb_ln2[:, i:i + 1], sb_x2, AX.X))
            p.op("vector", lambda E, i=i: E.tensor_scalar_mul(sb_mu[:, i:i + 1], sb_ln1[:, i:i + 1], 1.0 / D))
            p.op("vector", lambda E, i=i: E.tensor_scalar_mul(sb_ln2[:, i:i + 1], sb_ln2[:, i:i + 1], 1.0 / D), "ve")
            p.wait("scalar", "ve", p.c["ve"])
            p.op("scalar", lambda E, i=i: E.square(sb_tmp1[:, 0:1], sb_mu[:, i:i + 1]), "se")
            p.wait("vector", "se", p.c["se"])
            p.op("vector", lambda E, i=i: E.tensor_sub(sb_ln2[:, i:i + 1], sb_ln2[:, i:i + 1], sb_tmp1[:, 0:1]))
            p.op("vector", lambda E, i=i: E.tensor_scalar_add(sb_ln2[:, i:i + 1], sb_ln2[:, i:i + 1], 1e-5), "ve")
            p.wait("scalar", "ve", p.c["ve"])
            p.op("scalar", lambda E, i=i: E.activation(sb_ln1[:, i:i + 1], sb_ln2[:, i:i + 1], AF.Sqrt), "se")
            p.wait("vector", "se", p.c["se"])
            p.op("vector", lambda E, i=i: E.reciprocal(sb_rstd[:, i:i + 1], sb_ln1[:, i:i + 1]))
            p.wait("vector", "te", p.c["te"])
            p.op("vector", lambda E, xt=xt, i=i: E.tensor_scalar(
                sb_xnf, xt, sb_mu[:, i:i + 1], sb_rstd[:, i:i + 1], ALU.subtract, ALU.mult), "ve")
            ve_xn = p.c["ve"]
            p.wait("scalar", "ve", ve_xn)
            p.op("scalar", lambda E, i=i: E.activation(sb_xntok[:, D * i:D * (i + 1)], sb_xnf, AF.Copy), "se")
            for j in range(G):
                pst = ps[2 + (j % 2)][:, :128].bitcast(BF16)[:, :128]
                p.wait("tensor", "se", p.c["se"])
                p.op("tensor", lambda E, i=i, j=j, pst=pst: E.transpose(
                    pst, sb_xntok[:, D * i + 128 * j:D * i + 128 * (j + 1)], sb_idb), "te")
                p.wait("scalar", "te", p.c["te"])
                p.op("scalar", lambda E, i=i, j=j, pst=pst: E.activation(
                    sb_xnd[:, 2048 * j + 128 * i:2048 * j + 128 * (i + 1)], pst, AF.Copy), "se")
        ve_ln_done[grp] = p.c["ve"]
    se_xn_all = p.c["se"]

    if tap == "p0":
        p.wait("scalar", "se", p.c["se"])
        p.wait("scalar", "ve", p.c["ve"])
        p.dma("scalar", d_dev[:, 0:8192], sb_xnd.bitcast(F32), "st")
        p.dma("scalar", d_dev[:, 8192:16384], sb_xntok.bitcast(F32), "st")
        p.dma("scalar", d_dev[:, 16384:16400], sb_mu, "st")
        p.dma("scalar", d_dev[:, 16400:16416], sb_rstd, "st")
        return _finish(nc, stack, p)

    p.wait("sync", "ve", p.c["ve"])
    for wname, src, dst in (("win", d_win, sb_winb), ("phi", d_phi, sb_phib)):
        for half in range(2):
            p.wait("sync", "se", p.c["se"])
            ldw = p.dma("sync", sb_wstage.rearrange("p (j e) -> p j e", j=4),
                        src.rearrange("(j p) e -> p j e", p=128)[:, 4 * half:4 * (half + 1), :], "ld")
            p.wait("scalar", "ld", ldw)
            for jj in range(4):
                j = 4 * half + jj
                p.op("scalar", lambda E, jj=jj, j=j, dst=dst: E.activation(
                    dst[:, 2048 * j:2048 * (j + 1)], sb_wstage[:, 2048 * jj:2048 * (jj + 1)], AF.Copy), "se")

    p.wait("tensor", "se", p.c["se"])
    for j in range(2):
        p.op("tensor", lambda E, j=j: E.matmul(
            ps[3], sb_xnd[:, 2048 * j:2048 * j + 128],
            sb_phib[:, 2048 * j:2048 * j + 512],
            start=(j == 0), stop=(j == 1)), "te" if j == 1 else None)
    p.wait("vector", "te", p.c["te"])
    p.op("vector", lambda E: E.tensor_copy(sb_tmp1[:, 0:1], ps[3][:, 0:1]), "ve")

    last_store = {}

    def guarded_store(tag, bufs, idx):
        key = (tag, idx % len(bufs))
        if key in last_store:
            p.wait("scalar", "sp", last_store[key])

    def wait_slot_stores(eng, sem):
        p.wait(eng, sem, p.c[sem])

    for i in range(MT):
        for q in range(TQ):
            pq = ps[q % 2]
            p.wait("tensor", "se", p.c["se"])
            for j in range(G):
                p.op("tensor", lambda E, i=i, j=j, q=q, pq=pq: E.matmul(
                    pq, sb_xnd[:, 2048 * j + 128 * i:2048 * j + 128 * (i + 1)],
                    sb_phib[:, 2048 * j + 512 * q:2048 * j + 512 * (q + 1)],
                    start=(j == 0), stop=(j == G - 1)), "te" if j == G - 1 else None)
            p.wait("scalar", "te", p.c["te"])
            wait_slot_stores("scalar", f"s{q % 2}")
            eb, lb = sb_Et[q % 2], sb_Lt[q % 2]
            p.op("scalar", lambda E, pq=pq, eb=eb: E.activation(eb, pq, AF.Exp), "se")
            p.op("scalar", lambda E, pq=pq, lb=lb: E.activation(lb, pq, AF.Copy), "se")
            p.wait("scalar", "se", p.c["se"])
            p.dma("scalar", d_E[128 * i:128 * (i + 1), 512 * q:512 * (q + 1)], eb, f"s{q % 2}")
            p.dma("scalar", d_logits[128 * i:128 * (i + 1), 512 * q:512 * (q + 1)], lb, f"s{q % 2}")
            if tap == "p1" and i == 0 and q == 0:
                p.op("scalar", lambda E, pq=pq: E.activation(sb_sev.bitcast(F32)[:, 0:512], pq, AF.Copy), "se")
                p.dma("scalar", d_dev[:, 66048:66560], sb_sev.bitcast(F32)[:, 0:512], "st")
                p.dma("scalar", d_dev[:, 66560:67072], lb, "st")
    p.wait("gpsimd", "s0", p.c["s0"])
    p.wait("gpsimd", "s1", p.c["s1"])
    p.op("gpsimd", lambda E: E.collective_compute(
        "AllReduce", ALU.add, replica_groups=[list(range(8))],
        ins=[d_E.opt()], outs=[d_den.opt()]), "cc")
    cc_ar = p.c["cc"]

    if tap == "p1":
        p.wait("scalar", "sp", p.c["sp"])
        p.dma("scalar", d_dev[:, 0:32768].rearrange("p (i q) -> p i q", i=16), d_E.rearrange("(i p) q -> p i q", p=128), "st")
        p.dma("scalar", d_dev[:, 32768:65536].rearrange("p (i q) -> p i q", i=16), d_logits.rearrange("(i p) q -> p i q", p=128), "st")
        p.wait("tensor", "sp", p.c["sp"])
        for j in range(G):
            p.op("tensor", lambda E, j=j: E.matmul(
                ps[0], sb_xnd[:, 2048 * j:2048 * j + 128],
                sb_phib[:, 2048 * j:2048 * j + 512],
                start=(j == 0), stop=(j == G - 1)), "te" if j == G - 1 else None)
        p.wait("scalar", "te", p.c["te"])
        p.op("scalar", lambda E: E.activation(sb_Lt[0], ps[0], AF.Copy), "se")
        p.dma("scalar", d_dev[:, 65536:66048], sb_Lt[0], "st")
        return _finish(nc, stack, p)

    for g in range(16):
        for q in range(TQ):
            pq = ps[q % 2]
            p.wait("tensor", "se", p.c["se"])
            for j in range(G):
                p.op("tensor", lambda E, g=g, j=j, q=q, pq=pq: E.matmul(
                    pq, sb_winb[:, 2048 * j + 128 * g:2048 * j + 128 * (g + 1)],
                    sb_xnd[:, 2048 * j + 512 * q:2048 * j + 512 * (q + 1)],
                    start=(j == 0), stop=(j == G - 1)), "te" if j == G - 1 else None)
            p.wait("scalar", "te", p.c["te"])
            if g < 8:
                if q == 0:
                    p.wait("scalar", "ve", p.c["ve"])
                p.op("scalar", lambda E, q=q, pq=pq: E.activation(
                    sb_xinp[:, 16 + 512 * q:16 + 512 * (q + 1)], pq, AF.Copy), "se")
            else:
                if q == 0:
                    wait_slot_stores("scalar", "sp")
                p.op("scalar", lambda E, q=q, pq=pq: E.activation(
                    sb_ct0[:, 512 * q:512 * (q + 1)], pq, AF.Silu), "se")
        if g < 8:
            p.wait("vector", "se", p.c["se"])
            if g == 0:
                p.op("vector", lambda E: E.memset(sb_xinp[:, 0:16], 0.0))
            x0, x1 = sb_xinp[:, 13:13 + T], sb_xinp[:, 14:14 + T]
            x2, x3 = sb_xinp[:, 15:15 + T], sb_xinp[:, 16:16 + T]
            p.wait("vector", "te", p.c["te"])
            p.op("vector", lambda E, g=g, x0=x0: E.tensor_scalar_mul(sb_ct0, x0, sb_convw[:, 4 * g:4 * g + 1]))
            p.op("vector", lambda E, g=g, x1=x1: E.scalar_tensor_tensor(
                sb_ct1, x1, sb_convw[:, 4 * g + 1:4 * g + 2], sb_ct0, ALU.mult, ALU.add))
            p.op("vector", lambda E, g=g, x2=x2: E.scalar_tensor_tensor(
                sb_ct0, x2, sb_convw[:, 4 * g + 2:4 * g + 3], sb_ct1, ALU.mult, ALU.add))
            p.op("vector", lambda E, g=g, x3=x3: E.scalar_tensor_tensor(
                sb_ct1, x3, sb_convw[:, 4 * g + 3:4 * g + 4], sb_ct0, ALU.mult, ALU.add), "ve")
            p.wait("scalar", "ve", p.c["ve"])
            wait_slot_stores("scalar", "sp")
            p.wait("scalar", "te", p.c["te"])
            p.op("scalar", lambda E, g=g: E.activation(sb_xcg, sb_ct1, AF.Silu, bias=sb_convb[:, g:g + 1]), "se")
            se_xcv = p.c["se"]
            p.wait("scalar", "se", se_xcv)
            p.dma("scalar", d_xconv[g], sb_xcg, "sp")
            p.wait("tensor", "se", se_xcv)
            for q in range(TQ):
                p.op("tensor", lambda E, g=g, q=q: E.matmul(
                    ps_big[0:33, 512 * q:512 * (q + 1)], sb_xpj[:, 33 * g:33 * (g + 1)],
                    sb_xcg[:, 512 * q:512 * (q + 1)],
                    start=(g == 0), stop=(g == 7)), "te" if q == TQ - 1 else None)
        else:
            p.wait("scalar", "se", p.c["se"])
            p.dma("scalar", d_sz[g - 8], sb_ct0, "sp")
    p.wait("scalar", "te", p.c["te"])
    p.op("scalar", lambda E: E.activation(sb_ssm[0:33, :], ps_big[0:33, :], AF.Copy), "se")
    se_ssm = p.c["se"]
    p.wait("scalar", "se", se_ssm)
    p.dma("scalar", d_ssm, sb_ssm[0:33, :], "sp")
    sp_ssm = p.c["sp"]

    if tap == "p2":
        p.wait("scalar", "sp", p.c["sp"])
        p.dma("scalar", d_dev[:, 0:16384].rearrange("p (g t) -> p g t", g=8), d_xconv.rearrange("g p t -> p g t"), "st")
        p.dma("scalar", d_dev[0:33, 16384:18432], d_ssm, "st")
        p.dma("scalar", d_dev[:, 18432:34816].rearrange("p (g t) -> p g t", g=8), d_sz.rearrange("g p t -> p g t"), "st")
        return _finish(nc, stack, p)

    p.wait("sync", "sp", sp_ssm)
    ld_sbc = p.dma("sync", sb_sbc, d_ssm[32:33, :].partition_broadcast(128), "ld")
    p.wait("sync", "se", p.c["se"])
    ld_wo = p.dma("sync", sb_wfs.rearrange("p (j e) -> p j e", j=8),
                  d_wout.rearrange("(j p) e -> p j e", p=128), "ld")
    p.wait("scalar", "ld", ld_wo)
    for j in range(G):
        p.op("scalar", lambda E, j=j: E.activation(
            sb_woutb[:, 1024 * j:1024 * (j + 1)], sb_wfs[:, 1024 * j:1024 * (j + 1)], AF.Copy), "se")
    se_wout = p.c["se"]
    p.wait("scalar", "ld", ld_wo)
    p.wait("vector", "ld", ld_wo)
    p.wait("vector", "sp", p.c["sp"])
    for g in range(G):
        wait_slot_stores("scalar", "sp")
        p.wait("vector", "sp", p.c["sp"])
        p.op("vector", lambda E, g=g: E.tensor_scalar(
            sb_ct0, sb_sbc, sb_dtw[:, g:g + 1], sb_dtb[:, g:g + 1], ALU.mult, ALU.add), "ve")
        p.wait("scalar", "ve", p.c["ve"])
        p.op("scalar", lambda E: E.activation(sb_ct1, sb_ct0, AF.Abs))
        p.op("scalar", lambda E: E.activation(sb_ct1, sb_ct1, AF.Exp, scale=-1.0))
        p.op("scalar", lambda E: E.activation(sb_ct1, sb_ct1, AF.Ln, bias=1.0))
        p.op("scalar", lambda E: E.activation(sb_ct0, sb_ct0, AF.Relu), "se")
        p.wait("vector", "se", p.c["se"])
        p.op("vector", lambda E: E.tensor_add(sb_ct0, sb_ct0, sb_ct1), "ve")
        p.wait("scalar", "ve", p.c["ve"])
        p.dma("scalar", d_dt[g], sb_ct0, "sp")

    if tap == "p3":
        p.wait("scalar", "sp", p.c["sp"])
        p.dma("scalar", d_dev[:, 0:16384].rearrange("p (g t) -> p g t", g=8), d_dt.rearrange("g p t -> p g t"), "st")
        return _finish(nc, stack, p)

    p.op("vector", lambda E: E.memset(sb_hcar, 0.0), "ve")
    for c in range(NCH):
        c0 = TC * c
        p.wait("sync", "ve", p.c["ve"])
        p.wait("sync", "ge", p.c["ge"])
        p.wait("sync", "sp", p.c["sp"])
        bsrc = d_ssm[0:16, c0:c0 + TC].unsqueeze(0).broadcast_to([128, 16, TC])
        csrc = d_ssm[16:32, c0:c0 + TC].unsqueeze(0).broadcast_to([128, 16, TC])
        lsem = f"l{c % 2}"
        p.dma("sync", sb_Bbc.rearrange("p (n t) -> p n t", n=16), bsrc, lsem)
        p.dma("sync", sb_Cbc.rearrange("p (n t) -> p n t", n=16), csrc, lsem)
        p.dma("sync", sb_xcs.rearrange("p (g t) -> p g t", g=8),
              d_xconv[:, :, c0:c0 + TC].rearrange("g p t -> p g t"), lsem)
        p.dma("sync", sb_dts.rearrange("p (g t) -> p g t", g=8),
              d_dt[:, :, c0:c0 + TC].rearrange("g p t -> p g t"), lsem)
        p.dma("sync", sb_szs.rearrange("p (g t) -> p g t", g=8),
              d_sz[:, :, c0:c0 + TC].rearrange("g p t -> p g t"), lsem)
        ld_grp = p.dma("sync", sb_xres.rearrange("p (i d) -> p i d", i=2),
                       d_x.rearrange("(i p) d -> p i d", p=128)[:, 2 * c:2 * c + 2, :], lsem)
        p.wait("vector", "te", p.c["te"])
        for g in range(G):
            dts = sb_dts[:, TC * g:TC * (g + 1)]
            xcs = sb_xcs[:, TC * g:TC * (g + 1)]
            p.wait("scalar", lsem, ld_grp)
            p.wait("scalar", "ve", p.c["ve"])
            for n in range(NSS):
                p.op("scalar", lambda E, g=g, n=n, dts=dts: E.activation(
                    sb_dA[:, TC * n:TC * (n + 1)], dts, AF.Exp,
                    scale=sb_aneg[:, 16 * g + n:16 * g + n + 1]),
                    "se" if n == NSS - 1 else None)
            p.op("scalar", lambda E, g=g, dts=dts: E.activation(
                sb_e0, sb_aneg[:, 16 * g:16 * (g + 1)], AF.Exp, scale=dts[:, 0:1]), "se")
            se_e0 = p.c["se"]
            p.wait("vector", lsem, ld_grp)
            p.wait("vector", "ge", p.c["ge"])
            p.op("vector", lambda E, dts=dts, xcs=xcs: E.tensor_mul(sb_ug, dts, xcs), "ve")
            ve_u = p.c["ve"]
            p.wait("gpsimd", lsem, ld_grp)
            p.wait("gpsimd", "ve", ve_u)
            u3 = sb_ug.unsqueeze(1).broadcast_to([128, 16, TC])
            p.op("gpsimd", lambda E, u3=u3: E.tensor_tensor(
                sb_dBx.rearrange("p (n t) -> p n t", n=16),
                sb_Bbc.rearrange("p (n t) -> p n t", n=16), u3, ALU.mult), "ge")
            ge_dbx = p.c["ge"]
            dA3 = sb_dA.rearrange("p (n t) -> p n t", n=16)
            dBx3 = sb_dBx.rearrange("p (n t) -> p n t", n=16)
            p.wait("vector", "se", se_e0)
            p.op("vector", lambda E, g=g: E.tensor_mul(sb_fixv, sb_e0, sb_hcar[:, 16 * g:16 * (g + 1)]))
            p.wait("vector", "ge", ge_dbx)
            p.op("vector", lambda E, dBx3=dBx3: E.tensor_add(dBx3[:, :, 0], dBx3[:, :, 0], sb_fixv))
            p.op("vector", lambda E, dA3=dA3: E.memset(dA3[:, :, 0], 0.0))
            p.op("vector", lambda E: E.tensor_tensor_scan(sb_h, sb_dA, sb_dBx, 0.0, ALU.mult, ALU.add))
            h3 = sb_h.rearrange("p (n t) -> p n t", n=16)
            p.op("vector", lambda E, g=g, h3=h3: E.tensor_copy(sb_hcar[:, 16 * g:16 * (g + 1)], h3[:, :, TC - 1]), "ve")
            ve_scan = p.c["ve"]
            p.wait("gpsimd", "ve", ve_scan)
            p.op("gpsimd", lambda E: E.tensor_tensor(sb_dBx, sb_h, sb_Cbc, ALU.mult), "ge")
            ge_hc = p.c["ge"]
            p.wait("vector", "ge", ge_hc)
            hc3 = sb_dBx.rearrange("p (n t) -> p t n", n=16)
            p.op("vector", lambda E, hc3=hc3: E.tensor_reduce(sb_yg, hc3, AX.X, ALU.add))
            ygslot = sb_ygt[:, 2048 * (c % 2) + 256 * g:2048 * (c % 2) + 256 * (g + 1)]
            p.op("vector", lambda E, g=g, ygslot=ygslot: E.tensor_mul(
                ygslot, sb_yg, sb_szs[:, TC * g:TC * (g + 1)]), "ve")
        ve_chunk = p.c["ve"]
        p.wait("tensor", "ve", ve_chunk)
        if c == 0:
            p.wait("tensor", "se", se_wout)
        for ti in range(2):
            for eh in range(2):
                pq = ps[2 * ti + eh]
                for g in range(G):
                    stat = sb_ygt[:, 2048 * (c % 2) + 256 * g + 128 * ti:
                                  2048 * (c % 2) + 256 * g + 128 * (ti + 1)]
                    p.op("tensor", lambda E, stat=stat, g=g, eh=eh, pq=pq: E.matmul(
                        pq, stat, sb_woutb[:, 1024 * g + 512 * eh:1024 * g + 512 * (eh + 1)],
                        start=(g == 0), stop=(g == G - 1)), "te" if g == G - 1 else None)
        te_op = p.c["te"]
        p.wait("vector", "te", te_op)
        p.wait("vector", lsem, ld_grp)
        p.wait("vector", f"s{(c - 1) % 2}", p.c[f"s{(c - 1) % 2}"] if c > 0 else 0)
        for ti in range(2):
            for eh in range(2):
                pq = ps[2 * ti + eh]
                dst = sb_mact[ti][:, 512 * eh:512 * (eh + 1)]
                p.op("vector", lambda E, pq=pq, dst=dst, ti=ti, eh=eh: E.scalar_tensor_tensor(
                    dst, pq, 1.0, sb_xres[:, 1024 * ti + 512 * eh:1024 * ti + 512 * (eh + 1)],
                    ALU.mult, ALU.add), "ve")
        p.wait("scalar", "ve", p.c["ve"])
        wait_slot_stores("scalar", f"s{c % 2}")
        for ti in range(2):
            p.dma("scalar", d_macc[128 * (2 * c + ti):128 * (2 * c + ti + 1), :],
                  sb_mact[ti], f"s{c % 2}")

    if tap == "p4":
        p.wait("scalar", "sp", p.c["sp"])
        p.dma("scalar", d_dev[:, 0:16384].rearrange("p (i d) -> p i d", i=16), d_macc.rearrange("(i p) d -> p i d", p=128), "st")
        return _finish(nc, stack, p)

    p.wait("sync", "cc", cc_ar)
    p.wait("sync", "te", p.c["te"])
    for q in range(TQ):
        for i in range(MT):
            lb, db, dspb = sb_lgt[i % 2], sb_dnt[i % 2], sb_dsp[i % 2]
            if i >= 2 or q > 0:
                p.wait("sync", "se", p.c["se"])
            p.dma("sync", lb, d_logits[128 * i:128 * (i + 1), 512 * q:512 * (q + 1)], f"l{i % 2}")
            ld_lq = p.dma("sync", db, d_den[128 * i:128 * (i + 1), 512 * q:512 * (q + 1)], f"l{i % 2}")
            p.wait("scalar", f"l{i % 2}", ld_lq)
            p.op("scalar", lambda E, db=db: E.activation(db, db, AF.Ln), "se")
            p.wait("vector", "se", p.c["se"])
            p.op("vector", lambda E, lb=lb, db=db: E.tensor_sub(lb, lb, db), "ve")
            p.wait("scalar", "ve", p.c["ve"])
            p.wait("scalar", "te", p.c["te"])
            p.op("scalar", lambda E, lb=lb, dspb=dspb: E.activation(dspb, lb, AF.Exp), "se")
            se_dsp = p.c["se"]
            if tap == "p5q0" and q == 0:
                p.wait("scalar", "se", se_dsp)
                p.dma("scalar", d_dev[:, 47104 + 256 * i:47104 + 256 * (i + 1)], dspb.bitcast(F32), "st")
                p.dma("scalar", d_dev[:, 51200 + 512 * i:51200 + 512 * (i + 1)], db, "st")
                p.dma("scalar", d_dev[:, 59392 + 512 * i:59392 + 512 * (i + 1)], lb, "st")
            p.wait("tensor", "se", se_dsp)
            if i == 0:
                p.wait("tensor", "ve", p.c["ve"])
            for j in range(G):
                p.op("tensor", lambda E, i=i, j=j, dspb=dspb: E.matmul(
                    ps8[j], sb_xntok[:, 1024 * i + 128 * j:1024 * i + 128 * (j + 1)],
                    dspb, start=(i == 0), stop=(i == MT - 1)), "te" if j == G - 1 else None)
        te_sl = p.c["te"]
        p.wait("scalar", "te", te_sl)
        wait_slot_stores("scalar", "sp")
        for j in range(G):
            p.op("scalar", lambda E, j=j: E.activation(sb_sev[:, 512 * j:512 * (j + 1)], ps8[j], AF.Copy), "se")
        p.wait("scalar", "se", p.c["se"])
        sev3 = sb_sev.rearrange("p (j t) -> p j t", j=8)
        for half in range(2):
            r = 2 * q + half
            p.dma("scalar", d_ssnd[r].rearrange("(j p) t -> p j t", p=128),
                  sev3[:, :, 256 * half:256 * (half + 1)], "sp")
    sp_slots = p.c["sp"]
    p.wait("gpsimd", "sp", sp_slots)
    p.op("gpsimd", lambda E: E.collective_compute(
        "AllGather", ALU.bypass, replica_groups=[list(range(8))],
        ins=[d_ssnd.opt()], outs=[d_sag.opt()]), "cc2")
    cc_s = p.c["cc2"]

    if tap == "p5":
        p.wait("scalar", "cc2", cc_s)
        p.dma("scalar", d_dev[:, 0:8192], d_sag.bitcast(F32)[0, 0][0:128, :].unsqueeze(1).broadcast_to([128, 64, 128]), "st")
        return _finish(nc, stack, p)

    p.wait("gpsimd", "cc2", cc_s)
    srcv5 = sb_srcv.rearrange("p (j n b q) -> p j n b q", j=8, n=8, b=8)
    stg4 = sb_stg.rearrange("p (b j t) -> p b j t", b=8, j=8)
    for b8 in range(8):
        p.c["gl"] += 16
        ld_sr = p.c["gl"]
        p.emit("gpsimd", lambda E, b8=b8: E.dma_start(
            out=stg4[:, b8], in_=dyn_sr[0][b8], bounds_check=None).then_inc(p.sems["gl"], 16))
    p.wait("vector", "gl", ld_sr)
    stg_r = sb_stg.rearrange("p (b j n q) -> p j n b q", b=8, j=8, n=8)
    for j in range(G):
        p.op("vector", lambda E, j=j: E.tensor_copy(srcv5[:, j], stg_r[:, j]),
             "ve" if j == G - 1 else None)
    ve_srcv = p.c["ve"]
    if tap == "p6a":
        p.wait("scalar", "gl", ld_sr)
        p.dma("scalar", d_dev[:, 0:8192], sb_srcv.bitcast(F32), "st")
        return _finish(nc, stack, p)

    for n in range(8):
        p.wait("sync", "se", p.c["se"])
        ld_w = p.dma("sync", sb_wfs.rearrange("p (j e) -> p j e", j=8),
                     d_wffn[n].rearrange("(j p) e -> p j e", p=128), f"l{n % 2}")
        p.wait("scalar", f"l{n % 2}", ld_w)
        for j in range(G):
            p.op("scalar", lambda E, j=j: E.activation(
                sb_wfb[:, 1024 * j:1024 * (j + 1)], sb_wfs[:, 1024 * j:1024 * (j + 1)], AF.Copy), "se")
        se_wf = p.c["se"]
        p.wait("sync", "ve", p.c["ve"])
        ld_bb = p.dma("sync", sb_bbc, d_bffn[n:n + 1, :].partition_broadcast(128), "ld")
        p.wait("tensor", "se", se_wf)
        p.wait("tensor", "ld", ld_bb)
        p.wait("tensor", "ve", ve_srcv)
        p.wait("tensor", "ve", p.c["ve"])
        for rh in range(2):
            for eh in range(2):
                pq = ps[2 * rh + eh]
                for j in range(G):
                    stat = sb_srcv[:, (j * 8 + n) * 256 + 128 * rh:
                                   (j * 8 + n) * 256 + 128 * (rh + 1)]
                    p.op("tensor", lambda E, stat=stat, j=j, eh=eh, pq=pq: E.matmul(
                        pq, stat, sb_wfb[:, 1024 * j + 512 * eh:1024 * j + 512 * (eh + 1)],
                        start=(j == 0), stop=(j == G - 1)), "te" if j == G - 1 else None)
        te_y = p.c["te"]
        p.wait("vector", "te", te_y)
        p.wait("vector", "ld", ld_bb)
        for rh in range(2):
            for eh in range(2):
                pq = ps[2 * rh + eh]
                p.wait("vector", "se", p.c["se"])
                p.op("vector", lambda E, pq=pq, rh=rh, eh=eh: E.scalar_tensor_tensor(
                    sb_yev[rh], pq, 1.0, sb_bbc[:, 512 * eh:512 * (eh + 1)], ALU.mult, ALU.add), "ve")
                p.wait("scalar", "ve", p.c["ve"])
                wait_slot_stores("scalar", "sp")
                p.op("scalar", lambda E, rh=rh: E.activation(sb_ybf, sb_yev[rh], AF.Copy), "se")
                p.wait("scalar", "se", p.c["se"])
                for bb in range(4):
                    b = 4 * rh + bb
                    p.dma("scalar", d_ysnd[b, 32 * n:32 * (n + 1), 512 * eh:512 * (eh + 1)],
                          sb_ybf[32 * bb:32 * (bb + 1), :], "sp")
    sp_y = p.c["sp"]
    p.wait("gpsimd", "sp", sp_y)
    p.op("gpsimd", lambda E: E.collective_compute(
        "AllGather", ALU.bypass, replica_groups=[list(range(8))],
        ins=[d_ysnd.opt()], outs=[d_yag.opt()]), "cc3")
    cc_y = p.c["cc3"]

    if tap == "p6":
        p.wait("scalar", "cc3", cc_y)
        p.dma("scalar", d_dev[:, 0:1024].rearrange("p (k e) -> p k e", k=2),
              d_yag.bitcast(F32)[0, 0].rearrange("(k p) e -> p k e", p=128), "st")
        return _finish(nc, stack, p)

    p.wait("gpsimd", "cc3", cc_y)
    yb4 = sb_yb.rearrange("p (r k e) -> p r k e", r=8, k=2)
    for r in range(8):
        p.c["gl"] += 16
        ld_yb = p.c["gl"]
        p.emit("gpsimd", lambda E, r=r: E.dma_start(
            out=yb4[:, r], in_=dyn_y[0][r], bounds_check=None).then_inc(p.sems["gl"], 16))

    if tap == "p7":
        p.wait("scalar", "ve", p.c["ve"])
        p.wait("scalar", "gl", ld_yb)
        p.dma("scalar", d_dev[:, 0:16], sb_rs, "st")
        p.dma("scalar", d_dev[:, 16:32], sb_rrs, "st")
        p.dma("scalar", d_dev[:, 1024:9216], sb_yb.bitcast(F32), "st")
    p.wait("sync", "sp", p.c["sp"])
    p.wait("sync", "s0", p.c["s0"])
    p.wait("sync", "s1", p.c["s1"])
    for i in range(MT):
        p.wait("sync", "se", p.c["se"])
        p.wait("sync", "ve", p.c["ve"])
        p.dma("sync", sb_Ei, d_logits[128 * i:128 * (i + 1), :], f"l{i % 2}")
        ld_m = p.dma("sync", sb_mact[i % 2], d_macc[128 * i:128 * (i + 1), :], f"l{i % 2}")
        p.wait("scalar", f"l{i % 2}", ld_m)
        p.wait("scalar", "ve", p.c["ve"])
        p.op("scalar", lambda E: E.activation(sb_Ei, sb_Ei, AF.Exp), "se")
        p.wait("vector", "se", p.c["se"])
        p.wait("vector", "te", p.c["te"])
        p.op("vector", lambda E, i=i: E.reduce_sum(sb_rs[:, i:i + 1], sb_Ei, AX.X), "ve")
        p.wait("scalar", "ve", p.c["ve"])
        p.op("scalar", lambda E, i=i: E.activation(sb_rrs[:, i:i + 1], sb_rs[:, i:i + 1], AF.Ln, scale=1.0), "se")
        p.op("scalar", lambda E: E.activation(sb_cmbf, sb_Ei, AF.Ln), "se")
        p.wait("vector", "se", p.c["se"])
        p.op("vector", lambda E, i=i: E.tensor_scalar_mul(sb_rrs[:, i:i + 1], sb_rrs[:, i:i + 1], -1.0), "ve")
        p.wait("scalar", "ve", p.c["ve"])
        p.op("scalar", lambda E, i=i: E.activation(sb_cmbf, sb_cmbf, AF.Exp, bias=sb_rrs[:, i:i + 1]), "se")
        p.op("scalar", lambda E: E.engine_nop(), "se") if False else None
        p.wait("vector", "se", p.c["se"])
        p.op("vector", lambda E: E.tensor_copy(sb_tmp1[:, 0:1], sb_cmbf[:, 0:1]), "ve")
        ve_cmb = p.c["ve"]
        if tap == "p7" and i == 14:
            p.wait("scalar", "ve", ve_cmb)
            p.dma("scalar", d_dev[:, 9216:11264], sb_cmbf, "st")
            p.dma("scalar", d_dev[:, 11264:11280], sb_rs, "st")
            p.dma("scalar", d_dev[:, 11280:11296], sb_rrs, "st")
            p.dma("scalar", d_dev[:, 11296:13344], sb_Ei, "st")
        p.wait("tensor", "ve", ve_cmb)
        for kb in range(4):
            p.wait("tensor", "se", p.c["se"])
            for kk in range(4):
                k = 4 * kb + kk
                p.op("tensor", lambda E, k=k, kk=kk: E.transpose(
                    ps[kk][:, 0:128], sb_cmbf[:, 128 * k:128 * (k + 1)], sb_idf),
                    "te" if kk == 3 else None)
            p.wait("scalar", "te", p.c["te"])
            for kk in range(4):
                k = 4 * kb + kk
                p.op("scalar", lambda E, k=k, kk=kk: E.activation(
                    sb_cnp[:, 128 * k:128 * (k + 1)], ps[kk][:, 0:128], AF.Copy), "se")
        se_cnp = p.c["se"]
        p.wait("tensor", "se", se_cnp)
        p.wait("tensor", "gl", ld_yb)
        p.wait("tensor", "ve", p.c["ve"])
        for eh in range(2):
            pq = ps[eh]
            for k in range(16):
                p.op("tensor", lambda E, k=k, eh=eh, pq=pq: E.matmul(
                    pq, sb_cnp[:, 128 * k:128 * (k + 1)],
                    sb_yb[:, 1024 * k + 512 * eh:1024 * k + 512 * (eh + 1)],
                    start=(k == 0), stop=(k == 15)), "te" if k == 15 else None)
        te_out = p.c["te"]
        p.wait("vector", "te", te_out)
        p.wait("vector", f"l{i % 2}", ld_m)
        for eh in range(2):
            p.op("vector", lambda E, i=i, eh=eh: E.scalar_tensor_tensor(
                sb_outt[i % 2][:, 512 * eh:512 * (eh + 1)], ps[eh], 1.0,
                sb_mact[i % 2][:, 512 * eh:512 * (eh + 1)], ALU.mult, ALU.add), "ve")
        p.wait("scalar", "ve", p.c["ve"])
        wait_slot_stores("scalar", f"s{i % 2}")
        p.dma("scalar", d_out[128 * i:128 * (i + 1), :], sb_outt[i % 2], f"s{i % 2}")

    if tap in ("post", "p5q0"):
        p.wait("scalar", "st", p.c["st"])
        p.dma("scalar", d_dev[:, 0:8192], sb_yb.bitcast(F32), "st")
        for r in range(4):
            p.dma("scalar",
                  d_dev[:, 8192 + 1024 * r:8192 + 1024 * (r + 1)].rearrange("p (k e) -> p k e", k=2),
                  d_yag.bitcast(F32)[r, 0].rearrange("(k p) e -> p k e", p=128), "st")
        p.dma("scalar", d_dev[:, 12288:13312], d_macc[0:128, :], "st")
        p.dma("scalar", d_dev[:, 13312:15360], sb_cmbf, "st")
        p.dma("scalar", d_dev[:, 15360:15376], sb_rs, "st")
        p.dma("scalar", d_dev[:, 15376:15392], sb_rrs, "st")
        p.dma("scalar", d_dev[:, 15392:15904], d_ysnd.bitcast(F32)[0, 0:128, :], "st")
        for r in range(8):
            p.dma("scalar", d_dev[:, 36864 + 1024 * r:36864 + 1024 * (r + 1)].rearrange("p (j t) -> p j t", j=8),
                  d_ssnd.bitcast(F32)[r].rearrange("(j p) t -> p j t", p=128), "st")
        p.dma("scalar", d_dev[:, 45056:45568], d_den[0:128, 0:512], "st")
        p.dma("scalar", d_dev[:, 45568:46080], d_den[1920:2048, 0:512], "st")
        p.dma("scalar", d_dev[:, 46080:46592], d_logits[0:128, 0:512], "st")
        for s in range(8):
            p.dma("scalar", d_dev[:, 24576 + 1024 * s:24576 + 1024 * (s + 1)].rearrange("p (j t) -> p j t", j=8),
                  d_sag.bitcast(F32)[s, 0].rearrange("(j p) t -> p j t", p=128), "st")
        p.dma("scalar", d_dev[:, 32768:33792].rearrange("p (j t) -> p j t", j=8),
              d_ssnd.bitcast(F32)[0].rearrange("(j p) t -> p j t", p=128), "st")
        p.dma("scalar", d_dev[:, 33792:34816], d_den[0:128, 1024:2048], "st")
        p.dma("scalar", d_dev[:, 34816:35840], d_logits[0:128, 1024:2048], "st")

    return _finish(nc, stack, p)


def _finish(nc, stack, p):
    p.wait("sync", "st", p.c["st"])
    p.wait("sync", "sp", p.c["sp"])
    p.wait("sync", "s0", p.c["s0"])
    p.wait("sync", "s1", p.c["s1"])
    with nc.Block() as block:
        @block.sync
        def _(E):
            for fn in p.ops["sync"]:
                fn(E)

        @block.scalar
        def _(E):
            for fn in p.ops["scalar"]:
                fn(E)

        @block.vector
        def _(E):
            for fn in p.ops["vector"]:
                fn(E)

        @block.tensor
        def _(E):
            for fn in p.ops["tensor"]:
                fn(E)

        @block.gpsimd
        def _(E):
            for fn in p.ops["gpsimd"]:
                fn(E)
    stack.close()
    return nc


def kernel(L0_attn_x, L0_attn_in_proj_w, L0_attn_conv_w, L0_attn_conv_b,
           L0_attn_x_proj_w, L0_attn_dt_proj_w, L0_attn_dt_proj_b,
           L0_attn_A_log, L0_attn_out_proj_w, L0_ffn_phi, L0_ffn_W, L0_ffn_b):
    import ml_dtypes
    x = np.ascontiguousarray(np.asarray(L0_attn_x, dtype=np.float32))
    w_in_t = np.ascontiguousarray(np.asarray(L0_attn_in_proj_w, dtype=np.float32).T)
    w_out_t = np.ascontiguousarray(np.asarray(L0_attn_out_proj_w, dtype=np.float32).T)
    phi = np.ascontiguousarray(np.asarray(L0_ffn_phi, dtype=np.float32).reshape(D, NP))
    xproj_t = np.ascontiguousarray(np.asarray(L0_attn_x_proj_w, dtype=np.float32).T)
    conv_w = np.ascontiguousarray(
        np.asarray(L0_attn_conv_w, dtype=np.float32)[:, 0, :].reshape(8, 128, 4).transpose(1, 0, 2).reshape(128, 32))
    conv_b = np.ascontiguousarray(np.asarray(L0_attn_conv_b, dtype=np.float32).reshape(8, 128).T)
    dt_w = np.ascontiguousarray(np.asarray(L0_attn_dt_proj_w, dtype=np.float32).reshape(8, 128).T)
    dt_b = np.ascontiguousarray(np.asarray(L0_attn_dt_proj_b, dtype=np.float32).reshape(8, 128).T)
    a_log = np.ascontiguousarray(
        np.asarray(L0_attn_A_log, dtype=np.float32).reshape(8, 128, NSS).transpose(1, 0, 2).reshape(128, 128))
    W = np.asarray(L0_ffn_W, dtype=np.float32)
    bffn = np.asarray(L0_ffn_b, dtype=np.float32)
    idf = np.eye(128, dtype=np.float32)
    idb = np.eye(128, dtype=ml_dtypes.bfloat16)

    with allow_ici_all_to_all():
        nc = build(tap=DEV_TAP)
    in_maps = []
    for c in range(8):
        in_maps.append({
            "x": np.ascontiguousarray(x[c]),
            "w_in_t": w_in_t, "w_out_t": w_out_t, "phi": phi,
            "xproj_t": xproj_t, "conv_w": conv_w, "conv_b": conv_b,
            "dt_w": dt_w, "dt_b": dt_b, "a_log": a_log,
            "w_ffn": np.ascontiguousarray(W[8 * c:8 * (c + 1)]),
            "b_ffn": np.ascontiguousarray(bffn[8 * c:8 * (c + 1)]),
            "ident_f": idf, "ident_b": idb,
            "rank": np.array([[c]], dtype=np.uint32),
        })
    with allow_ici_all_to_all():
        res = run_bass_kernel_spmd(nc, in_maps, core_ids=list(range(8)))
    outs = [r["out"] if not isinstance(r["out"], dict) else r["out"] for r in res.results]
    return np.stack([np.asarray(o) for o in outs])


# version 03
# speedup vs baseline: 1.0271x; 1.0157x over previous
import contextlib
import numpy as np
import concourse.bass as bass
import concourse.mybir as mybir
from concourse.bass_utils import run_bass_kernel_spmd
from concourse.aot_env import allow_ici_all_to_all

F32 = mybir.dt.float32
BF16 = mybir.dt.bfloat16
AF = mybir.ActivationFunctionType
ALU = mybir.AluOpType
AX = mybir.AxisListType

T = 2048
D = 1024
NP = 2048
NSS = 16
G = 8
TQ = 4
TC = 256
NCH = T // TC
MT = 16

DEV_TAP = None

ENGS = ("sync", "scalar", "vector", "tensor", "gpsimd")


class P:

    def __init__(self, nc):
        self.nc = nc
        self.ops = {e: [] for e in ENGS}
        self.c = {}
        self.sems = {}

    def sem(self, stack, name):
        self.sems[name] = stack.enter_context(self.nc.semaphore(name))
        self.c[name] = 0

    def emit(self, eng, fn):
        self.ops[eng].append(fn)

    def wait(self, eng, sname, val=None):
        v = self.c[sname] if val is None else val
        if v <= 0:
            return
        s = self.sems[sname]
        self.emit(eng, lambda E, s=s, v=v: E.wait_ge(s, v))

    def dma(self, eng, out, in_, sname, slow=False):
        self.c[sname] += 16
        s = self.sems[sname]
        self.emit(eng, lambda E, o=out, i=in_, s=s, sl=slow: E.dma_start(
            out=o, in_=i, allow_slow_non_contiguous=sl).then_inc(s, 16))
        return self.c[sname]

    def op(self, eng, fn, sname=None):
        if sname is None:
            self.emit(eng, lambda E, f=fn: f(E))
            return None
        self.c[sname] += 1
        s = self.sems[sname]
        self.emit(eng, lambda E, f=fn, s=s: f(E).then_inc(s, 1))
        return self.c[sname]


def build(tap=None):
    nc = bass.Bass("TRN2", target_bir_lowering=False, debug=False)
    stack = contextlib.ExitStack()

    d_x = nc.dram_tensor("x", [T, D], F32, kind="ExternalInput").ap()
    d_win = nc.dram_tensor("w_in_t", [D, 2 * D], F32, kind="ExternalInput").ap()
    d_wout = nc.dram_tensor("w_out_t", [D, D], F32, kind="ExternalInput").ap()
    d_phi = nc.dram_tensor("phi", [D, NP], F32, kind="ExternalInput").ap()
    d_xproj = nc.dram_tensor("xproj_t", [D, 33], F32, kind="ExternalInput").ap()
    d_wffn = nc.dram_tensor("w_ffn", [8, D, D], F32, kind="ExternalInput").ap()
    d_bffn = nc.dram_tensor("b_ffn", [8, D], F32, kind="ExternalInput").ap()
    d_convw = nc.dram_tensor("conv_w", [128, 32], F32, kind="ExternalInput").ap()
    d_convb = nc.dram_tensor("conv_b", [128, 8], F32, kind="ExternalInput").ap()
    d_dtw = nc.dram_tensor("dt_w", [128, 8], F32, kind="ExternalInput").ap()
    d_dtb = nc.dram_tensor("dt_b", [128, 8], F32, kind="ExternalInput").ap()
    d_alog = nc.dram_tensor("a_log", [128, 128], F32, kind="ExternalInput").ap()
    d_rank = nc.dram_tensor("rank", [1, 1], mybir.dt.uint32, kind="ExternalInput").ap()
    d_idf = nc.dram_tensor("ident_f", [128, 128], F32, kind="ExternalInput").ap()
    d_idb = nc.dram_tensor("ident_b", [128, 128], BF16, kind="ExternalInput").ap()
    d_out = nc.dram_tensor("out", [T, D], F32, kind="ExternalOutput").ap()

    d_fin = nc.dram_tensor("fence_in", [128, 4], F32).ap()
    d_fout = nc.dram_tensor("fence_out", [128, 4], F32).ap()
    d_gA = nc.dram_tensor("guard_A", [T, 512], F32).ap()
    d_E = nc.dram_tensor("E_scr", [T, NP], F32).ap()
    d_gB = nc.dram_tensor("guard_B", [T, 512], F32).ap()
    d_logits = nc.dram_tensor("logits_scr", [T, NP], F32).ap()
    d_den = nc.dram_tensor("den_scr", [T, NP], F32).ap()
    d_gC = nc.dram_tensor("guard_C", [T, 512], F32).ap()
    d_sz = nc.dram_tensor("sz_scr", [G, 128, T], F32).ap()
    d_xconv = nc.dram_tensor("xconv_scr", [G, 128, T], F32).ap()
    d_dt = nc.dram_tensor("dt_scr", [G, 128, T], F32).ap()
    d_ssm = nc.dram_tensor("ssm_scr", [33, T], F32).ap()
    d_macc = nc.dram_tensor("macc_scr", [T, D], F32).ap()
    d_guard0 = nc.dram_tensor("guard0", [8, D, 256], BF16).ap()
    d_ssnd = nc.dram_tensor("slots_snd", [8, D, 256], BF16).ap()
    d_guard1 = nc.dram_tensor("guard1", [8, D, 256], BF16).ap()
    d_sag = nc.dram_tensor("slots_ag", [8, 8, D, 256], BF16).ap()
    d_guard2 = nc.dram_tensor("guard2", [8, 256, D], BF16).ap()
    d_ysnd = nc.dram_tensor("y_snd", [8, 256, D], BF16).ap()
    d_guard3 = nc.dram_tensor("guard3", [8, 256, D], BF16).ap()
    d_yag = nc.dram_tensor("y_ag", [8, 8, 256, D], BF16).ap()
    d_dev = None
    if tap is not None:
        d_dev = nc.dram_tensor("dev", [128, 81920], F32, kind="ExternalOutput").ap()

    NF32 = 53152
    mega = nc.alloc_sbuf_tensor("mega", [128, NF32], F32).ap()
    cursor = [0]

    def f32v(n_elems, at=None):
        off = cursor[0] if at is None else at
        v = mega[:, off:off + n_elems]
        if at is None:
            cursor[0] = off + n_elems
        return v

    def bf16v(n_elems, at=None):
        assert n_elems % 2 == 0
        off = cursor[0] if at is None else at
        v = mega[:, off:off + n_elems // 2].bitcast(BF16)
        if at is None:
            cursor[0] = off + n_elems // 2
        return v

    sb_aneg = f32v(128)
    sb_convw = f32v(32)
    sb_convb = f32v(8)
    sb_dtw = f32v(8)
    sb_dtb = f32v(8)
    sb_idf = f32v(128)
    sb_idb = bf16v(128)
    sb_mu = f32v(16)
    sb_rstd = f32v(16)
    sb_ln1 = f32v(16)
    sb_ln2 = f32v(16)
    sb_rs = f32v(16)
    sb_rrs = f32v(16)
    sb_hcar = f32v(128)
    sb_e0 = f32v(16)
    sb_fixv = f32v(16)
    sb_xpj = f32v(8 * 33)
    sb_bbc = f32v(1024)
    sb_tmp1 = f32v(16)
    sb_rank = f32v(2).bitcast(mybir.dt.uint32)
    assert cursor[0] <= 1952, cursor[0]
    cursor[0] = 1952
    sb_sbc = f32v(2048)
    sb_ssm = f32v(2048)
    sb_xntok = bf16v(16 * 1024)
    BASE_A = cursor[0]
    sb_xnd = bf16v(8 * 2048, at=BASE_A)
    sb_Bbc = f32v(4096, at=BASE_A)
    sb_Cbc = f32v(4096, at=BASE_A + 4096)
    sb_srcv = bf16v(16 * 1024, at=BASE_A)
    BASE_B = BASE_A + 8192
    sb_stg = bf16v(16 * 1024, at=BASE_B)
    sb_winb = bf16v(8 * 2048, at=BASE_B)
    sb_dA = f32v(4096, at=BASE_B)
    sb_dBx = f32v(4096, at=BASE_B + 4096)
    sb_yb = bf16v(16 * 1024, at=BASE_B)
    BASE_C = BASE_B + 8192
    sb_phib = bf16v(8 * 2048, at=BASE_C)
    sb_h = f32v(4096, at=BASE_C)
    sb_xcs = f32v(2048, at=BASE_C + 4096)
    sb_dts = f32v(2048, at=BASE_C + 6144)
    sb_wfs = f32v(8 * 1024, at=BASE_C)
    BASE_D = BASE_C + 8192
    sb_xt = [f32v(4096, at=BASE_D + 4096 * i) for i in range(2)]
    sb_wstage = f32v(8192, at=BASE_D)
    sb_xinp = f32v(2064, at=BASE_D)
    sb_ct0 = f32v(2048, at=BASE_D + 2176)
    sb_ct1 = f32v(2048, at=BASE_D + 4224)
    sb_xcg = f32v(2048, at=BASE_D + 6272)
    sb_szs = f32v(2048, at=BASE_D)
    sb_ygt = bf16v(2 * 2048, at=BASE_D + 2048)
    sb_yg = f32v(256, at=BASE_D + 4096)
    sb_ug = f32v(256, at=BASE_D + 4352)
    sb_xres = f32v(2048, at=BASE_D + 4608)
    sb_wfb = bf16v(8 * 1024, at=BASE_D)
    sb_yev = [f32v(512, at=BASE_D + 4608 + 512 * i) for i in range(2)]
    sb_ybf = bf16v(512, at=BASE_D + 5632)
    sb_Ei = f32v(2048, at=BASE_D)
    sb_cmbf = f32v(2048, at=BASE_D + 2048)
    sb_cmb = bf16v(512, at=BASE_D + 5632)
    sb_mact = [f32v(1024, at=BASE_D + 4096 + 1024 * i) for i in range(2)]
    sb_outt = [f32v(1024, at=BASE_D + 6144 + 1024 * i) for i in range(2)]
    BASE_E = BASE_D + 8192
    sb_woutb = bf16v(8 * 1024, at=BASE_E)
    sb_Et = [f32v(512, at=BASE_E + 4096 + 512 * i) for i in range(2)]
    sb_Lt = [f32v(512, at=BASE_E + 5120 + 512 * i) for i in range(2)]
    sb_x2 = f32v(1024, at=BASE_E + 4096)
    sb_xnf = f32v(1024, at=BASE_E + 5120)
    sb_cnp = bf16v(16 * 128, at=BASE_E + 4096)
    sb_lgt = [f32v(512, at=BASE_E + 512 * i) for i in range(2)]
    sb_dnt = [f32v(512, at=BASE_E + 1024 + 512 * i) for i in range(2)]
    sb_dsp = [bf16v(512, at=BASE_E + 2048 + 256 * i) for i in range(2)]
    sb_sev = bf16v(8 * 512, at=BASE_E + 2560)
    assert BASE_E + 6144 <= NF32, (BASE_E, NF32)

    ps = [nc.alloc_psum_tensor(f"ps{i}", [128, 512], F32).ap() for i in range(4)]
    ps_big = nc.alloc_psum_tensor("ps_big", [128, 2048], F32).ap()
    ps8 = ps + [ps_big[:, 512 * i:512 * (i + 1)] for i in range(4)]

    p = P(nc)
    for s in ("ld", "l0", "l1", "sp", "s0", "s1", "te", "ve", "se", "ge", "gl",
              "cc", "cc2", "cc3", "st"):
        p.sem(stack, s)

    p.dma("sync", sb_rank[0:1, 0:1], d_rank, "ld")
    ld_rank = p.c["ld"]
    sag_v = d_sag.rearrange("b c (j p) t -> c b p j t", p=128)
    yg5 = d_yag.rearrange("r c (k p) e -> c r p k e", p=128)
    dyn_sr = [None]
    dyn_y = [None]
    def _mk_dyn(E):
        rv = E.value_load(sb_rank[0:1, 0:1])
        dyn_sr[0] = sag_v[bass.ts(rv, 1)].squeeze(0)
        dyn_y[0] = yg5[bass.ts(rv, 1)].squeeze(0)
    p.dma("sync", sb_idf, d_idf, "ld")
    p.dma("sync", sb_idb, d_idb, "ld")
    p.dma("sync", sb_aneg, d_alog, "ld")
    p.dma("sync", sb_convw, d_convw, "ld")
    p.dma("sync", sb_convb, d_convb, "ld")
    p.dma("sync", sb_dtw, d_dtw, "ld")
    p.dma("sync", sb_dtb, d_dtb, "ld")
    p.dma("sync", sb_xpj.rearrange("p (g e) -> p g e", g=8),
          d_xproj.rearrange("(g p) e -> p g e", p=128), "ld")
    ld_params = p.c["ld"]
    p.wait("scalar", "ld", ld_params)
    p.op("scalar", lambda E: E.activation(sb_aneg, sb_aneg, AF.Exp), "se")
    p.wait("gpsimd", "se", p.c["se"])
    p.emit("gpsimd", _mk_dyn)
    p.wait("vector", "se", p.c["se"])
    p.op("vector", lambda E: E.tensor_scalar_mul(sb_aneg, sb_aneg, -1.0), "ve")

    x_r = d_x.rearrange("(i p) d -> p i d", p=128)
    ld_xgrp = []
    ve_ln_done = [0] * 4
    for grp in range(4):
        if grp >= 2:
            p.wait("sync", "ve", ve_ln_done[grp - 2])
        ld_xgrp.append(p.dma("sync", sb_xt[grp % 2].rearrange("p (i d) -> p i d", d=D),
                             x_r[:, 4 * grp:4 * (grp + 1), :], f"l{grp % 2}"))
        for sl in range(4):
            i = 4 * grp + sl
            xt = sb_xt[grp % 2][:, D * sl:D * (sl + 1)]
            p.wait("vector", f"l{grp % 2}", ld_xgrp[grp])
            p.op("vector", lambda E, xt=xt, i=i: E.reduce_sum(sb_ln1[:, i:i + 1], xt, AX.X))
            p.wait("scalar", f"l{grp % 2}", ld_xgrp[grp])
            p.op("scalar", lambda E, xt=xt: E.square(sb_x2, xt), "se")
            p.wait("vector", "se", p.c["se"])
            p.op("vector", lambda E, i=i: E.reduce_sum(sb_ln2[:, i:i + 1], sb_x2, AX.X))
            p.op("vector", lambda E, i=i: E.tensor_scalar_mul(sb_mu[:, i:i + 1], sb_ln1[:, i:i + 1], 1.0 / D))
            p.op("vector", lambda E, i=i: E.tensor_scalar_mul(sb_ln2[:, i:i + 1], sb_ln2[:, i:i + 1], 1.0 / D), "ve")
            p.wait("scalar", "ve", p.c["ve"])
            p.op("scalar", lambda E, i=i: E.square(sb_tmp1[:, 0:1], sb_mu[:, i:i + 1]), "se")
            p.wait("vector", "se", p.c["se"])
            p.op("vector", lambda E, i=i: E.tensor_sub(sb_ln2[:, i:i + 1], sb_ln2[:, i:i + 1], sb_tmp1[:, 0:1]))
            p.op("vector", lambda E, i=i: E.tensor_scalar_add(sb_ln2[:, i:i + 1], sb_ln2[:, i:i + 1], 1e-5), "ve")
            p.wait("scalar", "ve", p.c["ve"])
            p.op("scalar", lambda E, i=i: E.activation(sb_ln1[:, i:i + 1], sb_ln2[:, i:i + 1], AF.Sqrt), "se")
            p.wait("vector", "se", p.c["se"])
            p.op("vector", lambda E, i=i: E.reciprocal(sb_rstd[:, i:i + 1], sb_ln1[:, i:i + 1]))
            p.wait("vector", "te", p.c["te"])
            p.op("vector", lambda E, xt=xt, i=i: E.tensor_scalar(
                sb_xnf, xt, sb_mu[:, i:i + 1], sb_rstd[:, i:i + 1], ALU.subtract, ALU.mult), "ve")
            ve_xn = p.c["ve"]
            p.wait("scalar", "ve", ve_xn)
            p.op("scalar", lambda E, i=i: E.activation(sb_xntok[:, D * i:D * (i + 1)], sb_xnf, AF.Copy), "se")
            for jb in range(2):
                p.wait("tensor", "se", p.c["se"])
                for jj in range(4):
                    j = 4 * jb + jj
                    pst = ps[jj][:, :128].bitcast(BF16)[:, :128]
                    p.op("tensor", lambda E, i=i, j=j, pst=pst: E.transpose(
                        pst, sb_xntok[:, D * i + 128 * j:D * i + 128 * (j + 1)], sb_idb),
                        "te" if jj == 3 else None)
                p.wait("scalar", "te", p.c["te"])
                for jj in range(4):
                    j = 4 * jb + jj
                    pst = ps[jj][:, :128].bitcast(BF16)[:, :128]
                    p.op("scalar", lambda E, i=i, j=j, pst=pst: E.activation(
                        sb_xnd[:, 2048 * j + 128 * i:2048 * j + 128 * (i + 1)], pst, AF.Copy), "se")
        ve_ln_done[grp] = p.c["ve"]
    se_xn_all = p.c["se"]

    if tap == "p0":
        p.wait("scalar", "se", p.c["se"])
        p.wait("scalar", "ve", p.c["ve"])
        p.dma("scalar", d_dev[:, 0:8192], sb_xnd.bitcast(F32), "st")
        p.dma("scalar", d_dev[:, 8192:16384], sb_xntok.bitcast(F32), "st")
        p.dma("scalar", d_dev[:, 16384:16400], sb_mu, "st")
        p.dma("scalar", d_dev[:, 16400:16416], sb_rstd, "st")
        return _finish(nc, stack, p)

    p.wait("sync", "ve", p.c["ve"])
    for wname, src, dst in (("win", d_win, sb_winb), ("phi", d_phi, sb_phib)):
        for half in range(2):
            p.wait("sync", "se", p.c["se"])
            ldw = p.dma("sync", sb_wstage.rearrange("p (j e) -> p j e", j=4),
                        src.rearrange("(j p) e -> p j e", p=128)[:, 4 * half:4 * (half + 1), :], "ld")
            p.wait("scalar", "ld", ldw)
            for jj in range(4):
                j = 4 * half + jj
                p.op("scalar", lambda E, jj=jj, j=j, dst=dst: E.activation(
                    dst[:, 2048 * j:2048 * (j + 1)], sb_wstage[:, 2048 * jj:2048 * (jj + 1)], AF.Copy), "se")

    p.wait("tensor", "se", p.c["se"])
    for j in range(2):
        p.op("tensor", lambda E, j=j: E.matmul(
            ps[3], sb_xnd[:, 2048 * j:2048 * j + 128],
            sb_phib[:, 2048 * j:2048 * j + 512],
            start=(j == 0), stop=(j == 1)), "te" if j == 1 else None)
    p.wait("vector", "te", p.c["te"])
    p.op("vector", lambda E: E.tensor_copy(sb_tmp1[:, 0:1], ps[3][:, 0:1]), "ve")

    last_store = {}

    def guarded_store(tag, bufs, idx):
        key = (tag, idx % len(bufs))
        if key in last_store:
            p.wait("scalar", "sp", last_store[key])

    def wait_slot_stores(eng, sem):
        p.wait(eng, sem, p.c[sem])

    for i in range(MT):
        for q in range(TQ):
            pq = ps[q % 2]
            p.wait("tensor", "se", p.c["se"])
            for j in range(G):
                p.op("tensor", lambda E, i=i, j=j, q=q, pq=pq: E.matmul(
                    pq, sb_xnd[:, 2048 * j + 128 * i:2048 * j + 128 * (i + 1)],
                    sb_phib[:, 2048 * j + 512 * q:2048 * j + 512 * (q + 1)],
                    start=(j == 0), stop=(j == G - 1)), "te" if j == G - 1 else None)
            p.wait("scalar", "te", p.c["te"])
            wait_slot_stores("scalar", f"s{q % 2}")
            eb, lb = sb_Et[q % 2], sb_Lt[q % 2]
            p.op("scalar", lambda E, pq=pq, eb=eb: E.activation(eb, pq, AF.Exp), "se")
            p.op("scalar", lambda E, pq=pq, lb=lb: E.activation(lb, pq, AF.Copy), "se")
            p.wait("scalar", "se", p.c["se"])
            p.dma("scalar", d_E[128 * i:128 * (i + 1), 512 * q:512 * (q + 1)], eb, f"s{q % 2}")
            p.dma("scalar", d_logits[128 * i:128 * (i + 1), 512 * q:512 * (q + 1)], lb, f"s{q % 2}")
            if tap == "p1" and i == 0 and q == 0:
                p.op("scalar", lambda E, pq=pq: E.activation(sb_sev.bitcast(F32)[:, 0:512], pq, AF.Copy), "se")
                p.dma("scalar", d_dev[:, 66048:66560], sb_sev.bitcast(F32)[:, 0:512], "st")
                p.dma("scalar", d_dev[:, 66560:67072], lb, "st")
    p.wait("gpsimd", "s0", p.c["s0"])
    p.wait("gpsimd", "s1", p.c["s1"])
    p.op("gpsimd", lambda E: E.collective_compute(
        "AllReduce", ALU.add, replica_groups=[list(range(8))],
        ins=[d_E.opt()], outs=[d_den.opt()]), "cc")
    cc_ar = p.c["cc"]

    if tap == "p1":
        p.wait("scalar", "sp", p.c["sp"])
        p.dma("scalar", d_dev[:, 0:32768].rearrange("p (i q) -> p i q", i=16), d_E.rearrange("(i p) q -> p i q", p=128), "st")
        p.dma("scalar", d_dev[:, 32768:65536].rearrange("p (i q) -> p i q", i=16), d_logits.rearrange("(i p) q -> p i q", p=128), "st")
        p.wait("tensor", "sp", p.c["sp"])
        for j in range(G):
            p.op("tensor", lambda E, j=j: E.matmul(
                ps[0], sb_xnd[:, 2048 * j:2048 * j + 128],
                sb_phib[:, 2048 * j:2048 * j + 512],
                start=(j == 0), stop=(j == G - 1)), "te" if j == G - 1 else None)
        p.wait("scalar", "te", p.c["te"])
        p.op("scalar", lambda E: E.activation(sb_Lt[0], ps[0], AF.Copy), "se")
        p.dma("scalar", d_dev[:, 65536:66048], sb_Lt[0], "st")
        return _finish(nc, stack, p)

    for g in range(16):
        for q in range(TQ):
            pq = ps[q % 2]
            p.wait("tensor", "se", p.c["se"])
            for j in range(G):
                p.op("tensor", lambda E, g=g, j=j, q=q, pq=pq: E.matmul(
                    pq, sb_winb[:, 2048 * j + 128 * g:2048 * j + 128 * (g + 1)],
                    sb_xnd[:, 2048 * j + 512 * q:2048 * j + 512 * (q + 1)],
                    start=(j == 0), stop=(j == G - 1)), "te" if j == G - 1 else None)
            p.wait("scalar", "te", p.c["te"])
            if g < 8:
                if q == 0:
                    p.wait("scalar", "ve", p.c["ve"])
                p.op("scalar", lambda E, q=q, pq=pq: E.activation(
                    sb_xinp[:, 16 + 512 * q:16 + 512 * (q + 1)], pq, AF.Copy), "se")
            else:
                if q == 0:
                    wait_slot_stores("scalar", "sp")
                p.op("scalar", lambda E, q=q, pq=pq: E.activation(
                    sb_ct0[:, 512 * q:512 * (q + 1)], pq, AF.Silu), "se")
        if g < 8:
            p.wait("vector", "se", p.c["se"])
            if g == 0:
                p.op("vector", lambda E: E.memset(sb_xinp[:, 0:16], 0.0))
            x0, x1 = sb_xinp[:, 13:13 + T], sb_xinp[:, 14:14 + T]
            x2, x3 = sb_xinp[:, 15:15 + T], sb_xinp[:, 16:16 + T]
            p.wait("vector", "te", p.c["te"])
            p.op("vector", lambda E, g=g, x0=x0: E.tensor_scalar_mul(sb_ct0, x0, sb_convw[:, 4 * g:4 * g + 1]))
            p.op("vector", lambda E, g=g, x1=x1: E.scalar_tensor_tensor(
                sb_ct1, x1, sb_convw[:, 4 * g + 1:4 * g + 2], sb_ct0, ALU.mult, ALU.add))
            p.op("vector", lambda E, g=g, x2=x2: E.scalar_tensor_tensor(
                sb_ct0, x2, sb_convw[:, 4 * g + 2:4 * g + 3], sb_ct1, ALU.mult, ALU.add))
            p.op("vector", lambda E, g=g, x3=x3: E.scalar_tensor_tensor(
                sb_ct1, x3, sb_convw[:, 4 * g + 3:4 * g + 4], sb_ct0, ALU.mult, ALU.add), "ve")
            p.wait("scalar", "ve", p.c["ve"])
            wait_slot_stores("scalar", "sp")
            p.wait("scalar", "te", p.c["te"])
            p.op("scalar", lambda E, g=g: E.activation(sb_xcg, sb_ct1, AF.Silu, bias=sb_convb[:, g:g + 1]), "se")
            se_xcv = p.c["se"]
            p.wait("scalar", "se", se_xcv)
            p.dma("scalar", d_xconv[g], sb_xcg, "sp")
            p.wait("tensor", "se", se_xcv)
            for q in range(TQ):
                p.op("tensor", lambda E, g=g, q=q: E.matmul(
                    ps_big[0:33, 512 * q:512 * (q + 1)], sb_xpj[:, 33 * g:33 * (g + 1)],
                    sb_xcg[:, 512 * q:512 * (q + 1)],
                    start=(g == 0), stop=(g == 7)), "te" if q == TQ - 1 else None)
        else:
            p.wait("scalar", "se", p.c["se"])
            p.dma("scalar", d_sz[g - 8], sb_ct0, "sp")
    p.wait("scalar", "te", p.c["te"])
    p.op("scalar", lambda E: E.activation(sb_ssm[0:33, :], ps_big[0:33, :], AF.Copy), "se")
    se_ssm = p.c["se"]
    p.wait("scalar", "se", se_ssm)
    p.dma("scalar", d_ssm, sb_ssm[0:33, :], "sp")
    sp_ssm = p.c["sp"]

    if tap == "p2":
        p.wait("scalar", "sp", p.c["sp"])
        p.dma("scalar", d_dev[:, 0:16384].rearrange("p (g t) -> p g t", g=8), d_xconv.rearrange("g p t -> p g t"), "st")
        p.dma("scalar", d_dev[0:33, 16384:18432], d_ssm, "st")
        p.dma("scalar", d_dev[:, 18432:34816].rearrange("p (g t) -> p g t", g=8), d_sz.rearrange("g p t -> p g t"), "st")
        return _finish(nc, stack, p)

    p.wait("sync", "sp", sp_ssm)
    ld_sbc = p.dma("sync", sb_sbc, d_ssm[32:33, :].partition_broadcast(128), "ld")
    p.wait("sync", "se", p.c["se"])
    ld_wo = p.dma("sync", sb_wfs.rearrange("p (j e) -> p j e", j=8),
                  d_wout.rearrange("(j p) e -> p j e", p=128), "ld")
    p.wait("scalar", "ld", ld_wo)
    for j in range(G):
        p.op("scalar", lambda E, j=j: E.activation(
            sb_woutb[:, 1024 * j:1024 * (j + 1)], sb_wfs[:, 1024 * j:1024 * (j + 1)], AF.Copy), "se")
    se_wout = p.c["se"]
    p.wait("scalar", "ld", ld_wo)
    p.wait("vector", "ld", ld_wo)
    p.wait("vector", "sp", p.c["sp"])
    for g in range(G):
        wait_slot_stores("scalar", "sp")
        p.wait("vector", "sp", p.c["sp"])
        p.op("vector", lambda E, g=g: E.tensor_scalar(
            sb_ct0, sb_sbc, sb_dtw[:, g:g + 1], sb_dtb[:, g:g + 1], ALU.mult, ALU.add), "ve")
        p.wait("scalar", "ve", p.c["ve"])
        p.op("scalar", lambda E: E.activation(sb_ct1, sb_ct0, AF.Abs))
        p.op("scalar", lambda E: E.activation(sb_ct1, sb_ct1, AF.Exp, scale=-1.0))
        p.op("scalar", lambda E: E.activation(sb_ct1, sb_ct1, AF.Ln, bias=1.0))
        p.op("scalar", lambda E: E.activation(sb_ct0, sb_ct0, AF.Relu), "se")
        p.wait("vector", "se", p.c["se"])
        p.op("vector", lambda E: E.tensor_add(sb_ct0, sb_ct0, sb_ct1), "ve")
        p.wait("scalar", "ve", p.c["ve"])
        p.dma("scalar", d_dt[g], sb_ct0, "sp")

    if tap == "p3":
        p.wait("scalar", "sp", p.c["sp"])
        p.dma("scalar", d_dev[:, 0:16384].rearrange("p (g t) -> p g t", g=8), d_dt.rearrange("g p t -> p g t"), "st")
        return _finish(nc, stack, p)

    p.op("vector", lambda E: E.memset(sb_hcar, 0.0), "ve")
    for c in range(NCH):
        c0 = TC * c
        p.wait("sync", "ve", p.c["ve"])
        p.wait("sync", "ge", p.c["ge"])
        p.wait("sync", "sp", p.c["sp"])
        bsrc = d_ssm[0:16, c0:c0 + TC].unsqueeze(0).broadcast_to([128, 16, TC])
        csrc = d_ssm[16:32, c0:c0 + TC].unsqueeze(0).broadcast_to([128, 16, TC])
        lsem = f"l{c % 2}"
        p.dma("sync", sb_Bbc.rearrange("p (n t) -> p n t", n=16), bsrc, lsem)
        p.dma("sync", sb_Cbc.rearrange("p (n t) -> p n t", n=16), csrc, lsem)
        p.dma("sync", sb_xcs.rearrange("p (g t) -> p g t", g=8),
              d_xconv[:, :, c0:c0 + TC].rearrange("g p t -> p g t"), lsem)
        p.dma("sync", sb_dts.rearrange("p (g t) -> p g t", g=8),
              d_dt[:, :, c0:c0 + TC].rearrange("g p t -> p g t"), lsem)
        p.dma("sync", sb_szs.rearrange("p (g t) -> p g t", g=8),
              d_sz[:, :, c0:c0 + TC].rearrange("g p t -> p g t"), lsem)
        ld_grp = p.dma("sync", sb_xres.rearrange("p (i d) -> p i d", i=2),
                       d_x.rearrange("(i p) d -> p i d", p=128)[:, 2 * c:2 * c + 2, :], lsem)
        p.wait("vector", "te", p.c["te"])
        for g in range(G):
            dts = sb_dts[:, TC * g:TC * (g + 1)]
            xcs = sb_xcs[:, TC * g:TC * (g + 1)]
            p.wait("scalar", lsem, ld_grp)
            p.wait("scalar", "ve", p.c["ve"])
            for n in range(NSS):
                p.op("scalar", lambda E, g=g, n=n, dts=dts: E.activation(
                    sb_dA[:, TC * n:TC * (n + 1)], dts, AF.Exp,
                    scale=sb_aneg[:, 16 * g + n:16 * g + n + 1]),
                    "se" if n == NSS - 1 else None)
            p.op("scalar", lambda E, g=g, dts=dts: E.activation(
                sb_e0, sb_aneg[:, 16 * g:16 * (g + 1)], AF.Exp, scale=dts[:, 0:1]), "se")
            se_e0 = p.c["se"]
            p.wait("vector", lsem, ld_grp)
            p.wait("vector", "ge", p.c["ge"])
            p.op("vector", lambda E, dts=dts, xcs=xcs: E.tensor_mul(sb_ug, dts, xcs), "ve")
            ve_u = p.c["ve"]
            p.wait("gpsimd", lsem, ld_grp)
            p.wait("gpsimd", "ve", ve_u)
            u3 = sb_ug.unsqueeze(1).broadcast_to([128, 16, TC])
            p.op("gpsimd", lambda E, u3=u3: E.tensor_tensor(
                sb_dBx.rearrange("p (n t) -> p n t", n=16),
                sb_Bbc.rearrange("p (n t) -> p n t", n=16), u3, ALU.mult), "ge")
            ge_dbx = p.c["ge"]
            dA3 = sb_dA.rearrange("p (n t) -> p n t", n=16)
            dBx3 = sb_dBx.rearrange("p (n t) -> p n t", n=16)
            p.wait("vector", "se", se_e0)
            p.op("vector", lambda E, g=g: E.tensor_mul(sb_fixv, sb_e0, sb_hcar[:, 16 * g:16 * (g + 1)]))
            p.wait("vector", "ge", ge_dbx)
            p.op("vector", lambda E, dBx3=dBx3: E.tensor_add(dBx3[:, :, 0], dBx3[:, :, 0], sb_fixv))
            p.op("vector", lambda E, dA3=dA3: E.memset(dA3[:, :, 0], 0.0))
            p.op("vector", lambda E: E.tensor_tensor_scan(sb_h, sb_dA, sb_dBx, 0.0, ALU.mult, ALU.add))
            h3 = sb_h.rearrange("p (n t) -> p n t", n=16)
            p.op("vector", lambda E, g=g, h3=h3: E.tensor_copy(sb_hcar[:, 16 * g:16 * (g + 1)], h3[:, :, TC - 1]), "ve")
            ve_scan = p.c["ve"]
            p.wait("gpsimd", "ve", ve_scan)
            p.op("gpsimd", lambda E: E.tensor_tensor(sb_dBx, sb_h, sb_Cbc, ALU.mult), "ge")
            ge_hc = p.c["ge"]
            p.wait("vector", "ge", ge_hc)
            hc3 = sb_dBx.rearrange("p (n t) -> p t n", n=16)
            p.op("vector", lambda E, hc3=hc3: E.tensor_reduce(sb_yg, hc3, AX.X, ALU.add))
            ygslot = sb_ygt[:, 2048 * (c % 2) + 256 * g:2048 * (c % 2) + 256 * (g + 1)]
            p.op("vector", lambda E, g=g, ygslot=ygslot: E.tensor_mul(
                ygslot, sb_yg, sb_szs[:, TC * g:TC * (g + 1)]), "ve")
        ve_chunk = p.c["ve"]
        p.wait("tensor", "ve", ve_chunk)
        if c == 0:
            p.wait("tensor", "se", se_wout)
        for ti in range(2):
            for eh in range(2):
                pq = ps[2 * ti + eh]
                for g in range(G):
                    stat = sb_ygt[:, 2048 * (c % 2) + 256 * g + 128 * ti:
                                  2048 * (c % 2) + 256 * g + 128 * (ti + 1)]
                    p.op("tensor", lambda E, stat=stat, g=g, eh=eh, pq=pq: E.matmul(
                        pq, stat, sb_woutb[:, 1024 * g + 512 * eh:1024 * g + 512 * (eh + 1)],
                        start=(g == 0), stop=(g == G - 1)), "te" if g == G - 1 else None)
        te_op = p.c["te"]
        p.wait("vector", "te", te_op)
        p.wait("vector", lsem, ld_grp)
        p.wait("vector", f"s{(c - 1) % 2}", p.c[f"s{(c - 1) % 2}"] if c > 0 else 0)
        for ti in range(2):
            for eh in range(2):
                pq = ps[2 * ti + eh]
                dst = sb_mact[ti][:, 512 * eh:512 * (eh + 1)]
                p.op("vector", lambda E, pq=pq, dst=dst, ti=ti, eh=eh: E.scalar_tensor_tensor(
                    dst, pq, 1.0, sb_xres[:, 1024 * ti + 512 * eh:1024 * ti + 512 * (eh + 1)],
                    ALU.mult, ALU.add), "ve")
        p.wait("scalar", "ve", p.c["ve"])
        wait_slot_stores("scalar", f"s{c % 2}")
        for ti in range(2):
            p.dma("scalar", d_macc[128 * (2 * c + ti):128 * (2 * c + ti + 1), :],
                  sb_mact[ti], f"s{c % 2}")

    if tap == "p4":
        p.wait("scalar", "sp", p.c["sp"])
        p.dma("scalar", d_dev[:, 0:16384].rearrange("p (i d) -> p i d", i=16), d_macc.rearrange("(i p) d -> p i d", p=128), "st")
        return _finish(nc, stack, p)

    p.wait("sync", "cc", cc_ar)
    p.wait("sync", "te", p.c["te"])
    for q in range(TQ):
        for i in range(MT):
            lb, db, dspb = sb_lgt[i % 2], sb_dnt[i % 2], sb_dsp[i % 2]
            if i >= 2 or q > 0:
                p.wait("sync", "se", p.c["se"])
            p.dma("sync", lb, d_logits[128 * i:128 * (i + 1), 512 * q:512 * (q + 1)], f"l{i % 2}")
            ld_lq = p.dma("sync", db, d_den[128 * i:128 * (i + 1), 512 * q:512 * (q + 1)], f"l{i % 2}")
            p.wait("scalar", f"l{i % 2}", ld_lq)
            p.op("scalar", lambda E, db=db: E.activation(db, db, AF.Ln), "se")
            p.wait("vector", "se", p.c["se"])
            p.op("vector", lambda E, lb=lb, db=db: E.tensor_sub(lb, lb, db), "ve")
            p.wait("scalar", "ve", p.c["ve"])
            p.wait("scalar", "te", p.c["te"])
            p.op("scalar", lambda E, lb=lb, dspb=dspb: E.activation(dspb, lb, AF.Exp), "se")
            se_dsp = p.c["se"]
            if tap == "p5q0" and q == 0:
                p.wait("scalar", "se", se_dsp)
                p.dma("scalar", d_dev[:, 47104 + 256 * i:47104 + 256 * (i + 1)], dspb.bitcast(F32), "st")
                p.dma("scalar", d_dev[:, 51200 + 512 * i:51200 + 512 * (i + 1)], db, "st")
                p.dma("scalar", d_dev[:, 59392 + 512 * i:59392 + 512 * (i + 1)], lb, "st")
            p.wait("tensor", "se", se_dsp)
            if i == 0:
                p.wait("tensor", "ve", p.c["ve"])
            for j in range(G):
                p.op("tensor", lambda E, i=i, j=j, dspb=dspb: E.matmul(
                    ps8[j], sb_xntok[:, 1024 * i + 128 * j:1024 * i + 128 * (j + 1)],
                    dspb, start=(i == 0), stop=(i == MT - 1)), "te" if j == G - 1 else None)
        te_sl = p.c["te"]
        p.wait("scalar", "te", te_sl)
        wait_slot_stores("scalar", "sp")
        for j in range(G):
            p.op("scalar", lambda E, j=j: E.activation(sb_sev[:, 512 * j:512 * (j + 1)], ps8[j], AF.Copy), "se")
        p.wait("scalar", "se", p.c["se"])
        sev3 = sb_sev.rearrange("p (j t) -> p j t", j=8)
        for half in range(2):
            r = 2 * q + half
            p.dma("scalar", d_ssnd[r].rearrange("(j p) t -> p j t", p=128),
                  sev3[:, :, 256 * half:256 * (half + 1)], "sp")
    sp_slots = p.c["sp"]
    p.wait("gpsimd", "sp", sp_slots)
    p.op("gpsimd", lambda E: E.collective_compute(
        "AllGather", ALU.bypass, replica_groups=[list(range(8))],
        ins=[d_ssnd.opt()], outs=[d_sag.opt()]), "cc2")
    cc_s = p.c["cc2"]

    if tap == "p5":
        p.wait("scalar", "cc2", cc_s)
        p.dma("scalar", d_dev[:, 0:8192], d_sag.bitcast(F32)[0, 0][0:128, :].unsqueeze(1).broadcast_to([128, 64, 128]), "st")
        return _finish(nc, stack, p)

    p.wait("gpsimd", "cc2", cc_s)
    srcv5 = sb_srcv.rearrange("p (j n b q) -> p j n b q", j=8, n=8, b=8)
    stg4 = sb_stg.rearrange("p (b j t) -> p b j t", b=8, j=8)
    for b8 in range(8):
        p.c["gl"] += 16
        ld_sr = p.c["gl"]
        p.emit("gpsimd", lambda E, b8=b8: E.dma_start(
            out=stg4[:, b8], in_=dyn_sr[0][b8], bounds_check=None).then_inc(p.sems["gl"], 16))
    p.wait("vector", "gl", ld_sr)
    stg_r = sb_stg.rearrange("p (b j n q) -> p j n b q", b=8, j=8, n=8)
    for j in range(G):
        p.op("vector", lambda E, j=j: E.tensor_copy(srcv5[:, j], stg_r[:, j]),
             "ve" if j == G - 1 else None)
    ve_srcv = p.c["ve"]
    if tap == "p6a":
        p.wait("scalar", "gl", ld_sr)
        p.dma("scalar", d_dev[:, 0:8192], sb_srcv.bitcast(F32), "st")
        return _finish(nc, stack, p)

    for n in range(8):
        p.wait("sync", "se", p.c["se"])
        ld_w = p.dma("sync", sb_wfs.rearrange("p (j e) -> p j e", j=8),
                     d_wffn[n].rearrange("(j p) e -> p j e", p=128), f"l{n % 2}")
        p.wait("scalar", f"l{n % 2}", ld_w)
        for j in range(G):
            p.op("scalar", lambda E, j=j: E.activation(
                sb_wfb[:, 1024 * j:1024 * (j + 1)], sb_wfs[:, 1024 * j:1024 * (j + 1)], AF.Copy), "se")
        se_wf = p.c["se"]
        p.wait("sync", "ve", p.c["ve"])
        ld_bb = p.dma("sync", sb_bbc, d_bffn[n:n + 1, :].partition_broadcast(128), "ld")
        p.wait("tensor", "se", se_wf)
        p.wait("tensor", "ld", ld_bb)
        p.wait("tensor", "ve", ve_srcv)
        p.wait("tensor", "ve", p.c["ve"])
        for rh in range(2):
            for eh in range(2):
                pq = ps[2 * rh + eh]
                for j in range(G):
                    stat = sb_srcv[:, (j * 8 + n) * 256 + 128 * rh:
                                   (j * 8 + n) * 256 + 128 * (rh + 1)]
                    p.op("tensor", lambda E, stat=stat, j=j, eh=eh, pq=pq: E.matmul(
                        pq, stat, sb_wfb[:, 1024 * j + 512 * eh:1024 * j + 512 * (eh + 1)],
                        start=(j == 0), stop=(j == G - 1)), "te" if j == G - 1 else None)
        te_y = p.c["te"]
        p.wait("vector", "te", te_y)
        p.wait("vector", "ld", ld_bb)
        for rh in range(2):
            for eh in range(2):
                pq = ps[2 * rh + eh]
                p.wait("vector", "se", p.c["se"])
                p.op("vector", lambda E, pq=pq, rh=rh, eh=eh: E.scalar_tensor_tensor(
                    sb_yev[rh], pq, 1.0, sb_bbc[:, 512 * eh:512 * (eh + 1)], ALU.mult, ALU.add), "ve")
                p.wait("scalar", "ve", p.c["ve"])
                wait_slot_stores("scalar", "sp")
                p.op("scalar", lambda E, rh=rh: E.activation(sb_ybf, sb_yev[rh], AF.Copy), "se")
                p.wait("scalar", "se", p.c["se"])
                for bb in range(4):
                    b = 4 * rh + bb
                    p.dma("scalar", d_ysnd[b, 32 * n:32 * (n + 1), 512 * eh:512 * (eh + 1)],
                          sb_ybf[32 * bb:32 * (bb + 1), :], "sp")
    sp_y = p.c["sp"]
    p.wait("gpsimd", "sp", sp_y)
    p.op("gpsimd", lambda E: E.collective_compute(
        "AllGather", ALU.bypass, replica_groups=[list(range(8))],
        ins=[d_ysnd.opt()], outs=[d_yag.opt()]), "cc3")
    cc_y = p.c["cc3"]

    if tap == "p6":
        p.wait("scalar", "cc3", cc_y)
        p.dma("scalar", d_dev[:, 0:1024].rearrange("p (k e) -> p k e", k=2),
              d_yag.bitcast(F32)[0, 0].rearrange("(k p) e -> p k e", p=128), "st")
        return _finish(nc, stack, p)

    p.wait("gpsimd", "cc3", cc_y)
    yb4 = sb_yb.rearrange("p (r k e) -> p r k e", r=8, k=2)
    for r in range(8):
        p.c["gl"] += 16
        ld_yb = p.c["gl"]
        p.emit("gpsimd", lambda E, r=r: E.dma_start(
            out=yb4[:, r], in_=dyn_y[0][r], bounds_check=None).then_inc(p.sems["gl"], 16))

    if tap == "p7":
        p.wait("scalar", "ve", p.c["ve"])
        p.wait("scalar", "gl", ld_yb)
        p.dma("scalar", d_dev[:, 0:16], sb_rs, "st")
        p.dma("scalar", d_dev[:, 16:32], sb_rrs, "st")
        p.dma("scalar", d_dev[:, 1024:9216], sb_yb.bitcast(F32), "st")
    p.wait("sync", "sp", p.c["sp"])
    p.wait("sync", "s0", p.c["s0"])
    p.wait("sync", "s1", p.c["s1"])
    for i in range(MT):
        p.wait("sync", "se", p.c["se"])
        p.wait("sync", "ve", p.c["ve"])
        p.dma("sync", sb_Ei, d_logits[128 * i:128 * (i + 1), :], f"l{i % 2}")
        ld_m = p.dma("sync", sb_mact[i % 2], d_macc[128 * i:128 * (i + 1), :], f"l{i % 2}")
        p.wait("scalar", f"l{i % 2}", ld_m)
        p.wait("scalar", "ve", p.c["ve"])
        p.op("scalar", lambda E: E.activation(sb_Ei, sb_Ei, AF.Exp), "se")
        p.wait("vector", "se", p.c["se"])
        p.wait("vector", "te", p.c["te"])
        p.op("vector", lambda E, i=i: E.reduce_sum(sb_rs[:, i:i + 1], sb_Ei, AX.X), "ve")
        p.wait("scalar", "ve", p.c["ve"])
        p.op("scalar", lambda E, i=i: E.activation(sb_rrs[:, i:i + 1], sb_rs[:, i:i + 1], AF.Ln, scale=1.0), "se")
        p.op("scalar", lambda E: E.activation(sb_cmbf, sb_Ei, AF.Ln), "se")
        p.wait("vector", "se", p.c["se"])
        p.op("vector", lambda E, i=i: E.tensor_scalar_mul(sb_rrs[:, i:i + 1], sb_rrs[:, i:i + 1], -1.0), "ve")
        p.wait("scalar", "ve", p.c["ve"])
        p.op("scalar", lambda E, i=i: E.activation(sb_cmbf, sb_cmbf, AF.Exp, bias=sb_rrs[:, i:i + 1]), "se")
        p.op("scalar", lambda E: E.engine_nop(), "se") if False else None
        p.wait("vector", "se", p.c["se"])
        p.op("vector", lambda E: E.tensor_copy(sb_tmp1[:, 0:1], sb_cmbf[:, 0:1]), "ve")
        ve_cmb = p.c["ve"]
        if tap == "p7" and i == 14:
            p.wait("scalar", "ve", ve_cmb)
            p.dma("scalar", d_dev[:, 9216:11264], sb_cmbf, "st")
            p.dma("scalar", d_dev[:, 11264:11280], sb_rs, "st")
            p.dma("scalar", d_dev[:, 11280:11296], sb_rrs, "st")
            p.dma("scalar", d_dev[:, 11296:13344], sb_Ei, "st")
        p.wait("tensor", "ve", ve_cmb)
        for kb in range(4):
            p.wait("tensor", "se", p.c["se"])
            for kk in range(4):
                k = 4 * kb + kk
                p.op("tensor", lambda E, k=k, kk=kk: E.transpose(
                    ps[kk][:, 0:128], sb_cmbf[:, 128 * k:128 * (k + 1)], sb_idf),
                    "te" if kk == 3 else None)
            p.wait("scalar", "te", p.c["te"])
            for kk in range(4):
                k = 4 * kb + kk
                p.op("scalar", lambda E, k=k, kk=kk: E.activation(
                    sb_cnp[:, 128 * k:128 * (k + 1)], ps[kk][:, 0:128], AF.Copy), "se")
        se_cnp = p.c["se"]
        p.wait("tensor", "se", se_cnp)
        p.wait("tensor", "gl", ld_yb)
        p.wait("tensor", "ve", p.c["ve"])
        for eh in range(2):
            pq = ps[eh]
            for k in range(16):
                p.op("tensor", lambda E, k=k, eh=eh, pq=pq: E.matmul(
                    pq, sb_cnp[:, 128 * k:128 * (k + 1)],
                    sb_yb[:, 1024 * k + 512 * eh:1024 * k + 512 * (eh + 1)],
                    start=(k == 0), stop=(k == 15)), "te" if k == 15 else None)
        te_out = p.c["te"]
        p.wait("vector", "te", te_out)
        p.wait("vector", f"l{i % 2}", ld_m)
        for eh in range(2):
            p.op("vector", lambda E, i=i, eh=eh: E.scalar_tensor_tensor(
                sb_outt[i % 2][:, 512 * eh:512 * (eh + 1)], ps[eh], 1.0,
                sb_mact[i % 2][:, 512 * eh:512 * (eh + 1)], ALU.mult, ALU.add), "ve")
        p.wait("scalar", "ve", p.c["ve"])
        wait_slot_stores("scalar", f"s{i % 2}")
        p.dma("scalar", d_out[128 * i:128 * (i + 1), :], sb_outt[i % 2], f"s{i % 2}")

    if tap in ("post", "p5q0"):
        p.wait("scalar", "st", p.c["st"])
        p.dma("scalar", d_dev[:, 0:8192], sb_yb.bitcast(F32), "st")
        for r in range(4):
            p.dma("scalar",
                  d_dev[:, 8192 + 1024 * r:8192 + 1024 * (r + 1)].rearrange("p (k e) -> p k e", k=2),
                  d_yag.bitcast(F32)[r, 0].rearrange("(k p) e -> p k e", p=128), "st")
        p.dma("scalar", d_dev[:, 12288:13312], d_macc[0:128, :], "st")
        p.dma("scalar", d_dev[:, 13312:15360], sb_cmbf, "st")
        p.dma("scalar", d_dev[:, 15360:15376], sb_rs, "st")
        p.dma("scalar", d_dev[:, 15376:15392], sb_rrs, "st")
        p.dma("scalar", d_dev[:, 15392:15904], d_ysnd.bitcast(F32)[0, 0:128, :], "st")
        for r in range(8):
            p.dma("scalar", d_dev[:, 36864 + 1024 * r:36864 + 1024 * (r + 1)].rearrange("p (j t) -> p j t", j=8),
                  d_ssnd.bitcast(F32)[r].rearrange("(j p) t -> p j t", p=128), "st")
        p.dma("scalar", d_dev[:, 45056:45568], d_den[0:128, 0:512], "st")
        p.dma("scalar", d_dev[:, 45568:46080], d_den[1920:2048, 0:512], "st")
        p.dma("scalar", d_dev[:, 46080:46592], d_logits[0:128, 0:512], "st")
        for s in range(8):
            p.dma("scalar", d_dev[:, 24576 + 1024 * s:24576 + 1024 * (s + 1)].rearrange("p (j t) -> p j t", j=8),
                  d_sag.bitcast(F32)[s, 0].rearrange("(j p) t -> p j t", p=128), "st")
        p.dma("scalar", d_dev[:, 32768:33792].rearrange("p (j t) -> p j t", j=8),
              d_ssnd.bitcast(F32)[0].rearrange("(j p) t -> p j t", p=128), "st")
        p.dma("scalar", d_dev[:, 33792:34816], d_den[0:128, 1024:2048], "st")
        p.dma("scalar", d_dev[:, 34816:35840], d_logits[0:128, 1024:2048], "st")

    return _finish(nc, stack, p)


def _finish(nc, stack, p):
    p.wait("sync", "st", p.c["st"])
    p.wait("sync", "sp", p.c["sp"])
    p.wait("sync", "s0", p.c["s0"])
    p.wait("sync", "s1", p.c["s1"])
    with nc.Block() as block:
        @block.sync
        def _(E):
            for fn in p.ops["sync"]:
                fn(E)

        @block.scalar
        def _(E):
            for fn in p.ops["scalar"]:
                fn(E)

        @block.vector
        def _(E):
            for fn in p.ops["vector"]:
                fn(E)

        @block.tensor
        def _(E):
            for fn in p.ops["tensor"]:
                fn(E)

        @block.gpsimd
        def _(E):
            for fn in p.ops["gpsimd"]:
                fn(E)
    stack.close()
    return nc


def kernel(L0_attn_x, L0_attn_in_proj_w, L0_attn_conv_w, L0_attn_conv_b,
           L0_attn_x_proj_w, L0_attn_dt_proj_w, L0_attn_dt_proj_b,
           L0_attn_A_log, L0_attn_out_proj_w, L0_ffn_phi, L0_ffn_W, L0_ffn_b):
    import ml_dtypes
    x = np.ascontiguousarray(np.asarray(L0_attn_x, dtype=np.float32))
    w_in_t = np.ascontiguousarray(np.asarray(L0_attn_in_proj_w, dtype=np.float32).T)
    w_out_t = np.ascontiguousarray(np.asarray(L0_attn_out_proj_w, dtype=np.float32).T)
    phi = np.ascontiguousarray(np.asarray(L0_ffn_phi, dtype=np.float32).reshape(D, NP))
    xproj_t = np.ascontiguousarray(np.asarray(L0_attn_x_proj_w, dtype=np.float32).T)
    conv_w = np.ascontiguousarray(
        np.asarray(L0_attn_conv_w, dtype=np.float32)[:, 0, :].reshape(8, 128, 4).transpose(1, 0, 2).reshape(128, 32))
    conv_b = np.ascontiguousarray(np.asarray(L0_attn_conv_b, dtype=np.float32).reshape(8, 128).T)
    dt_w = np.ascontiguousarray(np.asarray(L0_attn_dt_proj_w, dtype=np.float32).reshape(8, 128).T)
    dt_b = np.ascontiguousarray(np.asarray(L0_attn_dt_proj_b, dtype=np.float32).reshape(8, 128).T)
    a_log = np.ascontiguousarray(
        np.asarray(L0_attn_A_log, dtype=np.float32).reshape(8, 128, NSS).transpose(1, 0, 2).reshape(128, 128))
    W = np.asarray(L0_ffn_W, dtype=np.float32)
    bffn = np.asarray(L0_ffn_b, dtype=np.float32)
    idf = np.eye(128, dtype=np.float32)
    idb = np.eye(128, dtype=ml_dtypes.bfloat16)

    with allow_ici_all_to_all():
        nc = build(tap=DEV_TAP)
    in_maps = []
    for c in range(8):
        in_maps.append({
            "x": np.ascontiguousarray(x[c]),
            "w_in_t": w_in_t, "w_out_t": w_out_t, "phi": phi,
            "xproj_t": xproj_t, "conv_w": conv_w, "conv_b": conv_b,
            "dt_w": dt_w, "dt_b": dt_b, "a_log": a_log,
            "w_ffn": np.ascontiguousarray(W[8 * c:8 * (c + 1)]),
            "b_ffn": np.ascontiguousarray(bffn[8 * c:8 * (c + 1)]),
            "ident_f": idf, "ident_b": idb,
            "rank": np.array([[c]], dtype=np.uint32),
        })
    with allow_ici_all_to_all():
        res = run_bass_kernel_spmd(nc, in_maps, core_ids=list(range(8)))
    outs = [r["out"] if not isinstance(r["out"], dict) else r["out"] for r in res.results]
    return np.stack([np.asarray(o) for o in outs])


# version 04
# speedup vs baseline: 1.0432x; 1.0157x over previous
import contextlib
import numpy as np
import concourse.bass as bass
import concourse.mybir as mybir
from concourse.bass_utils import run_bass_kernel_spmd
from concourse.aot_env import allow_ici_all_to_all

F32 = mybir.dt.float32
BF16 = mybir.dt.bfloat16
AF = mybir.ActivationFunctionType
ALU = mybir.AluOpType
AX = mybir.AxisListType

T = 2048
D = 1024
NP = 2048
NSS = 16
G = 8
TQ = 4
TC = 256
NCH = T // TC
MT = 16

DEV_TAP = None

ENGS = ("sync", "scalar", "vector", "tensor", "gpsimd")


class P:

    def __init__(self, nc):
        self.nc = nc
        self.ops = {e: [] for e in ENGS}
        self.c = {}
        self.sems = {}

    def sem(self, stack, name):
        self.sems[name] = stack.enter_context(self.nc.semaphore(name))
        self.c[name] = 0

    def emit(self, eng, fn):
        self.ops[eng].append(fn)

    def wait(self, eng, sname, val=None):
        v = self.c[sname] if val is None else val
        if v <= 0:
            return
        s = self.sems[sname]
        self.emit(eng, lambda E, s=s, v=v: E.wait_ge(s, v))

    def dma(self, eng, out, in_, sname, slow=False):
        self.c[sname] += 16
        s = self.sems[sname]
        self.emit(eng, lambda E, o=out, i=in_, s=s, sl=slow: E.dma_start(
            out=o, in_=i, allow_slow_non_contiguous=sl).then_inc(s, 16))
        return self.c[sname]

    def op(self, eng, fn, sname=None):
        if sname is None:
            self.emit(eng, lambda E, f=fn: f(E))
            return None
        self.c[sname] += 1
        s = self.sems[sname]
        self.emit(eng, lambda E, f=fn, s=s: f(E).then_inc(s, 1))
        return self.c[sname]


def build(tap=None):
    nc = bass.Bass("TRN2", target_bir_lowering=False, debug=False)
    stack = contextlib.ExitStack()

    d_x = nc.dram_tensor("x", [T, D], F32, kind="ExternalInput").ap()
    d_win = nc.dram_tensor("w_in_t", [D, 2 * D], F32, kind="ExternalInput").ap()
    d_wout = nc.dram_tensor("w_out_t", [D, D], F32, kind="ExternalInput").ap()
    d_phi = nc.dram_tensor("phi", [D, NP], F32, kind="ExternalInput").ap()
    d_xproj = nc.dram_tensor("xproj_t", [D, 33], F32, kind="ExternalInput").ap()
    d_wffn = nc.dram_tensor("w_ffn", [8, D, D], F32, kind="ExternalInput").ap()
    d_bffn = nc.dram_tensor("b_ffn", [8, D], F32, kind="ExternalInput").ap()
    d_convw = nc.dram_tensor("conv_w", [128, 32], F32, kind="ExternalInput").ap()
    d_convb = nc.dram_tensor("conv_b", [128, 8], F32, kind="ExternalInput").ap()
    d_dtw = nc.dram_tensor("dt_w", [128, 8], F32, kind="ExternalInput").ap()
    d_dtb = nc.dram_tensor("dt_b", [128, 8], F32, kind="ExternalInput").ap()
    d_alog = nc.dram_tensor("a_log", [128, 128], F32, kind="ExternalInput").ap()
    d_rank = nc.dram_tensor("rank", [1, 1], mybir.dt.uint32, kind="ExternalInput").ap()
    d_idf = nc.dram_tensor("ident_f", [128, 128], F32, kind="ExternalInput").ap()
    d_idb = nc.dram_tensor("ident_b", [128, 128], BF16, kind="ExternalInput").ap()
    d_out = nc.dram_tensor("out", [T, D], F32, kind="ExternalOutput").ap()

    d_fin = nc.dram_tensor("fence_in", [128, 4], F32).ap()
    d_fout = nc.dram_tensor("fence_out", [128, 4], F32).ap()
    d_gA = nc.dram_tensor("guard_A", [T, 512], F32).ap()
    d_E = nc.dram_tensor("E_scr", [T, NP], F32).ap()
    d_gB = nc.dram_tensor("guard_B", [T, 512], F32).ap()
    d_logits = nc.dram_tensor("logits_scr", [T, NP], F32).ap()
    d_den = nc.dram_tensor("den_scr", [T, NP], F32).ap()
    d_gC = nc.dram_tensor("guard_C", [T, 512], F32).ap()
    d_sz = nc.dram_tensor("sz_scr", [G, 128, T], F32).ap()
    d_xconv = nc.dram_tensor("xconv_scr", [G, 128, T], F32).ap()
    d_dt = nc.dram_tensor("dt_scr", [G, 128, T], F32).ap()
    d_ssm = nc.dram_tensor("ssm_scr", [33, T], F32).ap()
    d_macc = nc.dram_tensor("macc_scr", [T, D], F32).ap()
    d_guard0 = nc.dram_tensor("guard0", [8, D, 256], BF16).ap()
    d_ssnd = nc.dram_tensor("slots_snd", [8, D, 256], BF16).ap()
    d_guard1 = nc.dram_tensor("guard1", [8, D, 256], BF16).ap()
    d_sag = nc.dram_tensor("slots_ag", [8, 8, D, 256], BF16).ap()
    d_guard2 = nc.dram_tensor("guard2", [8, 256, D], BF16).ap()
    d_ysnd = nc.dram_tensor("y_snd", [8, 256, D], BF16).ap()
    d_guard3 = nc.dram_tensor("guard3", [8, 256, D], BF16).ap()
    d_yag = nc.dram_tensor("y_ag", [8, 8, 256, D], BF16).ap()
    d_dev = None
    if tap is not None:
        d_dev = nc.dram_tensor("dev", [128, 81920], F32, kind="ExternalOutput").ap()

    NF32 = 53152
    mega = nc.alloc_sbuf_tensor("mega", [128, NF32], F32).ap()
    cursor = [0]

    def f32v(n_elems, at=None):
        off = cursor[0] if at is None else at
        v = mega[:, off:off + n_elems]
        if at is None:
            cursor[0] = off + n_elems
        return v

    def bf16v(n_elems, at=None):
        assert n_elems % 2 == 0
        off = cursor[0] if at is None else at
        v = mega[:, off:off + n_elems // 2].bitcast(BF16)
        if at is None:
            cursor[0] = off + n_elems // 2
        return v

    sb_aneg = f32v(128)
    sb_convw = f32v(32)
    sb_convb = f32v(8)
    sb_dtw = f32v(8)
    sb_dtb = f32v(8)
    sb_idf = f32v(128)
    sb_idb = bf16v(128)
    sb_mu = f32v(16)
    sb_rstd = f32v(16)
    sb_ln1 = f32v(16)
    sb_ln2 = f32v(16)
    sb_rs = f32v(16)
    sb_rrs = f32v(16)
    sb_hcar = f32v(128)
    sb_e0 = f32v(16)
    sb_fixv = f32v(16)
    sb_xpj = f32v(8 * 33)
    sb_bbc = f32v(1024)
    sb_tmp1 = f32v(16)
    sb_rank = f32v(2).bitcast(mybir.dt.uint32)
    assert cursor[0] <= 1952, cursor[0]
    cursor[0] = 1952
    sb_sbc = f32v(2048)
    sb_ssm = f32v(2048)
    sb_xntok = bf16v(16 * 1024)
    BASE_A = cursor[0]
    sb_xnd = bf16v(8 * 2048, at=BASE_A)
    sb_Bbc = f32v(4096, at=BASE_A)
    sb_Cbc = f32v(4096, at=BASE_A + 4096)
    sb_srcv = bf16v(16 * 1024, at=BASE_A)
    BASE_B = BASE_A + 8192
    sb_stg = bf16v(16 * 1024, at=BASE_B)
    sb_winb = bf16v(8 * 2048, at=BASE_B)
    sb_dA = f32v(4096, at=BASE_B)
    sb_dBx = f32v(4096, at=BASE_B + 4096)
    sb_yb = bf16v(16 * 1024, at=BASE_B)
    BASE_C = BASE_B + 8192
    sb_phib = bf16v(8 * 2048, at=BASE_C)
    sb_h = f32v(4096, at=BASE_C)
    sb_xcs = f32v(2048, at=BASE_C + 4096)
    sb_dts = f32v(2048, at=BASE_C + 6144)
    sb_wfs = f32v(8 * 1024, at=BASE_C)
    BASE_D = BASE_C + 8192
    sb_xt = [f32v(4096, at=BASE_D + 4096 * i) for i in range(2)]
    sb_wstage = f32v(8192, at=BASE_D)
    sb_xinp = f32v(2064, at=BASE_D)
    sb_ct0 = f32v(2048, at=BASE_D + 2176)
    sb_ct1 = f32v(2048, at=BASE_D + 4224)
    sb_xcg = f32v(2048, at=BASE_D + 6272)
    sb_szs = f32v(2048, at=BASE_D)
    sb_ygt = bf16v(2 * 2048, at=BASE_D + 2048)
    sb_yg = f32v(256, at=BASE_D + 4096)
    sb_ug = f32v(256, at=BASE_D + 4352)
    sb_xres = f32v(2048, at=BASE_D + 4608)
    sb_wfb = bf16v(8 * 1024, at=BASE_D)
    sb_yev = [f32v(512, at=BASE_D + 4608 + 512 * i) for i in range(2)]
    sb_ybf = bf16v(512, at=BASE_D + 5632)
    sb_Ei = f32v(2048, at=BASE_D)
    sb_cmbf = f32v(2048, at=BASE_D + 2048)
    sb_cmb = bf16v(512, at=BASE_D + 5632)
    sb_mact = [f32v(1024, at=BASE_D + 4096 + 1024 * i) for i in range(2)]
    sb_outt = [f32v(1024, at=BASE_D + 6144 + 1024 * i) for i in range(2)]
    BASE_E = BASE_D + 8192
    sb_woutb = bf16v(8 * 1024, at=BASE_E)
    sb_Et = [f32v(512, at=BASE_E + 4096 + 512 * i) for i in range(2)]
    sb_Lt = [f32v(512, at=BASE_E + 5120 + 512 * i) for i in range(2)]
    sb_x2 = f32v(1024, at=BASE_E + 4096)
    sb_xnf = f32v(1024, at=BASE_E + 5120)
    sb_cnp = bf16v(16 * 128, at=BASE_E + 4096)
    sb_lgt = [f32v(512, at=BASE_E + 512 * i) for i in range(2)]
    sb_dnt = [f32v(512, at=BASE_E + 1024 + 512 * i) for i in range(2)]
    sb_dsp = [bf16v(512, at=BASE_E + 2048 + 256 * i) for i in range(2)]
    sb_sev = bf16v(8 * 512, at=BASE_E + 2560)
    assert BASE_E + 6144 <= NF32, (BASE_E, NF32)

    ps = [nc.alloc_psum_tensor(f"ps{i}", [128, 512], F32).ap() for i in range(4)]
    ps_big = nc.alloc_psum_tensor("ps_big", [128, 2048], F32).ap()
    ps8 = ps + [ps_big[:, 512 * i:512 * (i + 1)] for i in range(4)]

    p = P(nc)
    for s in ("ld", "l0", "l1", "sp", "s0", "s1", "te", "ve", "se", "ge", "gl",
              "cc", "cc2", "cc3", "st"):
        p.sem(stack, s)

    p.dma("sync", sb_rank[0:1, 0:1], d_rank, "ld")
    ld_rank = p.c["ld"]
    sag_v = d_sag.rearrange("b c (j p) t -> c b p j t", p=128)
    yg5 = d_yag.rearrange("r c (k p) e -> c r p k e", p=128)
    dyn_sr = [None]
    dyn_y = [None]
    def _mk_dyn(E):
        rv = E.value_load(sb_rank[0:1, 0:1])
        dyn_sr[0] = sag_v[bass.ts(rv, 1)].squeeze(0)
        dyn_y[0] = yg5[bass.ts(rv, 1)].squeeze(0)
    p.dma("sync", sb_idf, d_idf, "ld")
    p.dma("sync", sb_idb, d_idb, "ld")
    p.dma("sync", sb_aneg, d_alog, "ld")
    p.dma("sync", sb_convw, d_convw, "ld")
    p.dma("sync", sb_convb, d_convb, "ld")
    p.dma("sync", sb_dtw, d_dtw, "ld")
    p.dma("sync", sb_dtb, d_dtb, "ld")
    p.dma("sync", sb_xpj.rearrange("p (g e) -> p g e", g=8),
          d_xproj.rearrange("(g p) e -> p g e", p=128), "ld")
    ld_params = p.c["ld"]
    p.wait("scalar", "ld", ld_params)
    p.op("scalar", lambda E: E.activation(sb_aneg, sb_aneg, AF.Exp), "se")
    p.wait("gpsimd", "se", p.c["se"])
    p.emit("gpsimd", _mk_dyn)
    p.wait("vector", "se", p.c["se"])
    p.op("vector", lambda E: E.tensor_scalar_mul(sb_aneg, sb_aneg, -1.0), "ve")

    x_r = d_x.rearrange("(i p) d -> p i d", p=128)
    ld_xgrp = []
    ve_ln_done = [0] * 4
    for grp in range(4):
        if grp >= 2:
            p.wait("sync", "ve", ve_ln_done[grp - 2])
        ld_xgrp.append(p.dma("sync", sb_xt[grp % 2].rearrange("p (i d) -> p i d", d=D),
                             x_r[:, 4 * grp:4 * (grp + 1), :], f"l{grp % 2}"))
        for sl in range(4):
            i = 4 * grp + sl
            xt = sb_xt[grp % 2][:, D * sl:D * (sl + 1)]
            p.wait("vector", f"l{grp % 2}", ld_xgrp[grp])
            p.op("vector", lambda E, xt=xt, i=i: E.reduce_sum(sb_ln1[:, i:i + 1], xt, AX.X))
            p.wait("scalar", f"l{grp % 2}", ld_xgrp[grp])
            p.op("scalar", lambda E, xt=xt: E.square(sb_x2, xt), "se")
            p.wait("vector", "se", p.c["se"])
            p.op("vector", lambda E, i=i: E.reduce_sum(sb_ln2[:, i:i + 1], sb_x2, AX.X))
            p.op("vector", lambda E, i=i: E.tensor_scalar_mul(sb_mu[:, i:i + 1], sb_ln1[:, i:i + 1], 1.0 / D))
            p.op("vector", lambda E, i=i: E.tensor_scalar_mul(sb_ln2[:, i:i + 1], sb_ln2[:, i:i + 1], 1.0 / D), "ve")
            p.wait("scalar", "ve", p.c["ve"])
            p.op("scalar", lambda E, i=i: E.square(sb_tmp1[:, 0:1], sb_mu[:, i:i + 1]), "se")
            p.wait("vector", "se", p.c["se"])
            p.op("vector", lambda E, i=i: E.tensor_sub(sb_ln2[:, i:i + 1], sb_ln2[:, i:i + 1], sb_tmp1[:, 0:1]))
            p.op("vector", lambda E, i=i: E.tensor_scalar_add(sb_ln2[:, i:i + 1], sb_ln2[:, i:i + 1], 1e-5), "ve")
            p.wait("scalar", "ve", p.c["ve"])
            p.op("scalar", lambda E, i=i: E.activation(sb_ln1[:, i:i + 1], sb_ln2[:, i:i + 1], AF.Sqrt), "se")
            p.wait("vector", "se", p.c["se"])
            p.op("vector", lambda E, i=i: E.reciprocal(sb_rstd[:, i:i + 1], sb_ln1[:, i:i + 1]))
            p.wait("vector", "te", p.c["te"])
            p.op("vector", lambda E, xt=xt, i=i: E.tensor_scalar(
                sb_xnf, xt, sb_mu[:, i:i + 1], sb_rstd[:, i:i + 1], ALU.subtract, ALU.mult), "ve")
            ve_xn = p.c["ve"]
            p.wait("scalar", "ve", ve_xn)
            p.op("scalar", lambda E, i=i: E.activation(sb_xntok[:, D * i:D * (i + 1)], sb_xnf, AF.Copy), "se")
            for jb in range(2):
                p.wait("tensor", "se", p.c["se"])
                for jj in range(4):
                    j = 4 * jb + jj
                    pst = ps[jj][:, :128].bitcast(BF16)[:, :128]
                    p.op("tensor", lambda E, i=i, j=j, pst=pst: E.transpose(
                        pst, sb_xntok[:, D * i + 128 * j:D * i + 128 * (j + 1)], sb_idb),
                        "te" if jj == 3 else None)
                p.wait("scalar", "te", p.c["te"])
                for jj in range(4):
                    j = 4 * jb + jj
                    pst = ps[jj][:, :128].bitcast(BF16)[:, :128]
                    p.op("scalar", lambda E, i=i, j=j, pst=pst: E.activation(
                        sb_xnd[:, 2048 * j + 128 * i:2048 * j + 128 * (i + 1)], pst, AF.Copy), "se")
        ve_ln_done[grp] = p.c["ve"]
    se_xn_all = p.c["se"]

    if tap == "p0":
        p.wait("scalar", "se", p.c["se"])
        p.wait("scalar", "ve", p.c["ve"])
        p.dma("scalar", d_dev[:, 0:8192], sb_xnd.bitcast(F32), "st")
        p.dma("scalar", d_dev[:, 8192:16384], sb_xntok.bitcast(F32), "st")
        p.dma("scalar", d_dev[:, 16384:16400], sb_mu, "st")
        p.dma("scalar", d_dev[:, 16400:16416], sb_rstd, "st")
        return _finish(nc, stack, p)

    p.wait("sync", "ve", p.c["ve"])
    for wname, src, dst in (("win", d_win, sb_winb), ("phi", d_phi, sb_phib)):
        for half in range(2):
            p.wait("sync", "se", p.c["se"])
            ldw = p.dma("sync", sb_wstage.rearrange("p (j e) -> p j e", j=4),
                        src.rearrange("(j p) e -> p j e", p=128)[:, 4 * half:4 * (half + 1), :], "ld")
            p.wait("scalar", "ld", ldw)
            for jj in range(4):
                j = 4 * half + jj
                p.op("scalar", lambda E, jj=jj, j=j, dst=dst: E.activation(
                    dst[:, 2048 * j:2048 * (j + 1)], sb_wstage[:, 2048 * jj:2048 * (jj + 1)], AF.Copy), "se")

    p.wait("tensor", "se", p.c["se"])
    for j in range(2):
        p.op("tensor", lambda E, j=j: E.matmul(
            ps[3], sb_xnd[:, 2048 * j:2048 * j + 128],
            sb_phib[:, 2048 * j:2048 * j + 512],
            start=(j == 0), stop=(j == 1)), "te" if j == 1 else None)
    p.wait("vector", "te", p.c["te"])
    p.op("vector", lambda E: E.tensor_copy(sb_tmp1[:, 0:1], ps[3][:, 0:1]), "ve")

    last_store = {}

    def guarded_store(tag, bufs, idx):
        key = (tag, idx % len(bufs))
        if key in last_store:
            p.wait("scalar", "sp", last_store[key])

    def wait_slot_stores(eng, sem):
        p.wait(eng, sem, p.c[sem])

    se_h1 = [p.c["se"], p.c["se"]]
    for i in range(MT):
        for q in range(TQ):
            pq = ps[q % 2]
            p.wait("tensor", "se", se_h1[-2])
            for j in range(G):
                p.op("tensor", lambda E, i=i, j=j, q=q, pq=pq: E.matmul(
                    pq, sb_xnd[:, 2048 * j + 128 * i:2048 * j + 128 * (i + 1)],
                    sb_phib[:, 2048 * j + 512 * q:2048 * j + 512 * (q + 1)],
                    start=(j == 0), stop=(j == G - 1)), "te" if j == G - 1 else None)
            p.wait("scalar", "te", p.c["te"])
            wait_slot_stores("scalar", f"s{q % 2}")
            eb, lb = sb_Et[q % 2], sb_Lt[q % 2]
            p.op("scalar", lambda E, pq=pq, eb=eb: E.activation(eb, pq, AF.Exp), "se")
            p.op("scalar", lambda E, pq=pq, lb=lb: E.activation(lb, pq, AF.Copy), "se")
            p.wait("scalar", "se", p.c["se"])
            p.dma("scalar", d_E[128 * i:128 * (i + 1), 512 * q:512 * (q + 1)], eb, f"s{q % 2}")
            p.dma("scalar", d_logits[128 * i:128 * (i + 1), 512 * q:512 * (q + 1)], lb, f"s{q % 2}")
            se_h1.append(p.c["se"])
            if tap == "p1" and i == 0 and q == 0:
                p.op("scalar", lambda E, pq=pq: E.activation(sb_sev.bitcast(F32)[:, 0:512], pq, AF.Copy), "se")
                p.dma("scalar", d_dev[:, 66048:66560], sb_sev.bitcast(F32)[:, 0:512], "st")
                p.dma("scalar", d_dev[:, 66560:67072], lb, "st")
    p.wait("gpsimd", "s0", p.c["s0"])
    p.wait("gpsimd", "s1", p.c["s1"])
    p.op("gpsimd", lambda E: E.collective_compute(
        "AllReduce", ALU.add, replica_groups=[list(range(8))],
        ins=[d_E.opt()], outs=[d_den.opt()]), "cc")
    cc_ar = p.c["cc"]

    if tap == "p1":
        p.wait("scalar", "sp", p.c["sp"])
        p.dma("scalar", d_dev[:, 0:32768].rearrange("p (i q) -> p i q", i=16), d_E.rearrange("(i p) q -> p i q", p=128), "st")
        p.dma("scalar", d_dev[:, 32768:65536].rearrange("p (i q) -> p i q", i=16), d_logits.rearrange("(i p) q -> p i q", p=128), "st")
        p.wait("tensor", "sp", p.c["sp"])
        for j in range(G):
            p.op("tensor", lambda E, j=j: E.matmul(
                ps[0], sb_xnd[:, 2048 * j:2048 * j + 128],
                sb_phib[:, 2048 * j:2048 * j + 512],
                start=(j == 0), stop=(j == G - 1)), "te" if j == G - 1 else None)
        p.wait("scalar", "te", p.c["te"])
        p.op("scalar", lambda E: E.activation(sb_Lt[0], ps[0], AF.Copy), "se")
        p.dma("scalar", d_dev[:, 65536:66048], sb_Lt[0], "st")
        return _finish(nc, stack, p)

    se_h2 = [p.c["se"], p.c["se"]]
    for g in range(16):
        for q in range(TQ):
            pq = ps[q % 2]
            p.wait("tensor", "se", se_h2[-2])
            for j in range(G):
                p.op("tensor", lambda E, g=g, j=j, q=q, pq=pq: E.matmul(
                    pq, sb_winb[:, 2048 * j + 128 * g:2048 * j + 128 * (g + 1)],
                    sb_xnd[:, 2048 * j + 512 * q:2048 * j + 512 * (q + 1)],
                    start=(j == 0), stop=(j == G - 1)), "te" if j == G - 1 else None)
            p.wait("scalar", "te", p.c["te"])
            if g < 8:
                if q == 0:
                    p.wait("scalar", "ve", p.c["ve"])
                p.op("scalar", lambda E, q=q, pq=pq: E.activation(
                    sb_xinp[:, 16 + 512 * q:16 + 512 * (q + 1)], pq, AF.Copy), "se")
                se_h2.append(p.c["se"])
            else:
                if q == 0:
                    wait_slot_stores("scalar", "sp")
                p.op("scalar", lambda E, q=q, pq=pq: E.activation(
                    sb_ct0[:, 512 * q:512 * (q + 1)], pq, AF.Silu), "se")
                se_h2.append(p.c["se"])
        if g < 8:
            p.wait("vector", "se", p.c["se"])
            if g == 0:
                p.op("vector", lambda E: E.memset(sb_xinp[:, 0:16], 0.0))
            x0, x1 = sb_xinp[:, 13:13 + T], sb_xinp[:, 14:14 + T]
            x2, x3 = sb_xinp[:, 15:15 + T], sb_xinp[:, 16:16 + T]
            p.wait("vector", "te", p.c["te"])
            p.op("vector", lambda E, g=g, x0=x0: E.tensor_scalar_mul(sb_ct0, x0, sb_convw[:, 4 * g:4 * g + 1]))
            p.op("vector", lambda E, g=g, x1=x1: E.scalar_tensor_tensor(
                sb_ct1, x1, sb_convw[:, 4 * g + 1:4 * g + 2], sb_ct0, ALU.mult, ALU.add))
            p.op("vector", lambda E, g=g, x2=x2: E.scalar_tensor_tensor(
                sb_ct0, x2, sb_convw[:, 4 * g + 2:4 * g + 3], sb_ct1, ALU.mult, ALU.add))
            p.op("vector", lambda E, g=g, x3=x3: E.scalar_tensor_tensor(
                sb_ct1, x3, sb_convw[:, 4 * g + 3:4 * g + 4], sb_ct0, ALU.mult, ALU.add), "ve")
            p.wait("scalar", "ve", p.c["ve"])
            wait_slot_stores("scalar", "sp")
            p.wait("scalar", "te", p.c["te"])
            p.op("scalar", lambda E, g=g: E.activation(sb_xcg, sb_ct1, AF.Silu, bias=sb_convb[:, g:g + 1]), "se")
            se_xcv = p.c["se"]
            p.wait("scalar", "se", se_xcv)
            p.dma("scalar", d_xconv[g], sb_xcg, "sp")
            p.wait("tensor", "se", se_xcv)
            for q in range(TQ):
                p.op("tensor", lambda E, g=g, q=q: E.matmul(
                    ps_big[0:33, 512 * q:512 * (q + 1)], sb_xpj[:, 33 * g:33 * (g + 1)],
                    sb_xcg[:, 512 * q:512 * (q + 1)],
                    start=(g == 0), stop=(g == 7)), "te" if q == TQ - 1 else None)
        else:
            p.wait("scalar", "se", p.c["se"])
            p.dma("scalar", d_sz[g - 8], sb_ct0, "sp")
    p.wait("scalar", "te", p.c["te"])
    p.op("scalar", lambda E: E.activation(sb_ssm[0:33, :], ps_big[0:33, :], AF.Copy), "se")
    se_ssm = p.c["se"]
    p.wait("scalar", "se", se_ssm)
    p.dma("scalar", d_ssm, sb_ssm[0:33, :], "sp")
    sp_ssm = p.c["sp"]

    if tap == "p2":
        p.wait("scalar", "sp", p.c["sp"])
        p.dma("scalar", d_dev[:, 0:16384].rearrange("p (g t) -> p g t", g=8), d_xconv.rearrange("g p t -> p g t"), "st")
        p.dma("scalar", d_dev[0:33, 16384:18432], d_ssm, "st")
        p.dma("scalar", d_dev[:, 18432:34816].rearrange("p (g t) -> p g t", g=8), d_sz.rearrange("g p t -> p g t"), "st")
        return _finish(nc, stack, p)

    p.wait("sync", "sp", sp_ssm)
    ld_sbc = p.dma("sync", sb_sbc, d_ssm[32:33, :].partition_broadcast(128), "ld")
    p.wait("sync", "se", p.c["se"])
    ld_wo = p.dma("sync", sb_wfs.rearrange("p (j e) -> p j e", j=8),
                  d_wout.rearrange("(j p) e -> p j e", p=128), "ld")
    p.wait("scalar", "ld", ld_wo)
    for j in range(G):
        p.op("scalar", lambda E, j=j: E.activation(
            sb_woutb[:, 1024 * j:1024 * (j + 1)], sb_wfs[:, 1024 * j:1024 * (j + 1)], AF.Copy), "se")
    se_wout = p.c["se"]
    p.wait("scalar", "ld", ld_wo)
    p.wait("vector", "ld", ld_wo)
    p.wait("vector", "sp", p.c["sp"])
    for g in range(G):
        wait_slot_stores("scalar", "sp")
        p.wait("vector", "sp", p.c["sp"])
        p.op("vector", lambda E, g=g: E.tensor_scalar(
            sb_ct0, sb_sbc, sb_dtw[:, g:g + 1], sb_dtb[:, g:g + 1], ALU.mult, ALU.add), "ve")
        p.wait("scalar", "ve", p.c["ve"])
        p.op("scalar", lambda E: E.activation(sb_ct1, sb_ct0, AF.Abs))
        p.op("scalar", lambda E: E.activation(sb_ct1, sb_ct1, AF.Exp, scale=-1.0))
        p.op("scalar", lambda E: E.activation(sb_ct1, sb_ct1, AF.Ln, bias=1.0))
        p.op("scalar", lambda E: E.activation(sb_ct0, sb_ct0, AF.Relu), "se")
        p.wait("vector", "se", p.c["se"])
        p.op("vector", lambda E: E.tensor_add(sb_ct0, sb_ct0, sb_ct1), "ve")
        p.wait("scalar", "ve", p.c["ve"])
        p.dma("scalar", d_dt[g], sb_ct0, "sp")

    if tap == "p3":
        p.wait("scalar", "sp", p.c["sp"])
        p.dma("scalar", d_dev[:, 0:16384].rearrange("p (g t) -> p g t", g=8), d_dt.rearrange("g p t -> p g t"), "st")
        return _finish(nc, stack, p)

    p.op("vector", lambda E: E.memset(sb_hcar, 0.0), "ve")
    for c in range(NCH):
        c0 = TC * c
        p.wait("sync", "ve", p.c["ve"])
        p.wait("sync", "ge", p.c["ge"])
        p.wait("sync", "sp", p.c["sp"])
        bsrc = d_ssm[0:16, c0:c0 + TC].unsqueeze(0).broadcast_to([128, 16, TC])
        csrc = d_ssm[16:32, c0:c0 + TC].unsqueeze(0).broadcast_to([128, 16, TC])
        lsem = f"l{c % 2}"
        p.dma("sync", sb_Bbc.rearrange("p (n t) -> p n t", n=16), bsrc, lsem)
        p.dma("sync", sb_Cbc.rearrange("p (n t) -> p n t", n=16), csrc, lsem)
        p.dma("sync", sb_xcs.rearrange("p (g t) -> p g t", g=8),
              d_xconv[:, :, c0:c0 + TC].rearrange("g p t -> p g t"), lsem)
        p.dma("sync", sb_dts.rearrange("p (g t) -> p g t", g=8),
              d_dt[:, :, c0:c0 + TC].rearrange("g p t -> p g t"), lsem)
        p.dma("sync", sb_szs.rearrange("p (g t) -> p g t", g=8),
              d_sz[:, :, c0:c0 + TC].rearrange("g p t -> p g t"), lsem)
        ld_grp = p.dma("sync", sb_xres.rearrange("p (i d) -> p i d", i=2),
                       d_x.rearrange("(i p) d -> p i d", p=128)[:, 2 * c:2 * c + 2, :], lsem)
        p.wait("vector", "te", p.c["te"])
        for g in range(G):
            dts = sb_dts[:, TC * g:TC * (g + 1)]
            xcs = sb_xcs[:, TC * g:TC * (g + 1)]
            p.wait("scalar", lsem, ld_grp)
            p.wait("scalar", "ve", p.c["ve"])
            for n in range(NSS):
                p.op("scalar", lambda E, g=g, n=n, dts=dts: E.activation(
                    sb_dA[:, TC * n:TC * (n + 1)], dts, AF.Exp,
                    scale=sb_aneg[:, 16 * g + n:16 * g + n + 1]),
                    "se" if n == NSS - 1 else None)
            p.op("scalar", lambda E, g=g, dts=dts: E.activation(
                sb_e0, sb_aneg[:, 16 * g:16 * (g + 1)], AF.Exp, scale=dts[:, 0:1]), "se")
            se_e0 = p.c["se"]
            p.wait("vector", lsem, ld_grp)
            p.wait("vector", "ge", p.c["ge"])
            p.op("vector", lambda E, dts=dts, xcs=xcs: E.tensor_mul(sb_ug, dts, xcs), "ve")
            ve_u = p.c["ve"]
            p.wait("gpsimd", lsem, ld_grp)
            p.wait("gpsimd", "ve", ve_u)
            u3 = sb_ug.unsqueeze(1).broadcast_to([128, 16, TC])
            p.op("gpsimd", lambda E, u3=u3: E.tensor_tensor(
                sb_dBx.rearrange("p (n t) -> p n t", n=16),
                sb_Bbc.rearrange("p (n t) -> p n t", n=16), u3, ALU.mult), "ge")
            ge_dbx = p.c["ge"]
            dA3 = sb_dA.rearrange("p (n t) -> p n t", n=16)
            dBx3 = sb_dBx.rearrange("p (n t) -> p n t", n=16)
            p.wait("vector", "se", se_e0)
            p.op("vector", lambda E, g=g: E.tensor_mul(sb_fixv, sb_e0, sb_hcar[:, 16 * g:16 * (g + 1)]))
            p.wait("vector", "ge", ge_dbx)
            p.op("vector", lambda E, dBx3=dBx3: E.tensor_add(dBx3[:, :, 0], dBx3[:, :, 0], sb_fixv))
            p.op("vector", lambda E, dA3=dA3: E.memset(dA3[:, :, 0], 0.0))
            p.op("vector", lambda E: E.tensor_tensor_scan(sb_h, sb_dA, sb_dBx, 0.0, ALU.mult, ALU.add))
            h3 = sb_h.rearrange("p (n t) -> p n t", n=16)
            p.op("vector", lambda E, g=g, h3=h3: E.tensor_copy(sb_hcar[:, 16 * g:16 * (g + 1)], h3[:, :, TC - 1]), "ve")
            ve_scan = p.c["ve"]
            p.wait("gpsimd", "ve", ve_scan)
            p.op("gpsimd", lambda E: E.tensor_tensor(sb_dBx, sb_h, sb_Cbc, ALU.mult), "ge")
            ge_hc = p.c["ge"]
            p.wait("vector", "ge", ge_hc)
            hc3 = sb_dBx.rearrange("p (n t) -> p t n", n=16)
            p.op("vector", lambda E, hc3=hc3: E.tensor_reduce(sb_yg, hc3, AX.X, ALU.add))
            ygslot = sb_ygt[:, 2048 * (c % 2) + 256 * g:2048 * (c % 2) + 256 * (g + 1)]
            p.op("vector", lambda E, g=g, ygslot=ygslot: E.tensor_mul(
                ygslot, sb_yg, sb_szs[:, TC * g:TC * (g + 1)]), "ve")
        ve_chunk = p.c["ve"]
        p.wait("tensor", "ve", ve_chunk)
        if c == 0:
            p.wait("tensor", "se", se_wout)
        for ti in range(2):
            for eh in range(2):
                pq = ps[2 * ti + eh]
                for g in range(G):
                    stat = sb_ygt[:, 2048 * (c % 2) + 256 * g + 128 * ti:
                                  2048 * (c % 2) + 256 * g + 128 * (ti + 1)]
                    p.op("tensor", lambda E, stat=stat, g=g, eh=eh, pq=pq: E.matmul(
                        pq, stat, sb_woutb[:, 1024 * g + 512 * eh:1024 * g + 512 * (eh + 1)],
                        start=(g == 0), stop=(g == G - 1)), "te" if g == G - 1 else None)
        te_op = p.c["te"]
        p.wait("vector", "te", te_op)
        p.wait("vector", lsem, ld_grp)
        p.wait("vector", f"s{(c - 1) % 2}", p.c[f"s{(c - 1) % 2}"] if c > 0 else 0)
        for ti in range(2):
            for eh in range(2):
                pq = ps[2 * ti + eh]
                dst = sb_mact[ti][:, 512 * eh:512 * (eh + 1)]
                p.op("vector", lambda E, pq=pq, dst=dst, ti=ti, eh=eh: E.scalar_tensor_tensor(
                    dst, pq, 1.0, sb_xres[:, 1024 * ti + 512 * eh:1024 * ti + 512 * (eh + 1)],
                    ALU.mult, ALU.add), "ve")
        p.wait("scalar", "ve", p.c["ve"])
        wait_slot_stores("scalar", f"s{c % 2}")
        for ti in range(2):
            p.dma("scalar", d_macc[128 * (2 * c + ti):128 * (2 * c + ti + 1), :],
                  sb_mact[ti], f"s{c % 2}")

    if tap == "p4":
        p.wait("scalar", "sp", p.c["sp"])
        p.dma("scalar", d_dev[:, 0:16384].rearrange("p (i d) -> p i d", i=16), d_macc.rearrange("(i p) d -> p i d", p=128), "st")
        return _finish(nc, stack, p)

    p.wait("sync", "cc", cc_ar)
    p.wait("sync", "te", p.c["te"])
    for q in range(TQ):
        for i in range(MT):
            lb, db, dspb = sb_lgt[i % 2], sb_dnt[i % 2], sb_dsp[i % 2]
            if i >= 2 or q > 0:
                p.wait("sync", "se", p.c["se"])
            p.dma("sync", lb, d_logits[128 * i:128 * (i + 1), 512 * q:512 * (q + 1)], f"l{i % 2}")
            ld_lq = p.dma("sync", db, d_den[128 * i:128 * (i + 1), 512 * q:512 * (q + 1)], f"l{i % 2}")
            p.wait("scalar", f"l{i % 2}", ld_lq)
            p.op("scalar", lambda E, db=db: E.activation(db, db, AF.Ln), "se")
            p.wait("vector", "se", p.c["se"])
            p.op("vector", lambda E, lb=lb, db=db: E.tensor_sub(lb, lb, db), "ve")
            p.wait("scalar", "ve", p.c["ve"])
            p.wait("scalar", "te", p.c["te"])
            p.op("scalar", lambda E, lb=lb, dspb=dspb: E.activation(dspb, lb, AF.Exp), "se")
            se_dsp = p.c["se"]
            if tap == "p5q0" and q == 0:
                p.wait("scalar", "se", se_dsp)
                p.dma("scalar", d_dev[:, 47104 + 256 * i:47104 + 256 * (i + 1)], dspb.bitcast(F32), "st")
                p.dma("scalar", d_dev[:, 51200 + 512 * i:51200 + 512 * (i + 1)], db, "st")
                p.dma("scalar", d_dev[:, 59392 + 512 * i:59392 + 512 * (i + 1)], lb, "st")
            p.wait("tensor", "se", se_dsp)
            if i == 0:
                p.wait("tensor", "ve", p.c["ve"])
            for j in range(G):
                p.op("tensor", lambda E, i=i, j=j, dspb=dspb: E.matmul(
                    ps8[j], sb_xntok[:, 1024 * i + 128 * j:1024 * i + 128 * (j + 1)],
                    dspb, start=(i == 0), stop=(i == MT - 1)), "te" if j == G - 1 else None)
        te_sl = p.c["te"]
        p.wait("scalar", "te", te_sl)
        wait_slot_stores("scalar", "sp")
        for j in range(G):
            p.op("scalar", lambda E, j=j: E.activation(sb_sev[:, 512 * j:512 * (j + 1)], ps8[j], AF.Copy), "se")
        p.wait("scalar", "se", p.c["se"])
        sev3 = sb_sev.rearrange("p (j t) -> p j t", j=8)
        for half in range(2):
            r = 2 * q + half
            p.dma("scalar", d_ssnd[r].rearrange("(j p) t -> p j t", p=128),
                  sev3[:, :, 256 * half:256 * (half + 1)], "sp")
    sp_slots = p.c["sp"]
    p.wait("gpsimd", "sp", sp_slots)
    p.op("gpsimd", lambda E: E.collective_compute(
        "AllGather", ALU.bypass, replica_groups=[list(range(8))],
        ins=[d_ssnd.opt()], outs=[d_sag.opt()]), "cc2")
    cc_s = p.c["cc2"]

    if tap == "p5":
        p.wait("scalar", "cc2", cc_s)
        p.dma("scalar", d_dev[:, 0:8192], d_sag.bitcast(F32)[0, 0][0:128, :].unsqueeze(1).broadcast_to([128, 64, 128]), "st")
        return _finish(nc, stack, p)

    p.wait("gpsimd", "cc2", cc_s)
    srcv5 = sb_srcv.rearrange("p (j n b q) -> p j n b q", j=8, n=8, b=8)
    stg4 = sb_stg.rearrange("p (b j t) -> p b j t", b=8, j=8)
    for b8 in range(8):
        p.c["gl"] += 16
        ld_sr = p.c["gl"]
        p.emit("gpsimd", lambda E, b8=b8: E.dma_start(
            out=stg4[:, b8], in_=dyn_sr[0][b8], bounds_check=None).then_inc(p.sems["gl"], 16))
    p.wait("vector", "gl", ld_sr)
    stg_r = sb_stg.rearrange("p (b j n q) -> p j n b q", b=8, j=8, n=8)
    for j in range(G):
        p.op("vector", lambda E, j=j: E.tensor_copy(srcv5[:, j], stg_r[:, j]),
             "ve" if j == G - 1 else None)
    ve_srcv = p.c["ve"]
    if tap == "p6a":
        p.wait("scalar", "gl", ld_sr)
        p.dma("scalar", d_dev[:, 0:8192], sb_srcv.bitcast(F32), "st")
        return _finish(nc, stack, p)

    for n in range(8):
        p.wait("sync", "se", p.c["se"])
        ld_w = p.dma("sync", sb_wfs.rearrange("p (j e) -> p j e", j=8),
                     d_wffn[n].rearrange("(j p) e -> p j e", p=128), f"l{n % 2}")
        p.wait("scalar", f"l{n % 2}", ld_w)
        for j in range(G):
            p.op("scalar", lambda E, j=j: E.activation(
                sb_wfb[:, 1024 * j:1024 * (j + 1)], sb_wfs[:, 1024 * j:1024 * (j + 1)], AF.Copy), "se")
        se_wf = p.c["se"]
        p.wait("sync", "ve", p.c["ve"])
        ld_bb = p.dma("sync", sb_bbc, d_bffn[n:n + 1, :].partition_broadcast(128), "ld")
        p.wait("tensor", "se", se_wf)
        p.wait("tensor", "ld", ld_bb)
        p.wait("tensor", "ve", ve_srcv)
        p.wait("tensor", "ve", p.c["ve"])
        for rh in range(2):
            for eh in range(2):
                pq = ps[2 * rh + eh]
                for j in range(G):
                    stat = sb_srcv[:, (j * 8 + n) * 256 + 128 * rh:
                                   (j * 8 + n) * 256 + 128 * (rh + 1)]
                    p.op("tensor", lambda E, stat=stat, j=j, eh=eh, pq=pq: E.matmul(
                        pq, stat, sb_wfb[:, 1024 * j + 512 * eh:1024 * j + 512 * (eh + 1)],
                        start=(j == 0), stop=(j == G - 1)), "te" if j == G - 1 else None)
        te_y = p.c["te"]
        p.wait("vector", "te", te_y)
        p.wait("vector", "ld", ld_bb)
        for rh in range(2):
            for eh in range(2):
                pq = ps[2 * rh + eh]
                p.wait("vector", "se", p.c["se"])
                p.op("vector", lambda E, pq=pq, rh=rh, eh=eh: E.scalar_tensor_tensor(
                    sb_yev[rh], pq, 1.0, sb_bbc[:, 512 * eh:512 * (eh + 1)], ALU.mult, ALU.add), "ve")
                p.wait("scalar", "ve", p.c["ve"])
                wait_slot_stores("scalar", "sp")
                p.op("scalar", lambda E, rh=rh: E.activation(sb_ybf, sb_yev[rh], AF.Copy), "se")
                p.wait("scalar", "se", p.c["se"])
                for bb in range(4):
                    b = 4 * rh + bb
                    p.dma("scalar", d_ysnd[b, 32 * n:32 * (n + 1), 512 * eh:512 * (eh + 1)],
                          sb_ybf[32 * bb:32 * (bb + 1), :], "sp")
    sp_y = p.c["sp"]
    p.wait("gpsimd", "sp", sp_y)
    p.op("gpsimd", lambda E: E.collective_compute(
        "AllGather", ALU.bypass, replica_groups=[list(range(8))],
        ins=[d_ysnd.opt()], outs=[d_yag.opt()]), "cc3")
    cc_y = p.c["cc3"]

    if tap == "p6":
        p.wait("scalar", "cc3", cc_y)
        p.dma("scalar", d_dev[:, 0:1024].rearrange("p (k e) -> p k e", k=2),
              d_yag.bitcast(F32)[0, 0].rearrange("(k p) e -> p k e", p=128), "st")
        return _finish(nc, stack, p)

    p.wait("gpsimd", "cc3", cc_y)
    yb4 = sb_yb.rearrange("p (r k e) -> p r k e", r=8, k=2)
    for r in range(8):
        p.c["gl"] += 16
        ld_yb = p.c["gl"]
        p.emit("gpsimd", lambda E, r=r: E.dma_start(
            out=yb4[:, r], in_=dyn_y[0][r], bounds_check=None).then_inc(p.sems["gl"], 16))

    if tap == "p7":
        p.wait("scalar", "ve", p.c["ve"])
        p.wait("scalar", "gl", ld_yb)
        p.dma("scalar", d_dev[:, 0:16], sb_rs, "st")
        p.dma("scalar", d_dev[:, 16:32], sb_rrs, "st")
        p.dma("scalar", d_dev[:, 1024:9216], sb_yb.bitcast(F32), "st")
    p.wait("sync", "sp", p.c["sp"])
    p.wait("sync", "s0", p.c["s0"])
    p.wait("sync", "s1", p.c["s1"])
    for i in range(MT):
        p.wait("sync", "se", p.c["se"])
        p.wait("sync", "ve", p.c["ve"])
        p.dma("sync", sb_Ei, d_logits[128 * i:128 * (i + 1), :], f"l{i % 2}")
        ld_m = p.dma("sync", sb_mact[i % 2], d_macc[128 * i:128 * (i + 1), :], f"l{i % 2}")
        p.wait("scalar", f"l{i % 2}", ld_m)
        p.wait("scalar", "ve", p.c["ve"])
        p.op("scalar", lambda E: E.activation(sb_Ei, sb_Ei, AF.Exp), "se")
        p.wait("vector", "se", p.c["se"])
        p.wait("vector", "te", p.c["te"])
        p.op("vector", lambda E, i=i: E.reduce_sum(sb_rs[:, i:i + 1], sb_Ei, AX.X), "ve")
        p.wait("scalar", "ve", p.c["ve"])
        p.op("scalar", lambda E, i=i: E.activation(sb_rrs[:, i:i + 1], sb_rs[:, i:i + 1], AF.Ln, scale=1.0), "se")
        p.op("scalar", lambda E: E.activation(sb_cmbf, sb_Ei, AF.Ln), "se")
        p.wait("vector", "se", p.c["se"])
        p.op("vector", lambda E, i=i: E.tensor_scalar_mul(sb_rrs[:, i:i + 1], sb_rrs[:, i:i + 1], -1.0), "ve")
        p.wait("scalar", "ve", p.c["ve"])
        p.op("scalar", lambda E, i=i: E.activation(sb_cmbf, sb_cmbf, AF.Exp, bias=sb_rrs[:, i:i + 1]), "se")
        p.op("scalar", lambda E: E.engine_nop(), "se") if False else None
        p.wait("vector", "se", p.c["se"])
        p.op("vector", lambda E: E.tensor_copy(sb_tmp1[:, 0:1], sb_cmbf[:, 0:1]), "ve")
        ve_cmb = p.c["ve"]
        if tap == "p7" and i == 14:
            p.wait("scalar", "ve", ve_cmb)
            p.dma("scalar", d_dev[:, 9216:11264], sb_cmbf, "st")
            p.dma("scalar", d_dev[:, 11264:11280], sb_rs, "st")
            p.dma("scalar", d_dev[:, 11280:11296], sb_rrs, "st")
            p.dma("scalar", d_dev[:, 11296:13344], sb_Ei, "st")
        p.wait("tensor", "ve", ve_cmb)
        for kb in range(4):
            p.wait("tensor", "se", p.c["se"])
            for kk in range(4):
                k = 4 * kb + kk
                p.op("tensor", lambda E, k=k, kk=kk: E.transpose(
                    ps[kk][:, 0:128], sb_cmbf[:, 128 * k:128 * (k + 1)], sb_idf),
                    "te" if kk == 3 else None)
            p.wait("scalar", "te", p.c["te"])
            for kk in range(4):
                k = 4 * kb + kk
                p.op("scalar", lambda E, k=k, kk=kk: E.activation(
                    sb_cnp[:, 128 * k:128 * (k + 1)], ps[kk][:, 0:128], AF.Copy), "se")
        se_cnp = p.c["se"]
        p.wait("tensor", "se", se_cnp)
        p.wait("tensor", "gl", ld_yb)
        p.wait("tensor", "ve", p.c["ve"])
        for eh in range(2):
            pq = ps[eh]
            for k in range(16):
                p.op("tensor", lambda E, k=k, eh=eh, pq=pq: E.matmul(
                    pq, sb_cnp[:, 128 * k:128 * (k + 1)],
                    sb_yb[:, 1024 * k + 512 * eh:1024 * k + 512 * (eh + 1)],
                    start=(k == 0), stop=(k == 15)), "te" if k == 15 else None)
        te_out = p.c["te"]
        p.wait("vector", "te", te_out)
        p.wait("vector", f"l{i % 2}", ld_m)
        for eh in range(2):
            p.op("vector", lambda E, i=i, eh=eh: E.scalar_tensor_tensor(
                sb_outt[i % 2][:, 512 * eh:512 * (eh + 1)], ps[eh], 1.0,
                sb_mact[i % 2][:, 512 * eh:512 * (eh + 1)], ALU.mult, ALU.add), "ve")
        p.wait("scalar", "ve", p.c["ve"])
        wait_slot_stores("scalar", f"s{i % 2}")
        p.dma("scalar", d_out[128 * i:128 * (i + 1), :], sb_outt[i % 2], f"s{i % 2}")

    if tap in ("post", "p5q0"):
        p.wait("scalar", "st", p.c["st"])
        p.dma("scalar", d_dev[:, 0:8192], sb_yb.bitcast(F32), "st")
        for r in range(4):
            p.dma("scalar",
                  d_dev[:, 8192 + 1024 * r:8192 + 1024 * (r + 1)].rearrange("p (k e) -> p k e", k=2),
                  d_yag.bitcast(F32)[r, 0].rearrange("(k p) e -> p k e", p=128), "st")
        p.dma("scalar", d_dev[:, 12288:13312], d_macc[0:128, :], "st")
        p.dma("scalar", d_dev[:, 13312:15360], sb_cmbf, "st")
        p.dma("scalar", d_dev[:, 15360:15376], sb_rs, "st")
        p.dma("scalar", d_dev[:, 15376:15392], sb_rrs, "st")
        p.dma("scalar", d_dev[:, 15392:15904], d_ysnd.bitcast(F32)[0, 0:128, :], "st")
        for r in range(8):
            p.dma("scalar", d_dev[:, 36864 + 1024 * r:36864 + 1024 * (r + 1)].rearrange("p (j t) -> p j t", j=8),
                  d_ssnd.bitcast(F32)[r].rearrange("(j p) t -> p j t", p=128), "st")
        p.dma("scalar", d_dev[:, 45056:45568], d_den[0:128, 0:512], "st")
        p.dma("scalar", d_dev[:, 45568:46080], d_den[1920:2048, 0:512], "st")
        p.dma("scalar", d_dev[:, 46080:46592], d_logits[0:128, 0:512], "st")
        for s in range(8):
            p.dma("scalar", d_dev[:, 24576 + 1024 * s:24576 + 1024 * (s + 1)].rearrange("p (j t) -> p j t", j=8),
                  d_sag.bitcast(F32)[s, 0].rearrange("(j p) t -> p j t", p=128), "st")
        p.dma("scalar", d_dev[:, 32768:33792].rearrange("p (j t) -> p j t", j=8),
              d_ssnd.bitcast(F32)[0].rearrange("(j p) t -> p j t", p=128), "st")
        p.dma("scalar", d_dev[:, 33792:34816], d_den[0:128, 1024:2048], "st")
        p.dma("scalar", d_dev[:, 34816:35840], d_logits[0:128, 1024:2048], "st")

    return _finish(nc, stack, p)


def _finish(nc, stack, p):
    p.wait("sync", "st", p.c["st"])
    p.wait("sync", "sp", p.c["sp"])
    p.wait("sync", "s0", p.c["s0"])
    p.wait("sync", "s1", p.c["s1"])
    with nc.Block() as block:
        @block.sync
        def _(E):
            for fn in p.ops["sync"]:
                fn(E)

        @block.scalar
        def _(E):
            for fn in p.ops["scalar"]:
                fn(E)

        @block.vector
        def _(E):
            for fn in p.ops["vector"]:
                fn(E)

        @block.tensor
        def _(E):
            for fn in p.ops["tensor"]:
                fn(E)

        @block.gpsimd
        def _(E):
            for fn in p.ops["gpsimd"]:
                fn(E)
    stack.close()
    return nc


def kernel(L0_attn_x, L0_attn_in_proj_w, L0_attn_conv_w, L0_attn_conv_b,
           L0_attn_x_proj_w, L0_attn_dt_proj_w, L0_attn_dt_proj_b,
           L0_attn_A_log, L0_attn_out_proj_w, L0_ffn_phi, L0_ffn_W, L0_ffn_b):
    import ml_dtypes
    x = np.ascontiguousarray(np.asarray(L0_attn_x, dtype=np.float32))
    w_in_t = np.ascontiguousarray(np.asarray(L0_attn_in_proj_w, dtype=np.float32).T)
    w_out_t = np.ascontiguousarray(np.asarray(L0_attn_out_proj_w, dtype=np.float32).T)
    phi = np.ascontiguousarray(np.asarray(L0_ffn_phi, dtype=np.float32).reshape(D, NP))
    xproj_t = np.ascontiguousarray(np.asarray(L0_attn_x_proj_w, dtype=np.float32).T)
    conv_w = np.ascontiguousarray(
        np.asarray(L0_attn_conv_w, dtype=np.float32)[:, 0, :].reshape(8, 128, 4).transpose(1, 0, 2).reshape(128, 32))
    conv_b = np.ascontiguousarray(np.asarray(L0_attn_conv_b, dtype=np.float32).reshape(8, 128).T)
    dt_w = np.ascontiguousarray(np.asarray(L0_attn_dt_proj_w, dtype=np.float32).reshape(8, 128).T)
    dt_b = np.ascontiguousarray(np.asarray(L0_attn_dt_proj_b, dtype=np.float32).reshape(8, 128).T)
    a_log = np.ascontiguousarray(
        np.asarray(L0_attn_A_log, dtype=np.float32).reshape(8, 128, NSS).transpose(1, 0, 2).reshape(128, 128))
    W = np.asarray(L0_ffn_W, dtype=np.float32)
    bffn = np.asarray(L0_ffn_b, dtype=np.float32)
    idf = np.eye(128, dtype=np.float32)
    idb = np.eye(128, dtype=ml_dtypes.bfloat16)

    with allow_ici_all_to_all():
        nc = build(tap=DEV_TAP)
    in_maps = []
    for c in range(8):
        in_maps.append({
            "x": np.ascontiguousarray(x[c]),
            "w_in_t": w_in_t, "w_out_t": w_out_t, "phi": phi,
            "xproj_t": xproj_t, "conv_w": conv_w, "conv_b": conv_b,
            "dt_w": dt_w, "dt_b": dt_b, "a_log": a_log,
            "w_ffn": np.ascontiguousarray(W[8 * c:8 * (c + 1)]),
            "b_ffn": np.ascontiguousarray(bffn[8 * c:8 * (c + 1)]),
            "ident_f": idf, "ident_b": idb,
            "rank": np.array([[c]], dtype=np.uint32),
        })
    with allow_ici_all_to_all():
        res = run_bass_kernel_spmd(nc, in_maps, core_ids=list(range(8)))
    outs = [r["out"] if not isinstance(r["out"], dict) else r["out"] for r in res.results]
    return np.stack([np.asarray(o) for o in outs])


# version 05 (final)
# speedup vs baseline: 1.0875x; 1.0424x over previous
import contextlib
import numpy as np
import concourse.bass as bass
import concourse.mybir as mybir
from concourse.bass_utils import run_bass_kernel_spmd
from concourse.aot_env import allow_ici_all_to_all

F32 = mybir.dt.float32
BF16 = mybir.dt.bfloat16
AF = mybir.ActivationFunctionType
ALU = mybir.AluOpType
AX = mybir.AxisListType

T = 2048
D = 1024
NP = 2048
NSS = 16
G = 8
TQ = 4
TC = 256
NCH = T // TC
MT = 16

DEV_TAP = None

ENGS = ("sync", "scalar", "vector", "tensor", "gpsimd")


class P:

    def __init__(self, nc):
        self.nc = nc
        self.ops = {e: [] for e in ENGS}
        self.c = {}
        self.sems = {}

    def sem(self, stack, name):
        self.sems[name] = stack.enter_context(self.nc.semaphore(name))
        self.c[name] = 0

    def emit(self, eng, fn):
        self.ops[eng].append(fn)

    def wait(self, eng, sname, val=None):
        v = self.c[sname] if val is None else val
        if v <= 0:
            return
        s = self.sems[sname]
        self.emit(eng, lambda E, s=s, v=v: E.wait_ge(s, v))

    def dma(self, eng, out, in_, sname, slow=False):
        self.c[sname] += 16
        s = self.sems[sname]
        self.emit(eng, lambda E, o=out, i=in_, s=s, sl=slow: E.dma_start(
            out=o, in_=i, allow_slow_non_contiguous=sl).then_inc(s, 16))
        return self.c[sname]

    def op(self, eng, fn, sname=None):
        if sname is None:
            self.emit(eng, lambda E, f=fn: f(E))
            return None
        self.c[sname] += 1
        s = self.sems[sname]
        self.emit(eng, lambda E, f=fn, s=s: f(E).then_inc(s, 1))
        return self.c[sname]


def build(tap=None):
    nc = bass.Bass("TRN2", target_bir_lowering=False, debug=False)
    stack = contextlib.ExitStack()

    d_x = nc.dram_tensor("x", [T, D], F32, kind="ExternalInput").ap()
    d_win = nc.dram_tensor("w_in_t", [D, 2 * D], F32, kind="ExternalInput").ap()
    d_wout = nc.dram_tensor("w_out_t", [D, D], F32, kind="ExternalInput").ap()
    d_phi = nc.dram_tensor("phi", [D, NP], F32, kind="ExternalInput").ap()
    d_xproj = nc.dram_tensor("xproj_t", [D, 33], F32, kind="ExternalInput").ap()
    d_wffn = nc.dram_tensor("w_ffn", [8, D, D], F32, kind="ExternalInput").ap()
    d_bffn = nc.dram_tensor("b_ffn", [8, D], F32, kind="ExternalInput").ap()
    d_convw = nc.dram_tensor("conv_w", [128, 32], F32, kind="ExternalInput").ap()
    d_convb = nc.dram_tensor("conv_b", [128, 8], F32, kind="ExternalInput").ap()
    d_dtw = nc.dram_tensor("dt_w", [128, 8], F32, kind="ExternalInput").ap()
    d_dtb = nc.dram_tensor("dt_b", [128, 8], F32, kind="ExternalInput").ap()
    d_alog = nc.dram_tensor("a_log", [128, 128], F32, kind="ExternalInput").ap()
    d_rank = nc.dram_tensor("rank", [1, 1], mybir.dt.uint32, kind="ExternalInput").ap()
    d_idf = nc.dram_tensor("ident_f", [128, 128], F32, kind="ExternalInput").ap()
    d_idb = nc.dram_tensor("ident_b", [128, 128], BF16, kind="ExternalInput").ap()
    d_out = nc.dram_tensor("out", [T, D], F32, kind="ExternalOutput").ap()

    d_fin = nc.dram_tensor("fence_in", [128, 4], F32).ap()
    d_fout = nc.dram_tensor("fence_out", [128, 4], F32).ap()
    d_gA = nc.dram_tensor("guard_A", [T, 512], F32).ap()
    d_E = nc.dram_tensor("E_scr", [T, NP], F32).ap()
    d_gB = nc.dram_tensor("guard_B", [T, 512], F32).ap()
    d_logits = nc.dram_tensor("logits_scr", [T, NP], F32).ap()
    d_den = nc.dram_tensor("den_scr", [T, NP], F32).ap()
    d_gC = nc.dram_tensor("guard_C", [T, 512], F32).ap()
    d_sz = nc.dram_tensor("sz_scr", [G, 128, T], F32).ap()
    d_xconv = nc.dram_tensor("xconv_scr", [G, 128, T], F32).ap()
    d_dt = nc.dram_tensor("dt_scr", [G, 128, T], F32).ap()
    d_ssm = nc.dram_tensor("ssm_scr", [33, T], F32).ap()
    d_macc = nc.dram_tensor("macc_scr", [T, D], F32).ap()
    d_guard0 = nc.dram_tensor("guard0", [8, D, 256], BF16).ap()
    d_ssnd = nc.dram_tensor("slots_snd", [8, D, 256], BF16).ap()
    d_guard1 = nc.dram_tensor("guard1", [8, D, 256], BF16).ap()
    d_sag = nc.dram_tensor("slots_ag", [8, 8, D, 256], BF16).ap()
    d_guard2 = nc.dram_tensor("guard2", [8, 256, D], BF16).ap()
    d_ysnd = nc.dram_tensor("y_snd", [8, 256, D], BF16).ap()
    d_guard3 = nc.dram_tensor("guard3", [8, 256, D], BF16).ap()
    d_yag = nc.dram_tensor("y_ag", [8, 8, 256, D], BF16).ap()
    d_dev = None
    if tap is not None:
        d_dev = nc.dram_tensor("dev", [128, 81920], F32, kind="ExternalOutput").ap()

    NF32 = 53152
    mega = nc.alloc_sbuf_tensor("mega", [128, NF32], F32).ap()
    cursor = [0]

    def f32v(n_elems, at=None):
        off = cursor[0] if at is None else at
        v = mega[:, off:off + n_elems]
        if at is None:
            cursor[0] = off + n_elems
        return v

    def bf16v(n_elems, at=None):
        assert n_elems % 2 == 0
        off = cursor[0] if at is None else at
        v = mega[:, off:off + n_elems // 2].bitcast(BF16)
        if at is None:
            cursor[0] = off + n_elems // 2
        return v

    sb_aneg = f32v(128)
    sb_convw = f32v(32)
    sb_convb = f32v(8)
    sb_dtw = f32v(8)
    sb_dtb = f32v(8)
    sb_idf = f32v(128)
    sb_idb = bf16v(128)
    sb_mu = f32v(16)
    sb_rstd = f32v(16)
    sb_ln1 = f32v(16)
    sb_ln2 = f32v(16)
    sb_rs = f32v(16)
    sb_rrs = f32v(16)
    sb_hcar = f32v(128)
    sb_e0 = f32v(16)
    sb_fixv = f32v(16)
    sb_xpj = f32v(8 * 33)
    sb_bbc = f32v(1024)
    sb_tmp1 = f32v(16)
    sb_rank = f32v(2).bitcast(mybir.dt.uint32)
    assert cursor[0] <= 1952, cursor[0]
    cursor[0] = 1952
    sb_sbc = f32v(2048)
    sb_ssm = f32v(2048)
    sb_xntok = bf16v(16 * 1024)
    BASE_A = cursor[0]
    sb_xnd = bf16v(8 * 2048, at=BASE_A)
    sb_Bbc = f32v(4096, at=BASE_A)
    sb_Cbc = f32v(4096, at=BASE_A + 4096)
    sb_srcv = bf16v(16 * 1024, at=BASE_A)
    BASE_B = BASE_A + 8192
    sb_stg = bf16v(16 * 1024, at=BASE_B)
    sb_winb = bf16v(8 * 2048, at=BASE_B)
    sb_dA = f32v(4096, at=BASE_B)
    sb_dBx = f32v(4096, at=BASE_B + 4096)
    sb_yb = bf16v(16 * 1024, at=BASE_B)
    BASE_C = BASE_B + 8192
    sb_phib = bf16v(8 * 2048, at=BASE_C)
    sb_h = f32v(4096, at=BASE_C)
    sb_xcs = f32v(2048, at=BASE_C + 4096)
    sb_dts = f32v(2048, at=BASE_C + 6144)
    sb_wfs = f32v(8 * 1024, at=BASE_C)
    BASE_D = BASE_C + 8192
    sb_xt = [f32v(4096, at=BASE_D + 4096 * i) for i in range(2)]
    sb_wstage = f32v(8192, at=BASE_D)
    sb_xinp = f32v(2064, at=BASE_D)
    sb_ct0 = f32v(2048, at=BASE_D + 2176)
    sb_ct1 = f32v(2048, at=BASE_D + 4224)
    sb_xcg = f32v(2048, at=BASE_D + 6272)
    sb_szs = f32v(2048, at=BASE_D)
    sb_ygt = bf16v(2 * 2048, at=BASE_D + 2048)
    sb_yg = f32v(256, at=BASE_D + 4096)
    sb_ug = f32v(256, at=BASE_D + 4352)
    sb_xres = f32v(2048, at=BASE_D + 4608)
    sb_wfb = bf16v(8 * 1024, at=BASE_D)
    sb_yev = [f32v(512, at=BASE_D + 4608 + 512 * i) for i in range(2)]
    sb_ybf = bf16v(512, at=BASE_D + 5632)
    sb_Ei = f32v(2048, at=BASE_D)
    sb_cmbf = f32v(2048, at=BASE_D + 2048)
    sb_cmb = bf16v(512, at=BASE_D + 5632)
    sb_mact = [f32v(1024, at=BASE_D + 4096 + 1024 * i) for i in range(2)]
    sb_outt = [f32v(1024, at=BASE_D + 6144 + 1024 * i) for i in range(2)]
    BASE_E = BASE_D + 8192
    sb_woutb = bf16v(8 * 1024, at=BASE_E)
    sb_Et = [f32v(512, at=BASE_E + 4096 + 512 * i) for i in range(2)]
    sb_Lt = [f32v(512, at=BASE_E + 5120 + 512 * i) for i in range(2)]
    sb_x2 = f32v(1024, at=BASE_E + 4096)
    sb_xnf = f32v(1024, at=BASE_E + 5120)
    sb_cnp = bf16v(16 * 128, at=BASE_E + 4096)
    sb_lgt = [f32v(512, at=BASE_E + 512 * i) for i in range(2)]
    sb_dnt = [f32v(512, at=BASE_E + 1024 + 512 * i) for i in range(2)]
    sb_dsp = [bf16v(512, at=BASE_E + 2048 + 256 * i) for i in range(2)]
    sb_sev = bf16v(8 * 512, at=BASE_E + 2560)
    assert BASE_E + 6144 <= NF32, (BASE_E, NF32)

    ps = [nc.alloc_psum_tensor(f"ps{i}", [128, 512], F32).ap() for i in range(4)]
    ps_big = nc.alloc_psum_tensor("ps_big", [128, 2048], F32).ap()
    ps8 = ps + [ps_big[:, 512 * i:512 * (i + 1)] for i in range(4)]

    p = P(nc)
    for s in ("ld", "l0", "l1", "sp", "s0", "s1", "te", "ve", "se", "ge", "gl",
              "cc", "cc2", "cc3", "st"):
        p.sem(stack, s)

    p.dma("sync", sb_rank[0:1, 0:1], d_rank, "ld")
    ld_rank = p.c["ld"]
    sag_v = d_sag.rearrange("b c (j p) t -> c b p j t", p=128)
    yg5 = d_yag.rearrange("r c (k p) e -> c r p k e", p=128)
    dyn_sr = [None]
    dyn_y = [None]
    def _mk_dyn(E):
        rv = E.value_load(sb_rank[0:1, 0:1])
        dyn_sr[0] = sag_v[bass.ts(rv, 1)].squeeze(0)
        dyn_y[0] = yg5[bass.ts(rv, 1)].squeeze(0)
    p.dma("sync", sb_idf, d_idf, "ld")
    p.dma("sync", sb_idb, d_idb, "ld")
    p.dma("sync", sb_aneg, d_alog, "ld")
    p.dma("sync", sb_convw, d_convw, "ld")
    p.dma("sync", sb_convb, d_convb, "ld")
    p.dma("sync", sb_dtw, d_dtw, "ld")
    p.dma("sync", sb_dtb, d_dtb, "ld")
    p.dma("sync", sb_xpj.rearrange("p (g e) -> p g e", g=8),
          d_xproj.rearrange("(g p) e -> p g e", p=128), "ld")
    ld_params = p.c["ld"]
    p.wait("scalar", "ld", ld_params)
    p.op("scalar", lambda E: E.activation(sb_aneg, sb_aneg, AF.Exp), "se")
    p.wait("gpsimd", "se", p.c["se"])
    p.emit("gpsimd", _mk_dyn)
    p.wait("vector", "se", p.c["se"])
    p.op("vector", lambda E: E.tensor_scalar_mul(sb_aneg, sb_aneg, -1.0), "ve")

    x_r = d_x.rearrange("(i p) d -> p i d", p=128)
    ld_xgrp = []
    ve_ln_done = [0] * 4
    for grp in range(4):
        if grp >= 2:
            p.wait("sync", "ve", ve_ln_done[grp - 2])
        ld_xgrp.append(p.dma("sync", sb_xt[grp % 2].rearrange("p (i d) -> p i d", d=D),
                             x_r[:, 4 * grp:4 * (grp + 1), :], f"l{grp % 2}"))
        for sl in range(4):
            i = 4 * grp + sl
            xt = sb_xt[grp % 2][:, D * sl:D * (sl + 1)]
            p.wait("vector", f"l{grp % 2}", ld_xgrp[grp])
            p.op("vector", lambda E, xt=xt, i=i: E.reduce_sum(sb_ln1[:, i:i + 1], xt, AX.X))
            p.wait("scalar", f"l{grp % 2}", ld_xgrp[grp])
            p.op("scalar", lambda E, xt=xt: E.square(sb_x2, xt), "se")
            p.wait("vector", "se", p.c["se"])
            p.op("vector", lambda E, i=i: E.reduce_sum(sb_ln2[:, i:i + 1], sb_x2, AX.X))
            p.op("vector", lambda E, i=i: E.tensor_scalar_mul(sb_mu[:, i:i + 1], sb_ln1[:, i:i + 1], 1.0 / D))
            p.op("vector", lambda E, i=i: E.tensor_scalar_mul(sb_ln2[:, i:i + 1], sb_ln2[:, i:i + 1], 1.0 / D), "ve")
            p.wait("scalar", "ve", p.c["ve"])
            p.op("scalar", lambda E, i=i: E.square(sb_tmp1[:, 0:1], sb_mu[:, i:i + 1]), "se")
            p.wait("vector", "se", p.c["se"])
            p.op("vector", lambda E, i=i: E.tensor_sub(sb_ln2[:, i:i + 1], sb_ln2[:, i:i + 1], sb_tmp1[:, 0:1]))
            p.op("vector", lambda E, i=i: E.tensor_scalar_add(sb_ln2[:, i:i + 1], sb_ln2[:, i:i + 1], 1e-5), "ve")
            p.wait("scalar", "ve", p.c["ve"])
            p.op("scalar", lambda E, i=i: E.activation(sb_ln1[:, i:i + 1], sb_ln2[:, i:i + 1], AF.Sqrt), "se")
            p.wait("vector", "se", p.c["se"])
            p.op("vector", lambda E, i=i: E.reciprocal(sb_rstd[:, i:i + 1], sb_ln1[:, i:i + 1]))
            p.wait("vector", "te", p.c["te"])
            p.op("vector", lambda E, xt=xt, i=i: E.tensor_scalar(
                sb_xnf, xt, sb_mu[:, i:i + 1], sb_rstd[:, i:i + 1], ALU.subtract, ALU.mult), "ve")
            ve_xn = p.c["ve"]
            p.wait("scalar", "ve", ve_xn)
            p.op("scalar", lambda E, i=i: E.activation(sb_xntok[:, D * i:D * (i + 1)], sb_xnf, AF.Copy), "se")
            for jb in range(2):
                p.wait("tensor", "se", p.c["se"])
                for jj in range(4):
                    j = 4 * jb + jj
                    pst = ps[jj][:, :128].bitcast(BF16)[:, :128]
                    p.op("tensor", lambda E, i=i, j=j, pst=pst: E.transpose(
                        pst, sb_xntok[:, D * i + 128 * j:D * i + 128 * (j + 1)], sb_idb),
                        "te" if jj == 3 else None)
                p.wait("scalar", "te", p.c["te"])
                for jj in range(4):
                    j = 4 * jb + jj
                    pst = ps[jj][:, :128].bitcast(BF16)[:, :128]
                    p.op("scalar", lambda E, i=i, j=j, pst=pst: E.activation(
                        sb_xnd[:, 2048 * j + 128 * i:2048 * j + 128 * (i + 1)], pst, AF.Copy), "se")
        ve_ln_done[grp] = p.c["ve"]
    se_xn_all = p.c["se"]

    if tap == "p0":
        p.wait("scalar", "se", p.c["se"])
        p.wait("scalar", "ve", p.c["ve"])
        p.dma("scalar", d_dev[:, 0:8192], sb_xnd.bitcast(F32), "st")
        p.dma("scalar", d_dev[:, 8192:16384], sb_xntok.bitcast(F32), "st")
        p.dma("scalar", d_dev[:, 16384:16400], sb_mu, "st")
        p.dma("scalar", d_dev[:, 16400:16416], sb_rstd, "st")
        return _finish(nc, stack, p)

    p.wait("sync", "ve", p.c["ve"])
    for wname, src, dst in (("win", d_win, sb_winb), ("phi", d_phi, sb_phib)):
        for half in range(2):
            p.wait("sync", "se", p.c["se"])
            ldw = p.dma("sync", sb_wstage.rearrange("p (j e) -> p j e", j=4),
                        src.rearrange("(j p) e -> p j e", p=128)[:, 4 * half:4 * (half + 1), :], "ld")
            p.wait("scalar", "ld", ldw)
            for jj in range(4):
                j = 4 * half + jj
                p.op("scalar", lambda E, jj=jj, j=j, dst=dst: E.activation(
                    dst[:, 2048 * j:2048 * (j + 1)], sb_wstage[:, 2048 * jj:2048 * (jj + 1)], AF.Copy), "se")

    p.wait("tensor", "se", p.c["se"])
    for j in range(2):
        p.op("tensor", lambda E, j=j: E.matmul(
            ps[3], sb_xnd[:, 2048 * j:2048 * j + 128],
            sb_phib[:, 2048 * j:2048 * j + 512],
            start=(j == 0), stop=(j == 1)), "te" if j == 1 else None)
    p.wait("vector", "te", p.c["te"])
    p.op("vector", lambda E: E.tensor_copy(sb_tmp1[:, 0:1], ps[3][:, 0:1]), "ve")

    last_store = {}

    def guarded_store(tag, bufs, idx):
        key = (tag, idx % len(bufs))
        if key in last_store:
            p.wait("scalar", "sp", last_store[key])

    def wait_slot_stores(eng, sem):
        p.wait(eng, sem, p.c[sem])

    se_h1 = [p.c["se"], p.c["se"]]
    for i in range(MT):
        for q in range(TQ):
            pq = ps[q % 2]
            p.wait("tensor", "se", se_h1[-2])
            for j in range(G):
                p.op("tensor", lambda E, i=i, j=j, q=q, pq=pq: E.matmul(
                    pq, sb_xnd[:, 2048 * j + 128 * i:2048 * j + 128 * (i + 1)],
                    sb_phib[:, 2048 * j + 512 * q:2048 * j + 512 * (q + 1)],
                    start=(j == 0), stop=(j == G - 1)), "te" if j == G - 1 else None)
            p.wait("scalar", "te", p.c["te"])
            wait_slot_stores("scalar", f"s{q % 2}")
            eb, lb = sb_Et[q % 2], sb_Lt[q % 2]
            p.op("scalar", lambda E, pq=pq, eb=eb: E.activation(eb, pq, AF.Exp), "se")
            p.op("scalar", lambda E, pq=pq, lb=lb: E.activation(lb, pq, AF.Copy), "se")
            p.wait("scalar", "se", p.c["se"])
            p.dma("scalar", d_E[128 * i:128 * (i + 1), 512 * q:512 * (q + 1)], eb, f"s{q % 2}")
            p.dma("scalar", d_logits[128 * i:128 * (i + 1), 512 * q:512 * (q + 1)], lb, f"s{q % 2}")
            se_h1.append(p.c["se"])
            if tap == "p1" and i == 0 and q == 0:
                p.op("scalar", lambda E, pq=pq: E.activation(sb_sev.bitcast(F32)[:, 0:512], pq, AF.Copy), "se")
                p.dma("scalar", d_dev[:, 66048:66560], sb_sev.bitcast(F32)[:, 0:512], "st")
                p.dma("scalar", d_dev[:, 66560:67072], lb, "st")
    p.wait("gpsimd", "s0", p.c["s0"])
    p.wait("gpsimd", "s1", p.c["s1"])
    p.op("gpsimd", lambda E: E.collective_compute(
        "AllReduce", ALU.add, replica_groups=[list(range(8))],
        ins=[d_E.opt()], outs=[d_den.opt()]), "cc")
    cc_ar = p.c["cc"]

    if tap == "p1":
        p.wait("scalar", "sp", p.c["sp"])
        p.dma("scalar", d_dev[:, 0:32768].rearrange("p (i q) -> p i q", i=16), d_E.rearrange("(i p) q -> p i q", p=128), "st")
        p.dma("scalar", d_dev[:, 32768:65536].rearrange("p (i q) -> p i q", i=16), d_logits.rearrange("(i p) q -> p i q", p=128), "st")
        p.wait("tensor", "sp", p.c["sp"])
        for j in range(G):
            p.op("tensor", lambda E, j=j: E.matmul(
                ps[0], sb_xnd[:, 2048 * j:2048 * j + 128],
                sb_phib[:, 2048 * j:2048 * j + 512],
                start=(j == 0), stop=(j == G - 1)), "te" if j == G - 1 else None)
        p.wait("scalar", "te", p.c["te"])
        p.op("scalar", lambda E: E.activation(sb_Lt[0], ps[0], AF.Copy), "se")
        p.dma("scalar", d_dev[:, 65536:66048], sb_Lt[0], "st")
        return _finish(nc, stack, p)

    se_h2 = [p.c["se"], p.c["se"]]
    for g in range(16):
        for q in range(TQ):
            pq = ps[q % 2]
            p.wait("tensor", "se", se_h2[-2])
            for j in range(G):
                p.op("tensor", lambda E, g=g, j=j, q=q, pq=pq: E.matmul(
                    pq, sb_winb[:, 2048 * j + 128 * g:2048 * j + 128 * (g + 1)],
                    sb_xnd[:, 2048 * j + 512 * q:2048 * j + 512 * (q + 1)],
                    start=(j == 0), stop=(j == G - 1)), "te" if j == G - 1 else None)
            p.wait("scalar", "te", p.c["te"])
            if g < 8:
                if q == 0:
                    p.wait("scalar", "ve", p.c["ve"])
                p.op("scalar", lambda E, q=q, pq=pq: E.activation(
                    sb_xinp[:, 16 + 512 * q:16 + 512 * (q + 1)], pq, AF.Copy), "se")
                se_h2.append(p.c["se"])
            else:
                if q == 0:
                    wait_slot_stores("scalar", "sp")
                p.op("scalar", lambda E, q=q, pq=pq: E.activation(
                    sb_ct0[:, 512 * q:512 * (q + 1)], pq, AF.Silu), "se")
                se_h2.append(p.c["se"])
        if g < 8:
            p.wait("vector", "se", p.c["se"])
            if g == 0:
                p.op("vector", lambda E: E.memset(sb_xinp[:, 0:16], 0.0))
            x0, x1 = sb_xinp[:, 13:13 + T], sb_xinp[:, 14:14 + T]
            x2, x3 = sb_xinp[:, 15:15 + T], sb_xinp[:, 16:16 + T]
            p.wait("vector", "te", p.c["te"])
            p.op("vector", lambda E, g=g, x0=x0: E.tensor_scalar_mul(sb_ct0, x0, sb_convw[:, 4 * g:4 * g + 1]))
            p.op("vector", lambda E, g=g, x1=x1: E.scalar_tensor_tensor(
                sb_ct1, x1, sb_convw[:, 4 * g + 1:4 * g + 2], sb_ct0, ALU.mult, ALU.add))
            p.op("vector", lambda E, g=g, x2=x2: E.scalar_tensor_tensor(
                sb_ct0, x2, sb_convw[:, 4 * g + 2:4 * g + 3], sb_ct1, ALU.mult, ALU.add))
            p.op("vector", lambda E, g=g, x3=x3: E.scalar_tensor_tensor(
                sb_ct1, x3, sb_convw[:, 4 * g + 3:4 * g + 4], sb_ct0, ALU.mult, ALU.add), "ve")
            p.wait("scalar", "ve", p.c["ve"])
            wait_slot_stores("scalar", "sp")
            p.wait("scalar", "te", p.c["te"])
            p.op("scalar", lambda E, g=g: E.activation(sb_xcg, sb_ct1, AF.Silu, bias=sb_convb[:, g:g + 1]), "se")
            se_xcv = p.c["se"]
            p.wait("scalar", "se", se_xcv)
            p.dma("scalar", d_xconv[g], sb_xcg, "sp")
            p.wait("tensor", "se", se_xcv)
            for q in range(TQ):
                p.op("tensor", lambda E, g=g, q=q: E.matmul(
                    ps_big[0:33, 512 * q:512 * (q + 1)], sb_xpj[:, 33 * g:33 * (g + 1)],
                    sb_xcg[:, 512 * q:512 * (q + 1)],
                    start=(g == 0), stop=(g == 7)), "te" if q == TQ - 1 else None)
        else:
            p.wait("scalar", "se", p.c["se"])
            p.dma("scalar", d_sz[g - 8], sb_ct0, "sp")
    p.wait("scalar", "te", p.c["te"])
    p.op("scalar", lambda E: E.activation(sb_ssm[0:33, :], ps_big[0:33, :], AF.Copy), "se")
    se_ssm = p.c["se"]
    p.wait("scalar", "se", se_ssm)
    p.dma("scalar", d_ssm, sb_ssm[0:33, :], "sp")
    sp_ssm = p.c["sp"]

    if tap == "p2":
        p.wait("scalar", "sp", p.c["sp"])
        p.dma("scalar", d_dev[:, 0:16384].rearrange("p (g t) -> p g t", g=8), d_xconv.rearrange("g p t -> p g t"), "st")
        p.dma("scalar", d_dev[0:33, 16384:18432], d_ssm, "st")
        p.dma("scalar", d_dev[:, 18432:34816].rearrange("p (g t) -> p g t", g=8), d_sz.rearrange("g p t -> p g t"), "st")
        return _finish(nc, stack, p)

    p.wait("sync", "sp", sp_ssm)
    ld_sbc = p.dma("sync", sb_sbc, d_ssm[32:33, :].partition_broadcast(128), "ld")
    p.wait("sync", "se", p.c["se"])
    ld_wo = p.dma("sync", sb_wfs.rearrange("p (j e) -> p j e", j=8),
                  d_wout.rearrange("(j p) e -> p j e", p=128), "ld")
    p.wait("scalar", "ld", ld_wo)
    for j in range(G):
        p.op("scalar", lambda E, j=j: E.activation(
            sb_woutb[:, 1024 * j:1024 * (j + 1)], sb_wfs[:, 1024 * j:1024 * (j + 1)], AF.Copy), "se")
    se_wout = p.c["se"]
    p.wait("scalar", "ld", ld_wo)
    p.wait("vector", "ld", ld_wo)
    p.wait("vector", "sp", p.c["sp"])
    for g in range(G):
        wait_slot_stores("scalar", "sp")
        p.wait("vector", "sp", p.c["sp"])
        p.op("vector", lambda E, g=g: E.tensor_scalar(
            sb_ct0, sb_sbc, sb_dtw[:, g:g + 1], sb_dtb[:, g:g + 1], ALU.mult, ALU.add), "ve")
        p.wait("scalar", "ve", p.c["ve"])
        p.op("scalar", lambda E: E.activation(sb_ct1, sb_ct0, AF.Abs))
        p.op("scalar", lambda E: E.activation(sb_ct1, sb_ct1, AF.Exp, scale=-1.0))
        p.op("scalar", lambda E: E.activation(sb_ct1, sb_ct1, AF.Ln, bias=1.0))
        p.op("scalar", lambda E: E.activation(sb_ct0, sb_ct0, AF.Relu), "se")
        p.wait("vector", "se", p.c["se"])
        p.op("vector", lambda E: E.tensor_add(sb_ct0, sb_ct0, sb_ct1), "ve")
        p.wait("scalar", "ve", p.c["ve"])
        p.dma("scalar", d_dt[g], sb_ct0, "sp")

    if tap == "p3":
        p.wait("scalar", "sp", p.c["sp"])
        p.dma("scalar", d_dev[:, 0:16384].rearrange("p (g t) -> p g t", g=8), d_dt.rearrange("g p t -> p g t"), "st")
        return _finish(nc, stack, p)

    p.op("vector", lambda E: E.memset(sb_hcar, 0.0), "ve")
    for c in range(NCH):
        c0 = TC * c
        p.wait("sync", "ve", p.c["ve"])
        p.wait("sync", "ge", p.c["ge"])
        p.wait("sync", "sp", p.c["sp"])
        bsrc = d_ssm[0:16, c0:c0 + TC].unsqueeze(0).broadcast_to([128, 16, TC])
        csrc = d_ssm[16:32, c0:c0 + TC].unsqueeze(0).broadcast_to([128, 16, TC])
        lsem = f"l{c % 2}"
        p.dma("sync", sb_Bbc.rearrange("p (n t) -> p n t", n=16), bsrc, lsem)
        p.dma("sync", sb_Cbc.rearrange("p (n t) -> p n t", n=16), csrc, lsem)
        p.dma("sync", sb_xcs.rearrange("p (g t) -> p g t", g=8),
              d_xconv[:, :, c0:c0 + TC].rearrange("g p t -> p g t"), lsem)
        p.dma("sync", sb_dts.rearrange("p (g t) -> p g t", g=8),
              d_dt[:, :, c0:c0 + TC].rearrange("g p t -> p g t"), lsem)
        p.dma("sync", sb_szs.rearrange("p (g t) -> p g t", g=8),
              d_sz[:, :, c0:c0 + TC].rearrange("g p t -> p g t"), lsem)
        ld_grp = p.dma("sync", sb_xres.rearrange("p (i d) -> p i d", i=2),
                       d_x.rearrange("(i p) d -> p i d", p=128)[:, 2 * c:2 * c + 2, :], lsem)
        p.wait("vector", "te", p.c["te"])
        for g in range(G):
            dts = sb_dts[:, TC * g:TC * (g + 1)]
            xcs = sb_xcs[:, TC * g:TC * (g + 1)]
            p.wait("scalar", lsem, ld_grp)
            p.wait("scalar", "ve", p.c["ve"])
            for n in range(NSS):
                p.op("scalar", lambda E, g=g, n=n, dts=dts: E.activation(
                    sb_dA[:, TC * n:TC * (n + 1)], dts, AF.Exp,
                    scale=sb_aneg[:, 16 * g + n:16 * g + n + 1]),
                    "se" if n == NSS - 1 else None)
            p.op("scalar", lambda E, g=g, dts=dts: E.activation(
                sb_e0, sb_aneg[:, 16 * g:16 * (g + 1)], AF.Exp, scale=dts[:, 0:1]), "se")
            se_e0 = p.c["se"]
            p.wait("vector", lsem, ld_grp)
            p.wait("vector", "ge", p.c["ge"])
            p.op("vector", lambda E, dts=dts, xcs=xcs: E.tensor_mul(sb_ug, dts, xcs), "ve")
            ve_u = p.c["ve"]
            p.wait("gpsimd", lsem, ld_grp)
            p.wait("gpsimd", "ve", ve_u)
            u3 = sb_ug.unsqueeze(1).broadcast_to([128, 16, TC])
            p.op("gpsimd", lambda E, u3=u3: E.tensor_tensor(
                sb_dBx.rearrange("p (n t) -> p n t", n=16),
                sb_Bbc.rearrange("p (n t) -> p n t", n=16), u3, ALU.mult), "ge")
            ge_dbx = p.c["ge"]
            dA3 = sb_dA.rearrange("p (n t) -> p n t", n=16)
            dBx3 = sb_dBx.rearrange("p (n t) -> p n t", n=16)
            p.wait("vector", "se", se_e0)
            p.op("vector", lambda E, g=g: E.tensor_mul(sb_fixv, sb_e0, sb_hcar[:, 16 * g:16 * (g + 1)]))
            p.wait("vector", "ge", ge_dbx)
            p.op("vector", lambda E, dBx3=dBx3: E.tensor_add(dBx3[:, :, 0], dBx3[:, :, 0], sb_fixv))
            p.op("vector", lambda E, dA3=dA3: E.memset(dA3[:, :, 0], 0.0))
            p.op("vector", lambda E: E.tensor_tensor_scan(sb_h, sb_dA, sb_dBx, 0.0, ALU.mult, ALU.add))
            h3 = sb_h.rearrange("p (n t) -> p n t", n=16)
            p.op("vector", lambda E, g=g, h3=h3: E.tensor_copy(sb_hcar[:, 16 * g:16 * (g + 1)], h3[:, :, TC - 1]), "ve")
            ve_scan = p.c["ve"]
            p.wait("gpsimd", "ve", ve_scan)
            p.op("gpsimd", lambda E: E.tensor_tensor(sb_dBx, sb_h, sb_Cbc, ALU.mult), "ge")
            ge_hc = p.c["ge"]
            p.wait("vector", "ge", ge_hc)
            hc3 = sb_dBx.rearrange("p (n t) -> p t n", n=16)
            p.op("vector", lambda E, hc3=hc3: E.tensor_reduce(sb_yg, hc3, AX.X, ALU.add))
            ygslot = sb_ygt[:, 2048 * (c % 2) + 256 * g:2048 * (c % 2) + 256 * (g + 1)]
            p.op("vector", lambda E, g=g, ygslot=ygslot: E.tensor_mul(
                ygslot, sb_yg, sb_szs[:, TC * g:TC * (g + 1)]), "ve")
        ve_chunk = p.c["ve"]
        p.wait("tensor", "ve", ve_chunk)
        if c == 0:
            p.wait("tensor", "se", se_wout)
        for ti in range(2):
            for eh in range(2):
                pq = ps[2 * ti + eh]
                for g in range(G):
                    stat = sb_ygt[:, 2048 * (c % 2) + 256 * g + 128 * ti:
                                  2048 * (c % 2) + 256 * g + 128 * (ti + 1)]
                    p.op("tensor", lambda E, stat=stat, g=g, eh=eh, pq=pq: E.matmul(
                        pq, stat, sb_woutb[:, 1024 * g + 512 * eh:1024 * g + 512 * (eh + 1)],
                        start=(g == 0), stop=(g == G - 1)), "te" if g == G - 1 else None)
        te_op = p.c["te"]
        p.wait("vector", "te", te_op)
        p.wait("vector", lsem, ld_grp)
        p.wait("vector", f"s{(c - 1) % 2}", p.c[f"s{(c - 1) % 2}"] if c > 0 else 0)
        for ti in range(2):
            for eh in range(2):
                pq = ps[2 * ti + eh]
                dst = sb_mact[ti][:, 512 * eh:512 * (eh + 1)]
                p.op("vector", lambda E, pq=pq, dst=dst, ti=ti, eh=eh: E.scalar_tensor_tensor(
                    dst, pq, 1.0, sb_xres[:, 1024 * ti + 512 * eh:1024 * ti + 512 * (eh + 1)],
                    ALU.mult, ALU.add), "ve")
        p.wait("scalar", "ve", p.c["ve"])
        wait_slot_stores("scalar", f"s{c % 2}")
        for ti in range(2):
            p.dma("scalar", d_macc[128 * (2 * c + ti):128 * (2 * c + ti + 1), :],
                  sb_mact[ti], f"s{c % 2}")

    if tap == "p4":
        p.wait("scalar", "sp", p.c["sp"])
        p.dma("scalar", d_dev[:, 0:16384].rearrange("p (i d) -> p i d", i=16), d_macc.rearrange("(i p) d -> p i d", p=128), "st")
        return _finish(nc, stack, p)

    p.wait("sync", "cc", cc_ar)
    p.wait("sync", "te", p.c["te"])
    se_h5 = [p.c["se"], p.c["se"]]
    for q in range(TQ):
        for i in range(MT):
            lb, db, dspb = sb_lgt[i % 2], sb_dnt[i % 2], sb_dsp[i % 2]
            if i >= 2 or q > 0:
                p.wait("sync", "se", se_h5[-2])
            p.dma("sync", lb, d_logits[128 * i:128 * (i + 1), 512 * q:512 * (q + 1)], f"l{i % 2}")
            ld_lq = p.dma("sync", db, d_den[128 * i:128 * (i + 1), 512 * q:512 * (q + 1)], f"l{i % 2}")
            p.wait("scalar", f"l{i % 2}", ld_lq)
            p.op("scalar", lambda E, db=db: E.activation(db, db, AF.Ln), "se")
            p.wait("vector", "se", p.c["se"])
            p.op("vector", lambda E, lb=lb, db=db: E.tensor_sub(lb, lb, db), "ve")
            p.wait("scalar", "ve", p.c["ve"])
            p.wait("scalar", "te", p.c["te"])
            p.op("scalar", lambda E, lb=lb, dspb=dspb: E.activation(dspb, lb, AF.Exp), "se")
            se_dsp = p.c["se"]
            se_h5.append(se_dsp)
            if tap == "p5q0" and q == 0:
                p.wait("scalar", "se", se_dsp)
                p.dma("scalar", d_dev[:, 47104 + 256 * i:47104 + 256 * (i + 1)], dspb.bitcast(F32), "st")
                p.dma("scalar", d_dev[:, 51200 + 512 * i:51200 + 512 * (i + 1)], db, "st")
                p.dma("scalar", d_dev[:, 59392 + 512 * i:59392 + 512 * (i + 1)], lb, "st")
            p.wait("tensor", "se", se_dsp)
            if i == 0:
                p.wait("tensor", "ve", p.c["ve"])
            for j in range(G):
                p.op("tensor", lambda E, i=i, j=j, dspb=dspb: E.matmul(
                    ps8[j], sb_xntok[:, 1024 * i + 128 * j:1024 * i + 128 * (j + 1)],
                    dspb, start=(i == 0), stop=(i == MT - 1)), "te" if j == G - 1 else None)
        te_sl = p.c["te"]
        p.wait("scalar", "te", te_sl)
        wait_slot_stores("scalar", "sp")
        for j in range(G):
            p.op("scalar", lambda E, j=j: E.activation(sb_sev[:, 512 * j:512 * (j + 1)], ps8[j], AF.Copy), "se")
        p.wait("scalar", "se", p.c["se"])
        sev3 = sb_sev.rearrange("p (j t) -> p j t", j=8)
        for half in range(2):
            r = 2 * q + half
            p.dma("scalar", d_ssnd[r].rearrange("(j p) t -> p j t", p=128),
                  sev3[:, :, 256 * half:256 * (half + 1)], "sp")
    sp_slots = p.c["sp"]
    p.wait("gpsimd", "sp", sp_slots)
    p.op("gpsimd", lambda E: E.collective_compute(
        "AllGather", ALU.bypass, replica_groups=[list(range(8))],
        ins=[d_ssnd.opt()], outs=[d_sag.opt()]), "cc2")
    cc_s = p.c["cc2"]

    if tap == "p5":
        p.wait("scalar", "cc2", cc_s)
        p.dma("scalar", d_dev[:, 0:8192], d_sag.bitcast(F32)[0, 0][0:128, :].unsqueeze(1).broadcast_to([128, 64, 128]), "st")
        return _finish(nc, stack, p)

    p.wait("gpsimd", "cc2", cc_s)
    srcv5 = sb_srcv.rearrange("p (j n b q) -> p j n b q", j=8, n=8, b=8)
    stg4 = sb_stg.rearrange("p (b j t) -> p b j t", b=8, j=8)
    for b8 in range(8):
        p.c["gl"] += 16
        ld_sr = p.c["gl"]
        p.emit("gpsimd", lambda E, b8=b8: E.dma_start(
            out=stg4[:, b8], in_=dyn_sr[0][b8], bounds_check=None).then_inc(p.sems["gl"], 16))
    p.wait("vector", "gl", ld_sr)
    stg_r = sb_stg.rearrange("p (b j n q) -> p j n b q", b=8, j=8, n=8)
    for j in range(G):
        p.op("vector", lambda E, j=j: E.tensor_copy(srcv5[:, j], stg_r[:, j]),
             "ve" if j == G - 1 else None)
    ve_srcv = p.c["ve"]
    if tap == "p6a":
        p.wait("scalar", "gl", ld_sr)
        p.dma("scalar", d_dev[:, 0:8192], sb_srcv.bitcast(F32), "st")
        return _finish(nc, stack, p)

    for n in range(8):
        p.wait("sync", "se", p.c["se"])
        ld_w = p.dma("sync", sb_wfs.rearrange("p (j e) -> p j e", j=8),
                     d_wffn[n].rearrange("(j p) e -> p j e", p=128), f"l{n % 2}")
        p.wait("scalar", f"l{n % 2}", ld_w)
        for j in range(G):
            p.op("scalar", lambda E, j=j: E.activation(
                sb_wfb[:, 1024 * j:1024 * (j + 1)], sb_wfs[:, 1024 * j:1024 * (j + 1)], AF.Copy), "se")
        se_wf = p.c["se"]
        p.wait("sync", "ve", p.c["ve"])
        ld_bb = p.dma("sync", sb_bbc, d_bffn[n:n + 1, :].partition_broadcast(128), "ld")
        p.wait("tensor", "se", se_wf)
        p.wait("tensor", "ld", ld_bb)
        p.wait("tensor", "ve", ve_srcv)
        p.wait("tensor", "ve", p.c["ve"])
        for rh in range(2):
            for eh in range(2):
                pq = ps[2 * rh + eh]
                for j in range(G):
                    stat = sb_srcv[:, (j * 8 + n) * 256 + 128 * rh:
                                   (j * 8 + n) * 256 + 128 * (rh + 1)]
                    p.op("tensor", lambda E, stat=stat, j=j, eh=eh, pq=pq: E.matmul(
                        pq, stat, sb_wfb[:, 1024 * j + 512 * eh:1024 * j + 512 * (eh + 1)],
                        start=(j == 0), stop=(j == G - 1)), "te" if j == G - 1 else None)
        te_y = p.c["te"]
        p.wait("vector", "te", te_y)
        p.wait("vector", "ld", ld_bb)
        for rh in range(2):
            for eh in range(2):
                pq = ps[2 * rh + eh]
                p.wait("vector", "se", p.c["se"])
                p.op("vector", lambda E, pq=pq, rh=rh, eh=eh: E.scalar_tensor_tensor(
                    sb_yev[rh], pq, 1.0, sb_bbc[:, 512 * eh:512 * (eh + 1)], ALU.mult, ALU.add), "ve")
                p.wait("scalar", "ve", p.c["ve"])
                wait_slot_stores("scalar", "sp")
                p.op("scalar", lambda E, rh=rh: E.activation(sb_ybf, sb_yev[rh], AF.Copy), "se")
                p.wait("scalar", "se", p.c["se"])
                for bb in range(4):
                    b = 4 * rh + bb
                    p.dma("scalar", d_ysnd[b, 32 * n:32 * (n + 1), 512 * eh:512 * (eh + 1)],
                          sb_ybf[32 * bb:32 * (bb + 1), :], "sp")
    sp_y = p.c["sp"]
    p.wait("gpsimd", "sp", sp_y)
    p.op("gpsimd", lambda E: E.collective_compute(
        "AllGather", ALU.bypass, replica_groups=[list(range(8))],
        ins=[d_ysnd.opt()], outs=[d_yag.opt()]), "cc3")
    cc_y = p.c["cc3"]

    if tap == "p6":
        p.wait("scalar", "cc3", cc_y)
        p.dma("scalar", d_dev[:, 0:1024].rearrange("p (k e) -> p k e", k=2),
              d_yag.bitcast(F32)[0, 0].rearrange("(k p) e -> p k e", p=128), "st")
        return _finish(nc, stack, p)

    p.wait("gpsimd", "cc3", cc_y)
    yb4 = sb_yb.rearrange("p (r k e) -> p r k e", r=8, k=2)
    for r in range(8):
        p.c["gl"] += 16
        ld_yb = p.c["gl"]
        p.emit("gpsimd", lambda E, r=r: E.dma_start(
            out=yb4[:, r], in_=dyn_y[0][r], bounds_check=None).then_inc(p.sems["gl"], 16))

    if tap == "p7":
        p.wait("scalar", "ve", p.c["ve"])
        p.wait("scalar", "gl", ld_yb)
        p.dma("scalar", d_dev[:, 0:16], sb_rs, "st")
        p.dma("scalar", d_dev[:, 16:32], sb_rrs, "st")
        p.dma("scalar", d_dev[:, 1024:9216], sb_yb.bitcast(F32), "st")
    p.wait("sync", "sp", p.c["sp"])
    p.wait("sync", "s0", p.c["s0"])
    p.wait("sync", "s1", p.c["s1"])
    h7 = [(p.c["se"], p.c["ve"]), (p.c["se"], p.c["ve"])]
    for i in range(MT):
        p.wait("sync", "se", p.c["se"])
        p.wait("sync", "ve", h7[-2][1])
        p.dma("sync", sb_Ei, d_logits[128 * i:128 * (i + 1), :], f"l{i % 2}")
        ld_m = p.dma("sync", sb_mact[i % 2], d_macc[128 * i:128 * (i + 1), :], f"l{i % 2}")
        p.wait("scalar", f"l{i % 2}", ld_m)
        p.wait("scalar", "ve", p.c["ve"])
        p.op("scalar", lambda E: E.activation(sb_Ei, sb_Ei, AF.Exp), "se")
        p.wait("vector", "se", p.c["se"])
        p.wait("vector", "te", p.c["te"])
        p.op("vector", lambda E, i=i: E.reduce_sum(sb_rs[:, i:i + 1], sb_Ei, AX.X), "ve")
        p.wait("scalar", "ve", p.c["ve"])
        p.op("scalar", lambda E, i=i: E.activation(sb_rrs[:, i:i + 1], sb_rs[:, i:i + 1], AF.Ln, scale=1.0), "se")
        p.op("scalar", lambda E: E.activation(sb_cmbf, sb_Ei, AF.Ln), "se")
        p.wait("vector", "se", p.c["se"])
        p.op("vector", lambda E, i=i: E.tensor_scalar_mul(sb_rrs[:, i:i + 1], sb_rrs[:, i:i + 1], -1.0), "ve")
        p.wait("scalar", "ve", p.c["ve"])
        p.op("scalar", lambda E, i=i: E.activation(sb_cmbf, sb_cmbf, AF.Exp, bias=sb_rrs[:, i:i + 1]), "se")
        p.op("scalar", lambda E: E.engine_nop(), "se") if False else None
        p.wait("vector", "se", p.c["se"])
        p.op("vector", lambda E: E.tensor_copy(sb_tmp1[:, 0:1], sb_cmbf[:, 0:1]), "ve")
        ve_cmb = p.c["ve"]
        if tap == "p7" and i == 14:
            p.wait("scalar", "ve", ve_cmb)
            p.dma("scalar", d_dev[:, 9216:11264], sb_cmbf, "st")
            p.dma("scalar", d_dev[:, 11264:11280], sb_rs, "st")
            p.dma("scalar", d_dev[:, 11280:11296], sb_rrs, "st")
            p.dma("scalar", d_dev[:, 11296:13344], sb_Ei, "st")
        p.wait("tensor", "ve", ve_cmb)
        for kb in range(4):
            p.wait("tensor", "se", p.c["se"])
            for kk in range(4):
                k = 4 * kb + kk
                p.op("tensor", lambda E, k=k, kk=kk: E.transpose(
                    ps[kk][:, 0:128], sb_cmbf[:, 128 * k:128 * (k + 1)], sb_idf),
                    "te" if kk == 3 else None)
            p.wait("scalar", "te", p.c["te"])
            for kk in range(4):
                k = 4 * kb + kk
                p.op("scalar", lambda E, k=k, kk=kk: E.activation(
                    sb_cnp[:, 128 * k:128 * (k + 1)], ps[kk][:, 0:128], AF.Copy), "se")
        se_cnp = p.c["se"]
        p.wait("tensor", "se", se_cnp)
        p.wait("tensor", "gl", ld_yb)
        p.wait("tensor", "ve", p.c["ve"])
        for eh in range(2):
            pq = ps[eh]
            for k in range(16):
                p.op("tensor", lambda E, k=k, eh=eh, pq=pq: E.matmul(
                    pq, sb_cnp[:, 128 * k:128 * (k + 1)],
                    sb_yb[:, 1024 * k + 512 * eh:1024 * k + 512 * (eh + 1)],
                    start=(k == 0), stop=(k == 15)), "te" if k == 15 else None)
        te_out = p.c["te"]
        p.wait("vector", "te", te_out)
        p.wait("vector", f"l{i % 2}", ld_m)
        for eh in range(2):
            p.op("vector", lambda E, i=i, eh=eh: E.scalar_tensor_tensor(
                sb_outt[i % 2][:, 512 * eh:512 * (eh + 1)], ps[eh], 1.0,
                sb_mact[i % 2][:, 512 * eh:512 * (eh + 1)], ALU.mult, ALU.add), "ve")
        p.wait("scalar", "ve", p.c["ve"])
        wait_slot_stores("scalar", f"s{i % 2}")
        p.dma("scalar", d_out[128 * i:128 * (i + 1), :], sb_outt[i % 2], f"s{i % 2}")
        h7.append((p.c["se"], p.c["ve"]))

    if tap in ("post", "p5q0"):
        p.wait("scalar", "st", p.c["st"])
        p.dma("scalar", d_dev[:, 0:8192], sb_yb.bitcast(F32), "st")
        for r in range(4):
            p.dma("scalar",
                  d_dev[:, 8192 + 1024 * r:8192 + 1024 * (r + 1)].rearrange("p (k e) -> p k e", k=2),
                  d_yag.bitcast(F32)[r, 0].rearrange("(k p) e -> p k e", p=128), "st")
        p.dma("scalar", d_dev[:, 12288:13312], d_macc[0:128, :], "st")
        p.dma("scalar", d_dev[:, 13312:15360], sb_cmbf, "st")
        p.dma("scalar", d_dev[:, 15360:15376], sb_rs, "st")
        p.dma("scalar", d_dev[:, 15376:15392], sb_rrs, "st")
        p.dma("scalar", d_dev[:, 15392:15904], d_ysnd.bitcast(F32)[0, 0:128, :], "st")
        for r in range(8):
            p.dma("scalar", d_dev[:, 36864 + 1024 * r:36864 + 1024 * (r + 1)].rearrange("p (j t) -> p j t", j=8),
                  d_ssnd.bitcast(F32)[r].rearrange("(j p) t -> p j t", p=128), "st")
        p.dma("scalar", d_dev[:, 45056:45568], d_den[0:128, 0:512], "st")
        p.dma("scalar", d_dev[:, 45568:46080], d_den[1920:2048, 0:512], "st")
        p.dma("scalar", d_dev[:, 46080:46592], d_logits[0:128, 0:512], "st")
        for s in range(8):
            p.dma("scalar", d_dev[:, 24576 + 1024 * s:24576 + 1024 * (s + 1)].rearrange("p (j t) -> p j t", j=8),
                  d_sag.bitcast(F32)[s, 0].rearrange("(j p) t -> p j t", p=128), "st")
        p.dma("scalar", d_dev[:, 32768:33792].rearrange("p (j t) -> p j t", j=8),
              d_ssnd.bitcast(F32)[0].rearrange("(j p) t -> p j t", p=128), "st")
        p.dma("scalar", d_dev[:, 33792:34816], d_den[0:128, 1024:2048], "st")
        p.dma("scalar", d_dev[:, 34816:35840], d_logits[0:128, 1024:2048], "st")

    return _finish(nc, stack, p)


def _finish(nc, stack, p):
    p.wait("sync", "st", p.c["st"])
    p.wait("sync", "sp", p.c["sp"])
    p.wait("sync", "s0", p.c["s0"])
    p.wait("sync", "s1", p.c["s1"])
    with nc.Block() as block:
        @block.sync
        def _(E):
            for fn in p.ops["sync"]:
                fn(E)

        @block.scalar
        def _(E):
            for fn in p.ops["scalar"]:
                fn(E)

        @block.vector
        def _(E):
            for fn in p.ops["vector"]:
                fn(E)

        @block.tensor
        def _(E):
            for fn in p.ops["tensor"]:
                fn(E)

        @block.gpsimd
        def _(E):
            for fn in p.ops["gpsimd"]:
                fn(E)
    stack.close()
    return nc


def kernel(L0_attn_x, L0_attn_in_proj_w, L0_attn_conv_w, L0_attn_conv_b,
           L0_attn_x_proj_w, L0_attn_dt_proj_w, L0_attn_dt_proj_b,
           L0_attn_A_log, L0_attn_out_proj_w, L0_ffn_phi, L0_ffn_W, L0_ffn_b):
    import ml_dtypes
    x = np.ascontiguousarray(np.asarray(L0_attn_x, dtype=np.float32))
    w_in_t = np.ascontiguousarray(np.asarray(L0_attn_in_proj_w, dtype=np.float32).T)
    w_out_t = np.ascontiguousarray(np.asarray(L0_attn_out_proj_w, dtype=np.float32).T)
    phi = np.ascontiguousarray(np.asarray(L0_ffn_phi, dtype=np.float32).reshape(D, NP))
    xproj_t = np.ascontiguousarray(np.asarray(L0_attn_x_proj_w, dtype=np.float32).T)
    conv_w = np.ascontiguousarray(
        np.asarray(L0_attn_conv_w, dtype=np.float32)[:, 0, :].reshape(8, 128, 4).transpose(1, 0, 2).reshape(128, 32))
    conv_b = np.ascontiguousarray(np.asarray(L0_attn_conv_b, dtype=np.float32).reshape(8, 128).T)
    dt_w = np.ascontiguousarray(np.asarray(L0_attn_dt_proj_w, dtype=np.float32).reshape(8, 128).T)
    dt_b = np.ascontiguousarray(np.asarray(L0_attn_dt_proj_b, dtype=np.float32).reshape(8, 128).T)
    a_log = np.ascontiguousarray(
        np.asarray(L0_attn_A_log, dtype=np.float32).reshape(8, 128, NSS).transpose(1, 0, 2).reshape(128, 128))
    W = np.asarray(L0_ffn_W, dtype=np.float32)
    bffn = np.asarray(L0_ffn_b, dtype=np.float32)
    idf = np.eye(128, dtype=np.float32)
    idb = np.eye(128, dtype=ml_dtypes.bfloat16)

    with allow_ici_all_to_all():
        nc = build(tap=DEV_TAP)
    in_maps = []
    for c in range(8):
        in_maps.append({
            "x": np.ascontiguousarray(x[c]),
            "w_in_t": w_in_t, "w_out_t": w_out_t, "phi": phi,
            "xproj_t": xproj_t, "conv_w": conv_w, "conv_b": conv_b,
            "dt_w": dt_w, "dt_b": dt_b, "a_log": a_log,
            "w_ffn": np.ascontiguousarray(W[8 * c:8 * (c + 1)]),
            "b_ffn": np.ascontiguousarray(bffn[8 * c:8 * (c + 1)]),
            "ident_f": idf, "ident_b": idb,
            "rank": np.array([[c]], dtype=np.uint32),
        })
    with allow_ici_all_to_all():
        res = run_bass_kernel_spmd(nc, in_maps, core_ids=list(range(8)))
    outs = [r["out"] if not isinstance(r["out"], dict) else r["out"] for r in res.results]
    return np.stack([np.asarray(o) for o in outs])
